# version 02
# baseline (speedup 1.0000x reference)
import contextlib
import math
import re as _re
import numpy as np
import ml_dtypes
import concourse.bass as bass
import concourse.mybir as mybir
from concourse.bass_utils import run_bass_kernel_spmd
from concourse.alu_op_type import AluOpType

F32 = mybir.dt.float32
BF16 = mybir.dt.bfloat16
I32 = mybir.dt.int32
AF = mybir.ActivationFunctionType
OP = AluOpType
AX = mybir.AxisListType.X

D = 4096
T = 1024
NCH = 32
ISQ = 1.0 / math.sqrt(128.0)

CB_ID = 0
CB_ONEC = 128
CB_ONER = 129
CB_E8 = 257
CB_P32 = 385
CB_G32 = 513
CB_Q4 = 517
CB_W = 645

R_MEAN = 0
R_RSTD = 1024
R_KN = 0
R_SC = 100
R_GS = 256
R_RECIP = 512
R_GK2 = 1600
R_C = 1700


class Sem:
    def __init__(self, es, nc, name):
        self.h = es.enter_context(nc.semaphore(name))
        self.n = 0

    def inc(self, instr, k=1):
        instr.then_inc(self.h, k)
        self.n += k
        return self.n

    def wait(self, eng, n=None):
        eng.wait_ge(self.h, self.n if n is None else n)


class Bld:
    def __init__(self):
        self.nc = bass.Bass("TRN2", target_bir_lowering=False)
        self.es = contextlib.ExitStack()

    def sb(self, name, shape, dtype):
        setattr(self, name, self.es.enter_context(self.nc.sbuf_tensor(name, shape, dtype)))

    def ps(self, name, shape, dtype=F32):
        setattr(self, name, self.es.enter_context(self.nc.psum_tensor(name, shape, dtype)))

    def sem(self, name):
        setattr(self, name, Sem(self.es, self.nc, name))


def onesr(b):
    return b.cstb[0:1, CB_ONER:CB_ONER + 128]


def onesc(b):
    return b.cstb[:, CB_ONEC:CB_ONEC + 1]


def ident_f32(b):
    return b.cst[:, 129:257]


def xn_rhs(b, dc, half):
    return b.xn_sb[:, dc * T + half * 512:dc * T + half * 512 + 512]


def row(b, off, w):
    return b.rows[0:1, off:off + w]


def build_program(blocks, dbg=None):
    b = Bld()
    nc = b.nc
    dram = {}

    def din(name, shape, dtype=F32):
        dram[name] = nc.dram_tensor(name, shape, dtype, kind="ExternalInput").ap()

    def dint(name, shape, dtype=F32):
        dram[name] = nc.dram_tensor(name, shape, dtype).ap()

    din("xin", [D, T])
    dram["xout"] = nc.dram_tensor("xout", [D, T], F32, kind="ExternalOutput").ap()
    if dbg is not None:
        dram["dbg"] = nc.dram_tensor("dbg", [D, T], F32, kind="ExternalOutput").ap()

    need = set(blocks)
    din("consts", [128, 257])
    din("constsb", [128, CB_W])
    if 0 in need:
        din("l0_wq", [32, 128, D]); din("l0_wk", [8, 128, D]); din("l0_wv", [8, 128, D])
        din("l0_bq", [128, 32]); din("l0_bk", [128, 8]); din("l0_bv", [128, 8])
        din("l0_gw", [128, NCH * 8])
        din("l0_bw", [32, 128, D]); din("l0_bb", [128, 32])
        din("l0_A", [1, 128, D]); din("l0_B", [32, 128, 128])
        din("l0_cc", [128, T]); din("l0_ss", [128, T])
        dint("g0_kin", [1024, T], BF16); dint("g0_kout", [4096, T], BF16)
        dint("g0_vin", [8 * T, 128], BF16); dint("g0_vout", [4 * 8 * T, 128], BF16)
    if 1 in need:
        din("l1_w", [32, 128, D]); din("l1_mask", [D, D], I32)
        din("l1_A", [1, 128, D]); din("l1_B", [32, 128, 128])
        din("l1_wr", [128, NCH * 8]); din("l1_br", [8, 1]); din("qexp", [128, D])
        dint("t1scr", [128, D]); dint("bm01d", [128, 128])
        dint("em_scr", [NCH, 128, D], BF16)
    if 2 in need:
        din("l2_wq", [32, 128, D]); din("l2_wk", [32, 128, D])
        din("l2_wv", [32, 128, D]); din("l2_wo", [32, 128, D])
        din("l2_bq", [128, 32]); din("l2_bk", [128, 32]); din("l2_bv", [128, 32])
        din("l2_bo", [128, 32]); din("l2_gq", [128, 32]); din("l2_gk", [128, 32])
        din("l2_wrpp", [128, NCH * 24]); din("l2_b24", [24, 1]); din("l2_gate", [1, 1])
        din("l2_wbr", [8, 128, 1024])
        dint("g2_kin", [D, T], BF16); dint("g2_kout", [4 * D, T], BF16)
        dint("l2_hrows", [24, T])
        dint("g2_vin", [32 * T, 128], BF16); dint("g2_vout", [4 * 32 * T, 128], BF16)
    if 3 in need:
        din("l3_wq", [32, 128, D]); din("l3_wk", [8, 128, D]); din("l3_wv", [8, 128, D])
        din("l3_wo", [32, 128, D]); din("l3_w", [32, 128, D]); din("l3_b", [128, 32])
        din("l3_wqk", [4, 128, D])
        din("l3_A", [1, 128, D]); din("l3_B", [32, 128, 128])
        din("l3_me", [128, 16 * 128]); din("l3_mfd", [128, 256])
        dint("g3_kin", [1024, T], BF16); dint("g3_kout", [4096, T], BF16)
        dint("g3_vin", [8 * T, 128], BF16); dint("g3_vout", [4 * 8 * T, 128], BF16)
    if need & {0, 2, 3}:
        dint("gs_in", [1, 64]); dint("gs_out", [4, 64])
    dint("xbuf", [D, T]); dint("xmid", [D, T])

    b.sb("xn_sb", [128, NCH * T], BF16)
    b.sb("qk_sb", [128, NCH * T], BF16)
    b.sb("wst", [128, 2 * D], BF16)
    b.sb("wsm", [128, 2 * 128], BF16)
    b.sb("xf", [128, 2 * T], F32)
    b.sb("kv_sb", [128, 8192], BF16)
    b.sb("s1", [128, T], F32)
    b.sb("s2", [128, T], F32)
    b.sb("bc1", [128, T], F32)
    b.sb("bc2", [128, T], F32)
    b.sb("cbig", [128, 2560], BF16)
    b.sb("pT", [128, 1024], BF16)
    b.sb("sb16", [128, 1024], BF16)
    b.sb("rows", [128, 2048], F32)
    b.sb("cst", [128, 257], F32)
    b.sb("cstb", [128, CB_W], BF16)
    b.sb("btile", [128, 96], F32)
    b.sb("tm", [128, 64], F32)
    b.sb("tm2", [128, 64], F32)
    b.sb("tm3", [128, 192], F32)
    b.sb("qh_sb", [128, T], BF16)
    b.sb("biasC", [128, 16], F32)

    b.ps("psP0", [128, 1024])
    b.ps("psP1", [128, 1024])
    b.ps("psQ", [128, 1024])
    b.ps("psV", [128, 512])
    b.ps("psM", [128, 512])

    for s in ["sxa", "sxb", "swa", "swb", "sws", "skk", "skr", "skga", "skgb",
              "sgs", "sb0", "pe", "va", "ac", "pl", "cc", "st"]:
        b.sem(s)
    b.sx_slot = [b.sxa, b.sxb]
    b.sw_slot = [b.swa, b.swb]
    b.skg_slot = [b.skga, b.skgb]

    with b.es, nc.Block() as blk:
        @blk.gpsimd
        def _(g):
            i = g.dma_start(out=b.cst[:], in_=dram["consts"])
            b.sb0.inc(i, 16)
            i = g.dma_start(out=b.cstb[:], in_=dram["constsb"])
            b.sb0.inc(i, 16)

        first = True
        for bi in blocks:
            xsrc = dram["xin"] if first else dram["xbuf"]
            xdst = dram["xout"] if bi == blocks[-1] else dram["xbuf"]
            emit_block(b, blk, dram, bi, xsrc, xdst, dbg)
            first = False

        @blk.sync
        def _(sy):
            b.st.wait(sy)
    return nc


def emit_block(b, blk, dram, bi, xsrc, xdst, dbg):
    emit_stats_xn(b, blk, xsrc)
    if dbg is not None and f"xn{bi}" in dbg:
        dbg_dump(b, blk, dram, lambda ch: b.xn_sb[:, ch * T:(ch + 1) * T])
    [emit_block0, emit_block1, emit_block2, emit_block3][bi](b, blk, dram, xsrc, xdst, dbg)


def dbg_dump_dram(b, blk, dram, src_ap):
    for ch in range(NCH):
        @blk.sync
        def _(sy, ch=ch):
            b.st.wait(sy)
            b.va.wait(sy)
            b.sxa.wait(sy)
            i = sy.dma_start(out=b.xf[:, 0:T], in_=src_ap[ch * 128:(ch + 1) * 128, :])
            b.sxa.inc(i, 16)

        @blk.vector
        def _(v, ch=ch):
            b.sxa.wait(v)
            i = v.tensor_copy(b.s1[:], b.xf[:, 0:T])
            b.va.inc(i)

        @blk.sync
        def _(sy, ch=ch):
            b.va.wait(sy)
            i = sy.dma_start(out=dram["dbg"][ch * 128:(ch + 1) * 128, :], in_=b.s1[:])
            b.st.inc(i, 16)


def dbg_dump(b, blk, dram, src_fn):
    for ch in range(NCH):
        @blk.vector
        def _(v, ch=ch):
            b.st.wait(v)
            i = v.tensor_copy(b.s1[:], src_fn(ch))
            b.va.inc(i)

        @blk.sync
        def _(sy, ch=ch):
            b.va.wait(sy)
            i = sy.dma_start(out=dram["dbg"][ch * 128:(ch + 1) * 128, :], in_=b.s1[:])
            b.st.inc(i, 16)



def emit_stats_xn(b, blk, xsrc):
    ac_t, va_t, sx_t, pe_s = [], [], [], []
    for ch in range(NCH):
        @blk.sync
        def _(sy, ch=ch):
            if ch >= 2:
                b.ac.wait(sy, ac_t[ch - 2])
                b.va.wait(sy, va_t[ch - 2])
            else:
                b.ac.wait(sy)
                b.va.wait(sy)
            i = sy.dma_start(out=b.xf[:, (ch % 2) * T:(ch % 2 + 1) * T],
                             in_=xsrc[ch * 128:(ch + 1) * 128, :])
            sx_t.append(b.sx_slot[ch % 2].inc(i, 16))

        @blk.scalar
        def _(s, ch=ch):
            b.sx_slot[ch % 2].wait(s, sx_t[ch])
            if ch >= 1:
                b.va.wait(s, va_t[ch - 1])
            i = s.activation(b.s1[:], b.xf[:, (ch % 2) * T:(ch % 2 + 1) * T], AF.Square)
            ac_t.append(b.ac.inc(i))

        @blk.vector
        def _(v, ch=ch):
            b.sx_slot[ch % 2].wait(v, sx_t[ch])
            if ch == 0:
                b.pe.wait(v)
            else:
                b.pe.wait(v, pe_s[ch - 1])
            v.tensor_copy(b.sb16[:], b.xf[:, (ch % 2) * T:(ch % 2 + 1) * T])
            b.ac.wait(v, ac_t[ch])
            i = v.tensor_copy(b.pT[:], b.s1[:])
            va_t.append(b.va.inc(i))

        @blk.tensor
        def _(t, ch=ch):
            if ch == 0:
                b.sb0.wait(t)
            b.va.wait(t, va_t[ch])
            for half in range(2):
                dst = b.psV[0:1, 0:512] if half == 0 else b.psM[0:1, 0:512]
                t.matmul(dst, onesc(b), b.sb16[:, half * 512:(half + 1) * 512],
                         start=(ch == 0), stop=(ch == NCH - 1), skip_group_check=True)
            for half in range(2):
                last = t.matmul(b.psQ[0:1, half * 512:(half + 1) * 512], onesc(b),
                                b.pT[:, half * 512:(half + 1) * 512],
                                start=(ch == 0), stop=(ch == NCH - 1), skip_group_check=True)
            pe_s.append(b.pe.inc(last))

    @blk.vector
    def _(v):
        b.pe.wait(v)
        v.tensor_scalar(row(b, R_MEAN, 512), b.psV[0:1, 0:512], 1.0 / D, None, OP.mult)
        v.tensor_scalar(row(b, R_MEAN + 512, 512), b.psM[0:1, 0:512], 1.0 / D, None, OP.mult)
        i = v.tensor_scalar(row(b, R_RSTD, 1024), b.psQ[0:1, 0:1024], 1.0 / D, None, OP.mult)
        b.va.inc(i); b.va.wait(v)
        i = v.tensor_tensor(b.s1[0:1, 0:1024], row(b, R_MEAN, 1024), row(b, R_MEAN, 1024), OP.mult)
        b.va.inc(i); b.va.wait(v)
        v.tensor_tensor(row(b, R_RSTD, 1024), row(b, R_RSTD, 1024), b.s1[0:1, 0:1024], OP.subtract)
        i = v.memset(b.s1[0:1, 0:1], 1e-5)
        b.va.inc(i)

    @blk.scalar
    def _(s):
        b.va.wait(s)
        i = s.activation(row(b, R_RSTD, 1024), row(b, R_RSTD, 1024), AF.Sqrt,
                         bias=b.s1[0:1, 0:1])
        b.ac.inc(i)

    @blk.vector
    def _(v):
        b.ac.wait(v)
        i = v.reciprocal(row(b, R_RSTD, 1024), row(b, R_RSTD, 1024))
        b.va.inc(i); b.va.wait(v)
        v.tensor_copy(b.sb16[0:1, 0:1024], row(b, R_MEAN, 1024))
        i = v.tensor_copy(b.pT[0:1, 0:1024], row(b, R_RSTD, 1024))
        b.va.inc(i)

    @blk.tensor
    def _(t):
        b.va.wait(t)
        for half in range(2):
            t.matmul(b.psP0[:, half * 512:(half + 1) * 512], onesr(b),
                     b.sb16[0:1, half * 512:(half + 1) * 512], start=True, stop=True)
            last = t.matmul(b.psP1[:, half * 512:(half + 1) * 512], onesr(b),
                            b.pT[0:1, half * 512:(half + 1) * 512], start=True, stop=True)
        b.pe.inc(last)

    @blk.vector
    def _(v):
        b.pe.wait(v)
        v.tensor_copy(b.bc1[:], b.psP0[:])
        i = v.tensor_copy(b.bc2[:], b.psP1[:])
        b.va.inc(i)

    va2_t, sx2_t = [], []
    for ch in range(NCH):
        @blk.sync
        def _(sy, ch=ch):
            if ch >= 2:
                b.va.wait(sy, va2_t[ch - 2])
            else:
                b.va.wait(sy)
                b.ac.wait(sy)
            i = sy.dma_start(out=b.xf[:, (ch % 2) * T:(ch % 2 + 1) * T],
                             in_=xsrc[ch * 128:(ch + 1) * 128, :])
            sx2_t.append(b.sx_slot[ch % 2].inc(i, 16))

        @blk.vector
        def _(v, ch=ch):
            b.sx_slot[ch % 2].wait(v, sx2_t[ch])
            v.tensor_tensor(b.s1[:], b.xf[:, (ch % 2) * T:(ch % 2 + 1) * T], b.bc1[:], OP.subtract)
            i = v.tensor_tensor(b.xn_sb[:, ch * T:(ch + 1) * T], b.s1[:], b.bc2[:], OP.mult)
            va2_t.append(b.va.inc(i))



def emit_proj(b, blk, wdram, js, rhs_tiles, evac, extra_mm=None, wsm_dram=None,
              psums=None, pre_wait=None):
    if psums is None:
        psums = [b.psP0, b.psP1]
    w_t, pe_t, free_t = {}, {}, {}

    for idx, j in enumerate(js):
        @blk.gpsimd
        def _(g, idx=idx, j=j):
            if idx >= 2:
                b.pe.wait(g, pe_t[idx - 2])
            else:
                b.pe.wait(g)
            sl = b.sw_slot[idx % 2]
            i = g.dma_start(out=b.wst[:, (idx % 2) * D:(idx % 2 + 1) * D], in_=wdram[j])
            tok = sl.inc(i, 16)
            if wsm_dram is not None:
                i = g.dma_start(out=b.wsm[:, (idx % 2) * 128:(idx % 2 + 1) * 128],
                                in_=wsm_dram[j])
                tok = sl.inc(i, 16)
            w_t[idx] = tok

        @blk.tensor
        def _(t, idx=idx, j=j):
            b.sw_slot[idx % 2].wait(t, w_t[idx])
            if idx == 0:
                b.va.wait(t)
                if pre_wait is not None:
                    pre_wait(t)
            if idx < 2:
                b.ac.wait(t)
                b.va.wait(t)
            else:
                sem, tok = free_t[idx - 2]
                sem.wait(t, tok)
            ps = psums[idx % 2]
            w = b.wst[:, (idx % 2) * D:(idx % 2 + 1) * D]
            for half in range(2):
                for dc in range(NCH):
                    last = t.matmul(ps[:, half * 512:(half + 1) * 512],
                                    w[:, dc * 128:(dc + 1) * 128],
                                    rhs_tiles(dc, half),
                                    start=(dc == 0),
                                    stop=(dc == NCH - 1 and extra_mm is None),
                                    skip_group_check=True)
            if extra_mm is not None:
                last = extra_mm(t, idx, j, ps)
            pe_t[idx] = b.pe.inc(last)

        free_t[idx] = evac(idx, j, psums[idx % 2], pe_t)



def emit_kv_and_gather(b, blk, dram, pfx, nkv, wk, wv, bk_col, bv_col, *, rope=False,
                       l2_gain=None, kvph=99):
    kin, vin = dram[pfx + "_kin"], dram[pfx + "_vin"]

    def evac_k(idx, j, ps, pe_t):
        @blk.scalar
        def _(s, idx=idx, j=j, ps=ps):
            b.pe.wait(s, pe_t[idx])
            b.va.wait(s)
            b.sb0.wait(s)
            if bk_col is None:
                s.copy(b.s1[:], ps[:])
            else:
                s.activation(b.s1[:], ps[:], AF.Identity, bias=bk_col(j))
            if rope:
                s.activation(b.s2[0:64, :], ps[64:128, :], AF.Identity, bias=bk_col(j, 1))
                i = s.activation(b.s2[64:128, :], ps[0:64, :], AF.Identity, bias=bk_col(j, 2))
            elif l2_gain is not None:
                i = s.activation(b.s2[:], b.s1[:], AF.Square)
            else:
                i = s.copy(b.s2[0:1, 0:1], b.s1[0:1, 0:1])
            b.ac.inc(i)

        @blk.vector
        def _(v, idx=idx, j=j):
            b.ac.wait(v)
            if rope:
                b.sb0.wait(v)
                v.tensor_tensor(b.s1[:], b.s1[:], b.cbig[:, 0:T], OP.mult)
                v.tensor_tensor(b.s2[:], b.s2[:], b.cbig[:, T:2 * T], OP.mult)
                i = v.tensor_tensor(b.qk_sb[:, j * T:(j + 1) * T], b.s1[:], b.s2[:], OP.add)
            elif l2_gain is not None:
                b.pe.wait(v)
                v.tensor_copy(b.qk_sb[:, j * T:(j + 1) * T], b.s1[:])
                i = v.tensor_copy(b.pT[:], b.s2[:])
                b.va.inc(i)
                i = None
            else:
                i = v.tensor_copy(b.qk_sb[:, j * T:(j + 1) * T], b.s1[:])
            if i is not None:
                b.va.inc(i)
        if l2_gain is not None:
            @blk.tensor
            def _(t, idx=idx, j=j):
                b.va.wait(t)
                for half in range(2):
                    last = t.matmul(b.psQ[0:1, half * 512:(half + 1) * 512], onesc(b),
                                    b.pT[:, half * 512:(half + 1) * 512],
                                    start=(j == 0), stop=(j == nkv - 1), skip_group_check=True)
                b.pe.inc(last)
        return (b.va, b.va.n)

    emit_proj(b, blk, wk, list(range(nkv)), lambda dc, half: xn_rhs(b, dc, half), evac_k)
    if kvph < 2:
        return b.cc.n

    if l2_gain is not None:
        emit_rstd_bcast(b, blk, 1e-6)

    for j in range(nkv):
        if l2_gain is not None:
            @blk.vector
            def _(v, j=j):
                b.va.wait(v)
                b.pe.wait(v)
                b.skk.wait(v)
                v.tensor_tensor(b.s1[:], b.qk_sb[:, j * T:(j + 1) * T], b.bc2[:], OP.mult)
                i = v.tensor_scalar(b.sb16[:], b.s1[:], l2_gain(j), None, OP.mult)
                b.va.inc(i)

            @blk.scalar
            def _(s, j=j):
                b.va.wait(s)
                i = s.activation(b.s2[:], b.sb16[:], AF.Square)
                b.ac.inc(i)
        else:
            @blk.scalar
            def _(s, j=j):
                b.va.wait(s)
                i = s.activation(b.s2[:], b.qk_sb[:, j * T:(j + 1) * T], AF.Square)
                b.ac.inc(i)

        sqb = b.pT if l2_gain is not None else b.sb16

        @blk.vector
        def _(v, j=j, sqb=sqb):
            b.ac.wait(v)
            b.pe.wait(v)
            i = v.tensor_copy(sqb[:], b.s2[:])
            b.va.inc(i)

        @blk.tensor
        def _(t, j=j, sqb=sqb):
            b.va.wait(t)
            t.matmul(b.psM[0:1, 0:512], onesc(b), sqb[:, 0:512], start=True, stop=True,
                     skip_group_check=True)
            i = t.matmul(b.psV[0:1, 0:512], onesc(b), sqb[:, 512:1024], start=True,
                         stop=True, skip_group_check=True)
            b.pe.inc(i)

        @blk.vector
        def _(v, j=j):
            b.pe.wait(v)
            if j == 0:
                v.memset(row(b, R_KN, 64), 0.0)
            v.reduce_max(row(b, R_SC, 1), b.psM[0:1, 0:512], axis=AX)
            i = v.reduce_max(row(b, R_SC + 1, 1), b.psV[0:1, 0:512], axis=AX)
            b.va.inc(i); b.va.wait(v)
            i = v.tensor_tensor(row(b, R_KN + j, 1), row(b, R_SC, 1), row(b, R_SC + 1, 1), OP.max)
            b.va.inc(i)

        @blk.sync
        def _(sy, j=j):
            b.va.wait(sy)
            i = sy.dma_start(out=kin[j * 128:(j + 1) * 128, :],
                             in_=b.qk_sb[:, j * T:(j + 1) * T])
            b.skk.inc(i, 16)

    @blk.sync
    def _(sy):
        b.va.wait(sy)
        i = sy.dma_start(out=dram["gs_in"][0:1, 0:64], in_=row(b, R_KN, 64))
        b.skk.inc(i, 16)

    if kvph < 3:
        return b.cc.n

    def evac_v(idx, j, ps, pe_t):
        @blk.scalar
        def _(s, idx=idx, j=j, ps=ps):
            b.pe.wait(s, pe_t[idx])
            b.va.wait(s)
            b.sb0.wait(s)
            if bv_col is None:
                i = s.copy(b.s1[:], ps[:])
            else:
                i = s.activation(b.s1[:], ps[:], AF.Identity, bias=bv_col(j))
            b.ac.inc(i)

        @blk.vector
        def _(v, j=j):
            b.ac.wait(v)
            voff2 = 0 if nkv == 32 else nkv
            i = v.tensor_copy(b.qk_sb[:, (voff2 + j) * T:(voff2 + j + 1) * T], b.s1[:])
            b.va.inc(i)
        return (b.va, b.va.n)

    emit_proj(b, blk, wv, list(range(nkv)), lambda dc, half: xn_rhs(b, dc, half), evac_v)
    if kvph < 4:
        return b.cc.n

    voff = 0 if nkv == 32 else nkv
    gi = 0
    for tt in range(8):
        for g0 in range(0, nkv, 8):
            g1 = min(g0 + 8, nkv)
            par = gi % 2
            gi += 1
            stg = b.sb16 if par == 0 else b.pT

            @blk.tensor
            def _(t, tt=tt, g0=g0, g1=g1):
                b.va.wait(t)
                for j in range(g0, g1):
                    i = t.transpose(
                        b.psQ[:, (j - g0) * 128:(j - g0 + 1) * 128].bitcast(BF16)[:, 0:128],
                        b.qk_sb[:, (voff + j) * T + tt * 128:(voff + j) * T + (tt + 1) * 128],
                        b.cstb[:, CB_ID:CB_ID + 128])
                b.pe.inc(i)

            @blk.vector
            def _(v, tt=tt, g0=g0, g1=g1, par=par, stg=stg):
                b.pe.wait(v)
                b.skg_slot[par].wait(v)
                for j in range(g0, g1):
                    i = v.tensor_copy(stg[:, (j - g0) * 128:(j - g0 + 1) * 128],
                                      b.psQ[:, (j - g0) * 128:(j - g0 + 1) * 128].bitcast(BF16)[:, 0:128])
                b.va.inc(i)

            @blk.sync
            def _(sy, tt=tt, g0=g0, g1=g1, par=par, stg=stg):
                b.va.wait(sy)
                i = sy.dma_start(
                    out=vin.rearrange("(h t) d -> t h d", h=nkv)[tt * 128:(tt + 1) * 128, g0:g1, :],
                    in_=stg[:, 0:(g1 - g0) * 128].rearrange("p (h d) -> p h d", h=g1 - g0))
                b.skg_slot[par].inc(i, 16)

    if kvph < 5:
        return b.cc.n

    @blk.gpsimd
    def _(g):
        b.skk.wait(g)
        b.skga.wait(g)
        b.skgb.wait(g)
        rg = [[0, 1, 2, 3], [4, 5, 6, 7]]
        i = g.collective_compute("AllGather", OP.bypass, replica_groups=rg,
                                 ins=[dram["gs_in"].opt()], outs=[dram["gs_out"].opt()])
        b.cc.inc(i, 1)
        i = g.collective_compute("AllGather", OP.bypass, replica_groups=rg,
                                 ins=[kin.opt()], outs=[dram[pfx + "_kout"].opt()])
        b.cc.inc(i, 1)
        i = g.collective_compute("AllGather", OP.bypass, replica_groups=rg,
                                 ins=[vin.opt()], outs=[dram[pfx + "_vout"].opt()])
        b.cc.inc(i, 1)
    return b.cc.n


def emit_rstd_bcast(b, blk, eps):
    @blk.vector
    def _(v):
        b.pe.wait(v)
        i = v.tensor_scalar(row(b, R_RSTD, 1024), b.psQ[0:1, 0:1024], 1.0 / D, None, OP.mult)
        b.va.inc(i); b.va.wait(v)
        i = v.memset(b.s1[0:1, 0:1], eps)
        b.va.inc(i)

    @blk.scalar
    def _(s):
        b.va.wait(s)
        i = s.activation(row(b, R_RSTD, 1024), row(b, R_RSTD, 1024), AF.Sqrt,
                         bias=b.s1[0:1, 0:1])
        b.ac.inc(i)

    @blk.vector
    def _(v):
        b.ac.wait(v)
        i = v.reciprocal(row(b, R_RSTD, 1024), row(b, R_RSTD, 1024))
        b.va.inc(i); b.va.wait(v)
        i = v.tensor_copy(b.pT[0:1, 0:1024], row(b, R_RSTD, 1024))
        b.va.inc(i)

    @blk.tensor
    def _(t):
        b.va.wait(t)
        for half in range(2):
            last = t.matmul(b.psP0[:, half * 512:(half + 1) * 512], onesr(b),
                            b.pT[0:1, half * 512:(half + 1) * 512], start=True, stop=True,
                            skip_group_check=True)
        b.pe.inc(last)

    @blk.vector
    def _(v):
        b.pe.wait(v)
        i = v.tensor_copy(b.bc2[:], b.psP0[:])
        b.va.inc(i)


def emit_gs_readback(b, blk, dram, cc_stats_tok):
    @blk.sync
    def _(sy):
        b.cc.wait(sy, cc_stats_tok)
        i = sy.dma_start(out=row(b, R_GS, 256),
                         in_=dram["gs_out"].rearrange("r k -> (r k)").unsqueeze(0))
        b.sgs.inc(i, 16)

    @blk.vector
    def _(v):
        b.sgs.wait(v)
        i = v.tensor_reduce(row(b, R_GK2, 64),
                            row(b, R_GS, 256).rearrange("p (r k) -> p k r", r=4),
                            axis=AX, op=OP.max)
        b.va.inc(i)



def emit_router(b, blk, dram, gw_name, topk, scale_out, br_name=None):
    E = 8

    @blk.gpsimd
    def _(g):
        b.va.wait(g)
        b.pe.wait(g)
        b.sws.wait(g)
        i = g.dma_start(out=b.sb16[:, 0:NCH * E], in_=dram[gw_name])
        b.sws.inc(i, 16)

    if br_name is not None:
        @blk.gpsimd
        def _(g):
            b.sb0.wait(g)
            i = g.dma_start(out=b.btile[0:8, 64:65], in_=dram[br_name])
            b.sb0.inc(i, 16)

    @blk.tensor
    def _(t):
        b.sws.wait(t)
        b.va.wait(t)
        for half in range(2):
            dst = b.psM[0:E, 0:512] if half == 0 else b.psV[0:E, 0:512]
            for dc in range(NCH):
                last = t.matmul(dst, b.sb16[:, dc * E:(dc + 1) * E], xn_rhs(b, dc, half),
                                start=(dc == 0), stop=(dc == NCH - 1), skip_group_check=True)
        b.pe.inc(last)

    @blk.scalar
    def _(s):
        b.pe.wait(s)
        if br_name is not None:
            b.sb0.wait(s)
            s.activation(b.s1[0:E, 0:512], b.psM[0:E, 0:512], AF.Identity, bias=b.btile[0:8, 64:65])
            i = s.activation(b.s1[0:E, 512:1024], b.psV[0:E, 0:512], AF.Identity, bias=b.btile[0:8, 64:65])
        else:
            s.copy(b.s1[0:E, 0:512], b.psM[0:E, 0:512])
            i = s.copy(b.s1[0:E, 512:1024], b.psV[0:E, 0:512])
        b.ac.inc(i)

    @blk.tensor
    def _(t):
        b.ac.wait(t)
        for tt in range(8):
            last = t.transpose(b.psQ[:, tt * 8:(tt + 1) * 8],
                               b.s1[0:E, tt * 128:(tt + 1) * 128], ident_f32(b)[0:E, 0:E])
        b.pe.inc(last)

    @blk.vector
    def _(v):
        b.pe.wait(v)
        i = v.tensor_copy(b.tm[:, 0:64], b.psQ[:, 0:64])
        b.va.inc(i)

    for tt in range(8):
        @blk.vector
        def _(v, tt=tt):
            b.va.wait(v)
            i = v.reduce_max(b.tm2[:, tt:tt + 1], b.tm[:, tt * 8:(tt + 1) * 8], axis=AX)
            b.va.inc(i); b.va.wait(v)
            i = v.tensor_scalar(b.tm2[:, 8 + tt:9 + tt], b.tm2[:, tt:tt + 1], -1.0, None, OP.mult)
            b.va.inc(i)

        @blk.scalar
        def _(s, tt=tt):
            b.va.wait(s)
            i = s.activation(b.tm3[:, tt * 8:(tt + 1) * 8], b.tm[:, tt * 8:(tt + 1) * 8],
                             AF.Identity, bias=b.tm2[:, 8 + tt:9 + tt])
            b.ac.inc(i)

        if topk == 2:
            @blk.vector
            def _(v, tt=tt):
                b.ac.wait(v)
                tl = b.tm3[:, tt * 8:(tt + 1) * 8]
                i = v.tensor_scalar(b.s2[:, 0:8], tl, 0.0, None, OP.is_ge)
                b.va.inc(i); b.va.wait(v)
                i = v.scalar_tensor_tensor(b.s2[:, 8:16], b.s2[:, 0:8], -1e30, tl, OP.mult, OP.add)
                b.va.inc(i); b.va.wait(v)
                i = v.reduce_max(b.tm2[:, 24 + tt:25 + tt], b.s2[:, 8:16], axis=AX)
                b.va.inc(i)

            @blk.scalar
            def _(s, tt=tt):
                b.va.wait(s)
                s.activation(b.tm3[:, 64 + tt * 8:72 + tt * 8], b.tm3[:, tt * 8:(tt + 1) * 8], AF.Exp)
                i = s.activation(b.tm2[:, 32 + tt:33 + tt], b.tm2[:, 24 + tt:25 + tt], AF.Exp)
                b.ac.inc(i)

            @blk.vector
            def _(v, tt=tt):
                b.ac.wait(v)
                i = v.tensor_scalar(b.tm2[:, 40 + tt:41 + tt], b.tm2[:, 32 + tt:33 + tt], 1.0, None, OP.add)
                b.va.inc(i); b.va.wait(v)
                i = v.reciprocal(b.tm2[:, 48 + tt:49 + tt], b.tm2[:, 40 + tt:41 + tt])
                b.va.inc(i)

            @blk.gpsimd
            def _(g, tt=tt):
                b.va.wait(g)
                b.ac.wait(g)
                i = g.tensor_scalar(b.s2[:, 16:24],
                                b.tm3[:, tt * 8:(tt + 1) * 8], b.tm2[:, 24 + tt:25 + tt],
                                None, OP.is_ge)
                b.pl.inc(i); b.pl.wait(g)
                i = g.tensor_tensor(b.s2[:, 24:32], b.tm3[:, 64 + tt * 8:72 + tt * 8],
                                b.s2[:, 16:24], OP.mult)
                b.pl.inc(i); b.pl.wait(g)
                i = g.tensor_scalar(b.tm3[:, 128 + tt * 8:136 + tt * 8], b.s2[:, 24:32],
                                    b.tm2[:, 48 + tt:49 + tt], scale_out, OP.mult, OP.mult)
                b.pl.inc(i)
        else:
            @blk.scalar
            def _(s, tt=tt):
                i = s.activation(b.tm3[:, 64 + tt * 8:72 + tt * 8],
                                 b.tm3[:, tt * 8:(tt + 1) * 8], AF.Exp)
                b.ac.inc(i)

            @blk.vector
            def _(v, tt=tt):
                b.ac.wait(v)
                i = v.reduce_sum(b.tm2[:, 40 + tt:41 + tt], b.tm3[:, 64 + tt * 8:72 + tt * 8], axis=AX)
                b.va.inc(i); b.va.wait(v)
                i = v.reciprocal(b.tm2[:, 48 + tt:49 + tt], b.tm2[:, 40 + tt:41 + tt])
                b.va.inc(i)

            @blk.gpsimd
            def _(g, tt=tt):
                b.va.wait(g)
                b.ac.wait(g)
                i = g.tensor_scalar(b.tm3[:, 128 + tt * 8:136 + tt * 8],
                                    b.tm3[:, 64 + tt * 8:72 + tt * 8],
                                    b.tm2[:, 48 + tt:49 + tt], scale_out, OP.mult, OP.mult)
                b.pl.inc(i)

    @blk.tensor
    def _(t):
        b.pl.wait(t)
        for tt in range(8):
            last = t.transpose(b.psP0[0:8, tt * 128:(tt + 1) * 128],
                               b.tm3[:, 128 + tt * 8:136 + tt * 8], ident_f32(b))
        b.pe.inc(last)

    @blk.vector
    def _(v):
        b.pe.wait(v)
        i = v.tensor_copy(b.pT[0:8, 0:1024], b.psP0[0:8, 0:1024])
        b.va.inc(i)


def emit_lora_z(b, blk, dram, a_name, scaling, wb_off=17 * T):
    @blk.tensor
    def _(t):
        b.va.wait(t)
        for half in range(2):
            last = t.matmul(b.psQ[:, half * 512:(half + 1) * 512],
                            b.cstb[0:8, CB_E8:CB_E8 + 128],
                            b.pT[0:8, half * 512:(half + 1) * 512],
                            start=True, stop=True, skip_group_check=True)
        b.pe.inc(last)

    @blk.vector
    def _(v):
        b.pe.wait(v)
        i = v.tensor_copy(b.qk_sb[:, wb_off:wb_off + 1024], b.psQ[:, 0:1024])
        b.va.inc(i)

    @blk.gpsimd
    def _(g):
        b.pe.wait(g)
        b.sws.wait(g)
        i = g.dma_start(out=b.wst[:, 0:D], in_=dram[a_name][0])
        b.sws.inc(i, 16)

    @blk.tensor
    def _(t):
        b.sws.wait(t)
        b.va.wait(t)
        for half in range(2):
            for dc in range(NCH):
                last = t.matmul(b.psP0[:, half * 512:(half + 1) * 512],
                                b.wst[:, dc * 128:(dc + 1) * 128], xn_rhs(b, dc, half),
                                start=(dc == 0), stop=(dc == NCH - 1), skip_group_check=True)
        b.pe.inc(last)

    @blk.vector
    def _(v):
        b.pe.wait(v)
        i = v.scalar_tensor_tensor(b.qh_sb[:, 0:1024], b.psP0[:, 0:1024], scaling,
                                   b.qk_sb[:, wb_off:wb_off + 1024], OP.mult, OP.mult)
        b.va.inc(i)


def emit_ffn_base(b, blk, dram, w_name, b_name, bsm_name, xsrc):
    xmid = dram["xmid"]

    @blk.gpsimd
    def _(g):
        b.sb0.wait(g)
        i = g.dma_start(out=b.btile[:, 0:32], in_=dram[b_name])
        b.sb0.inc(i, 16)
    bias_tok = b.sb0.n

    def extra(t, idx, j, ps):
        for half in range(2):
            last = t.matmul(ps[:, half * 512:(half + 1) * 512],
                            b.wsm[:, (idx % 2) * 128:(idx % 2 + 1) * 128],
                            b.qh_sb[:, half * 512:half * 512 + 512],
                            start=False, stop=True, skip_group_check=True)
        return last

    def evac(idx, j, ps, pe_t):
        @blk.sync
        def _(sy, idx=idx, j=j):
            if idx >= 2:
                b.va.wait(sy)
            i = sy.dma_start(out=b.xf[:, (idx % 2) * T:(idx % 2 + 1) * T],
                             in_=xsrc[j * 128:(j + 1) * 128, :])
            b.sx_slot[idx % 2].inc(i, 16)
        sx_tok = b.sx_slot[idx % 2].n

        @blk.scalar
        def _(s, idx=idx, j=j, ps=ps):
            b.pe.wait(s, pe_t[idx])
            b.va.wait(s)
            if idx == 0:
                b.sb0.wait(s, bias_tok)
            i = s.activation(b.s1[:], ps[:], AF.Identity, bias=b.btile[:, j:j + 1])
            b.ac.inc(i)

        @blk.vector
        def _(v, idx=idx, j=j, sx_tok=sx_tok):
            b.ac.wait(v)
            b.sx_slot[idx % 2].wait(v, sx_tok)
            i = v.tensor_tensor(b.s2[:], b.s1[:], b.xf[:, (idx % 2) * T:(idx % 2 + 1) * T], OP.add)
            b.va.inc(i)

        @blk.sync
        def _(sy, idx=idx, j=j):
            b.va.wait(sy)
            i = sy.dma_start(out=xmid[j * 128:(j + 1) * 128, :], in_=b.s2[:])
            b.st.inc(i, 16)
        return (b.va, b.va.n)

    emit_proj(b, blk, dram[w_name], list(range(NCH)),
              lambda dc, half: xn_rhs(b, dc, half), evac, extra_mm=extra,
              wsm_dram=dram[bsm_name], pre_wait=lambda t: b.va.wait(t))



def emit_attention(b, blk, dram, cfg, xdst):
    nkv, grp = cfg["nkv"], cfg["group"]
    kout, vout = dram[cfg["pfx"] + "_kout"], dram[cfg["pfx"] + "_vout"]

    for kv in range(nkv):
        @blk.sync
        def _(sy, kv=kv):
            b.cc.wait(sy, cfg["cc_v"])
            if kv >= 1:
                b.pe.wait(sy)
            i = sy.dma_start(out=b.kv_sb[:, 0:4096].rearrange("p (r t) -> p r t", r=4),
                             in_=kout.rearrange("(r d) t -> d r t", r=4)[kv * 128:(kv + 1) * 128, :, :])
            b.skr.inc(i, 16)
            vv = vout.rearrange("(r h c p) d -> r h p c d", r=4, h=nkv, c=8)
            for r in range(4):
                i = sy.dma_start(
                    out=b.kv_sb[:, 4096 + r * 1024:4096 + (r + 1) * 1024].rearrange(
                        "p (c d) -> p c d", c=8),
                    in_=vv[r, kv])
                b.skr.inc(i, 16)
        kv_tok = b.skr.n

        for hh in range(grp):
            emit_attn_head(b, blk, dram, cfg, kv * grp + hh, kv, kv_tok, xdst)

    if cfg["out"] != "residual":
        emit_o_proj(b, blk, dram, cfg, xdst)


def emit_attn_head(b, blk, dram, cfg, h, kv, kv_tok, xdst):
    if cfg["q_mode"] == "jit":
        wq = dram[cfg["wq_name"]]

        @blk.gpsimd
        def _(g, h=h):
            b.pe.wait(g)
            i = g.dma_start(out=b.wst[:, (h % 2) * D:(h % 2 + 1) * D], in_=wq[h])
            b.sw_slot[h % 2].inc(i, 16)
        sw_tok = b.sw_slot[h % 2].n

        @blk.tensor
        def _(t, h=h, sw_tok=sw_tok):
            b.sw_slot[h % 2].wait(t, sw_tok)
            b.ac.wait(t)
            b.va.wait(t)
            for half in range(2):
                for dc in range(NCH):
                    last = t.matmul(b.psP0[:, half * 512:(half + 1) * 512],
                                    b.wst[:, (h % 2) * D + dc * 128:(h % 2) * D + (dc + 1) * 128],
                                    xn_rhs(b, dc, half),
                                    start=(dc == 0), stop=(dc == NCH - 1), skip_group_check=True)
            b.pe.inc(last)

        @blk.scalar
        def _(s, h=h):
            b.pe.wait(s)
            b.va.wait(s)
            b.sb0.wait(s)
            bq = cfg.get("bq_col")
            if cfg.get("rope"):
                s.activation(b.s1[:], b.psP0[:], AF.Identity, bias=bq(h, 0))
                s.activation(b.s2[0:64, :], b.psP0[64:128, :], AF.Identity, bias=bq(h, 1))
                i = s.activation(b.s2[64:128, :], b.psP0[0:64, :], AF.Identity, bias=bq(h, 2))
            elif bq is not None:
                i = s.activation(b.s1[:], b.psP0[:], AF.Identity, bias=bq(h, 0))
            else:
                i = s.copy(b.s1[:], b.psP0[:])
            b.ac.inc(i)

        @blk.vector
        def _(v, h=h):
            b.ac.wait(v)
            if cfg.get("rope"):
                b.sb0.wait(v)
                v.tensor_tensor(b.s1[:], b.s1[:], b.cbig[:, 0:T], OP.mult)
                v.tensor_tensor(b.s2[:], b.s2[:], b.cbig[:, T:2 * T], OP.mult)
                i = v.tensor_tensor(b.qh_sb[:], b.s1[:], b.s2[:], OP.add)
            else:
                i = v.tensor_copy(b.qh_sb[:], b.s1[:])
            b.va.inc(i)
        qsrc = b.qh_sb[:]
    else:
        qsrc = b.qk_sb[:, h * T:(h + 1) * T]

    @blk.scalar
    def _(s, h=h):
        b.va.wait(s)
        i = s.activation(b.s2[:], qsrc, AF.Square)
        b.ac.inc(i)

    @blk.vector
    def _(v, h=h):
        b.ac.wait(v)
        b.pe.wait(v)
        i = v.tensor_copy(b.sb16[:], b.s2[:])
        b.va.inc(i)

    @blk.tensor
    def _(t, h=h):
        b.va.wait(t)
        t.matmul(b.psM[0:1, 0:512], onesc(b), b.sb16[:, 0:512], start=True, stop=True,
                 skip_group_check=True)
        i = t.matmul(b.psV[0:1, 0:512], onesc(b), b.sb16[:, 512:1024], start=True,
                     stop=True, skip_group_check=True)
        b.pe.inc(i)

    @blk.vector
    def _(v, h=h, kv=kv):
        b.pe.wait(v)
        v.reduce_max(row(b, R_C, 1), b.psM[0:1, 0:512], axis=AX)
        i = v.reduce_max(row(b, R_C + 1, 1), b.psV[0:1, 0:512], axis=AX)
        b.va.inc(i); b.va.wait(v)
        i = v.tensor_tensor(row(b, R_C + 2, 1), row(b, R_C, 1), row(b, R_C + 1, 1), OP.max)
        b.va.inc(i); b.va.wait(v)
        i = v.tensor_tensor(row(b, R_C + 3, 1), row(b, R_C + 2, 1),
                            row(b, R_GK2 + kv, 1), OP.mult)
        b.va.inc(i)

    @blk.scalar
    def _(s, h=h):
        b.va.wait(s)
        i = s.activation(row(b, R_C + 4, 1), row(b, R_C + 3, 1), AF.Sqrt)
        b.ac.inc(i)

    @blk.vector
    def _(v, h=h):
        b.ac.wait(v)
        i = v.tensor_scalar(row(b, R_C + 5, 1), row(b, R_C + 4, 1), -ISQ, None, OP.mult)
        b.va.inc(i); b.va.wait(v)
        i = v.tensor_copy(b.sb16[0:1, 0:1], row(b, R_C + 5, 1))
        b.va.inc(i)

    @blk.tensor
    def _(t, h=h):
        b.va.wait(t)
        i = t.matmul(b.psM[:, 0:1], onesr(b), b.sb16[0:1, 0:1], start=True, stop=True,
                     skip_group_check=True)
        b.pe.inc(i)

    @blk.vector
    def _(v, h=h):
        b.pe.wait(v)
        i = v.tensor_copy(b.biasC[:, h % 16:h % 16 + 1], b.psM[:, 0:1])
        b.va.inc(i)
    biasC_tok = b.va.n

    nchunks = 32
    for qt in range(2):
        ac_e, pe_q = {}, {}
        for c in range(nchunks):
            @blk.tensor
            def _(t, c=c, qt=qt, h=h):
                if c == 0 and qt == 0:
                    b.skr.wait(t, kv_tok)
                if c >= 2:
                    b.ac.wait(t, ac_e[c - 2])
                i = t.matmul(b.psQ[:, (c % 2) * 512:(c % 2 + 1) * 512],
                             b.kv_sb[:, c * 128:(c + 1) * 128],
                             qsrc[:, qt * 512:(qt + 1) * 512],
                             start=True, stop=True, skip_group_check=True)
                pe_q[c] = b.pe.inc(i)

            @blk.scalar
            def _(s, c=c, qt=qt, h=h):
                b.pe.wait(s, pe_q[c])
                if c == 0:
                    b.va.wait(s, biasC_tok)
                i = s.activation(b.pT[:, (c % 2) * 512:(c % 2 + 1) * 512],
                                 b.psQ[:, (c % 2) * 512:(c % 2 + 1) * 512], AF.Exp,
                                 bias=b.biasC[:, h % 16:h % 16 + 1], scale=ISQ)
                ac_e[c] = b.ac.inc(i)

            @blk.tensor
            def _(t, c=c, qt=qt):
                b.ac.wait(t, ac_e[c])
                if c == 0:
                    b.va.wait(t)
                t.matmul(b.psM[0:1, 0:512], onesc(b), b.pT[:, (c % 2) * 512:(c % 2 + 1) * 512],
                         start=(c == 0), stop=(c == nchunks - 1), skip_group_check=True)
                i = t.matmul(b.psV[:, 0:512], b.kv_sb[:, 4096 + c * 128:4096 + (c + 1) * 128],
                             b.pT[:, (c % 2) * 512:(c % 2 + 1) * 512],
                             start=(c == 0), stop=(c == nchunks - 1), skip_group_check=True)
                b.pe.inc(i)

        @blk.vector
        def _(v, qt=qt, h=h):
            b.pe.wait(v)
            i = v.reciprocal(row(b, R_RECIP, 512), b.psM[0:1, 0:512])
            b.va.inc(i); b.va.wait(v)
            i = v.tensor_copy(b.sb16[0:1, 0:512], row(b, R_RECIP, 512))
            b.va.inc(i)

        @blk.tensor
        def _(t, qt=qt):
            b.va.wait(t)
            i = t.matmul(b.psM[:, 0:512], onesr(b), b.sb16[0:1, 0:512], start=True,
                         stop=True, skip_group_check=True)
            b.pe.inc(i)

        @blk.vector
        def _(v, qt=qt, h=h):
            b.pe.wait(v)
            v.tensor_copy(b.bc1[:, 0:512], b.psM[:, 0:512])
            och = 16 if cfg["out"] == "residual" else h
            i = v.tensor_tensor(b.qk_sb[:, och * T + qt * 512:och * T + qt * 512 + 512],
                                b.psV[:, 0:512], b.bc1[:, 0:512], OP.mult)
            b.va.inc(i)

    if cfg["out"] == "residual":
        @blk.sync
        def _(sy, h=h):
            b.va.wait(sy)
            i = sy.dma_start(out=b.xf[:, (h % 2) * T:(h % 2 + 1) * T],
                             in_=dram["xmid"][h * 128:(h + 1) * 128, :])
            b.sx_slot[h % 2].inc(i, 16)
        sx_tok = b.sx_slot[h % 2].n

        @blk.vector
        def _(v, h=h, sx_tok=sx_tok):
            b.sx_slot[h % 2].wait(v, sx_tok)
            i = v.tensor_tensor(b.s1[:], b.qk_sb[:, 16 * T:17 * T],
                                b.xf[:, (h % 2) * T:(h % 2 + 1) * T], OP.add)
            b.va.inc(i)

        @blk.sync
        def _(sy, h=h):
            b.va.wait(sy)
            i = sy.dma_start(out=xdst[h * 128:(h + 1) * 128, :], in_=b.s1[:])
            b.st.inc(i, 16)


def emit_o_proj(b, blk, dram, cfg, xdst):
    xmid = dram["xmid"]

    if cfg.get("bo_name"):
        @blk.gpsimd
        def _(g):
            b.sb0.wait(g)
            i = g.dma_start(out=b.btile[:, 32:64], in_=dram[cfg["bo_name"]])
            b.sb0.inc(i, 16)
    bias_tok = b.sb0.n

    def evac(idx, j, ps, pe_t):
        @blk.sync
        def _(sy, idx=idx, j=j):
            if idx >= 2:
                b.va.wait(sy)
            i = sy.dma_start(out=b.xf[:, (idx % 2) * T:(idx % 2 + 1) * T],
                             in_=xmid[j * 128:(j + 1) * 128, :])
            b.sx_slot[idx % 2].inc(i, 16)
        sx_tok = b.sx_slot[idx % 2].n

        @blk.scalar
        def _(s, idx=idx, j=j, ps=ps):
            b.pe.wait(s, pe_t[idx])
            b.va.wait(s)
            if idx == 0:
                b.sb0.wait(s, bias_tok)
            if cfg.get("bo_name"):
                i = s.activation(b.s1[:], ps[:], AF.Identity, bias=b.btile[:, 32 + j:33 + j])
            else:
                i = s.copy(b.s1[:], ps[:])
            b.ac.inc(i)

        @blk.vector
        def _(v, idx=idx, j=j, sx_tok=sx_tok):
            b.ac.wait(v)
            b.sx_slot[idx % 2].wait(v, sx_tok)
            i = v.tensor_tensor(b.s2[:], b.s1[:], b.xf[:, (idx % 2) * T:(idx % 2 + 1) * T], OP.add)
            b.va.inc(i)

        @blk.sync
        def _(sy, idx=idx, j=j):
            b.va.wait(sy)
            i = sy.dma_start(out=xdst[j * 128:(j + 1) * 128, :], in_=b.s2[:])
            b.st.inc(i, 16)
        return (b.va, b.va.n)

    emit_proj(b, blk, dram[cfg["out"]], list(range(NCH)),
              lambda dc, half: b.qk_sb[:, dc * T + half * 512:dc * T + half * 512 + 512],
              evac)



def emit_block0(b, blk, dram, xsrc, xdst, dbg):
    ph = 99
    if dbg:
        m = _re.search(r"ph(\d+)", dbg)
        if m:
            ph = int(m.group(1))
    @blk.gpsimd
    def _(g):
        b.sb0.wait(g)
        i = g.dma_start(out=b.cbig[:, 0:T], in_=dram["l0_cc"]); b.sb0.inc(i, 16)
        i = g.dma_start(out=b.cbig[:, T:2 * T], in_=dram["l0_ss"]); b.sb0.inc(i, 16)
        i = g.dma_start(out=b.btile[:, 32:64], in_=dram["l0_bq"]); b.sb0.inc(i, 16)
        i = g.dma_start(out=b.btile[:, 80:88], in_=dram["l0_bk"]); b.sb0.inc(i, 16)
        i = g.dma_start(out=b.btile[:, 88:96], in_=dram["l0_bv"]); b.sb0.inc(i, 16)

    def bk_col(j, swap=0):
        col = b.btile[:, 80 + j:81 + j]
        return [col, b.btile[64:128, 80 + j:81 + j], b.btile[0:64, 80 + j:81 + j]][swap]

    def bv_col(j):
        return b.btile[:, 88 + j:89 + j]

    def bq_col(h, swap=0):
        col = b.btile[:, 32 + h:33 + h]
        return [col, b.btile[64:128, 32 + h:33 + h], b.btile[0:64, 32 + h:33 + h]][swap]

    if ph < 1:
        return
    kvph = 99
    if dbg:
        m2 = _re.search(r"kv(\d+)", dbg)
        if m2:
            kvph = int(m2.group(1))
    cc_end = emit_kv_and_gather(b, blk, dram, "g0", 8, dram["l0_wk"], dram["l0_wv"],
                                bk_col, bv_col, rope=True, kvph=kvph)
    if ph < 2:
        return
    emit_router(b, blk, dram, "l0_gw", 2, 2.0)
    if ph < 3:
        return
    emit_lora_z(b, blk, dram, "l0_A", 1.0)
    if ph < 4:
        return
    emit_ffn_base(b, blk, dram, "l0_bw", "l0_bb", "l0_B", xsrc)
    if ph < 5:
        if dbg and "mid" in dbg:
            dbg_dump_dram(b, blk, dram, dram["xmid"])
        if dbg and "ffn" in dbg:
            @blk.sync
            def _(sy):
                b.va.wait(sy)
                b.st.wait(sy)
                b.pl.wait(sy)
                i = sy.dma_start(out=dram["dbg"][0:128, 0:192], in_=b.tm3[:]); b.st.inc(i, 16)
                i = sy.dma_start(out=dram["dbg"][128:256, 0:512], in_=b.pT[:].bitcast(F32)); b.st.inc(i, 16)
                i = sy.dma_start(out=dram["dbg"][256:384, 0:512], in_=b.qh_sb[:].bitcast(F32)); b.st.inc(i, 16)
                i = sy.dma_start(out=dram["dbg"][384:512, 0:512],
                                 in_=b.qk_sb[:, 17 * T:18 * T].bitcast(F32)[:, 0:512]); b.st.inc(i, 16)
                i = sy.dma_start(out=dram["dbg"][512:640, 0:512], in_=b.sb16[:].bitcast(F32)); b.st.inc(i, 16)
                i = sy.dma_start(out=dram["dbg"][640:768, 0:512],
                                 in_=b.xn_sb[:, 0:1024].bitcast(F32)[:, 0:512]); b.st.inc(i, 16)
                i = sy.dma_start(out=dram["dbg"][768:896, 0:64], in_=b.tm[:]); b.st.inc(i, 16)
        return
    emit_gs_readback(b, blk, dram, cc_end - 2)
    if ph < 6:
        return
    cfg = dict(pfx="g0", nkv=8, group=4, q_mode="jit", wq_name="l0_wq",
               bq_col=bq_col, rope=True, out="residual",
               cc_k=cc_end - 1, cc_v=cc_end)
    emit_attention(b, blk, dram, cfg, xdst)
    if dbg and "mid" in dbg:
        dbg_dump_dram(b, blk, dram, dram["xmid"])
    if dbg and "rows" in dbg:
        @blk.sync
        def _(sy):
            b.va.wait(sy)
            b.st.wait(sy)
            i = sy.dma_start(out=dram["dbg"][0:128, :], in_=b.rows[:, 0:1024])
            b.st.inc(i, 16)
            i = sy.dma_start(out=dram["dbg"][128:256, :], in_=b.rows[:, 1024:2048])
            b.st.inc(i, 16)
            i = sy.dma_start(out=dram["dbg"][256:384, 0:16], in_=b.biasC[:, 0:16])
            b.st.inc(i, 16)
            i = sy.dma_start(out=dram["dbg"][384:512, 0:512],
                             in_=b.qk_sb[:, 16 * T:17 * T].bitcast(F32)[:, 0:512])
            b.st.inc(i, 16)




def emit_bm01(b, blk, dram):
    mask = dram["l1_mask"]
    for nc_ in range(NCH):
        for qtr in range(4):
            sl = (nc_ * 4 + qtr) % 2

            @blk.sync
            def _(sy, nc_=nc_, qtr=qtr, sl=sl):
                b.va.wait(sy)
                b.ac.wait(sy)
                i = sy.dma_start(
                    out=b.xf[:, sl * T:(sl + 1) * T].bitcast(I32),
                    in_=mask[nc_ * 128:(nc_ + 1) * 128, qtr * 1024:(qtr + 1) * 1024])
                b.sx_slot[sl].inc(i, 16)

            @blk.vector
            def _(v, nc_=nc_, qtr=qtr, sl=sl):
                b.sx_slot[sl].wait(v)
                i = v.tensor_copy(b.s1[:], b.xf[:, sl * T:(sl + 1) * T].bitcast(I32))
                b.va.inc(i); b.va.wait(v)
                i = v.tensor_reduce(b.s2[:, qtr * 32:(qtr + 1) * 32],
                                    b.s1[:].rearrange("p (kb q) -> p kb q", q=32),
                                    axis=AX, op=OP.add)
                b.va.inc(i); b.va.wait(v)
                if qtr == 3:
                    i = v.tensor_copy(b.sb16[:, 0:128], b.s2[:, 0:128])
                    b.va.inc(i)

        @blk.tensor
        def _(t, nc_=nc_):
            b.va.wait(t)
            i = t.matmul(b.psM[0:4, 0:128], b.cstb[:, CB_G32:CB_G32 + 4], b.sb16[:, 0:128],
                         start=True, stop=True, skip_group_check=True)
            b.pe.inc(i)

        @blk.scalar
        def _(s, nc_=nc_):
            b.pe.wait(s)
            b.st.wait(s)
            i = s.copy(b.s1[0:4, 0:128], b.psM[0:4, 0:128])
            b.ac.inc(i)

        @blk.sync
        def _(sy, nc_=nc_):
            b.ac.wait(sy)
            i = sy.dma_start(out=dram["bm01d"][nc_ * 4:(nc_ + 1) * 4, :], in_=b.s1[0:4, 0:128])
            b.st.inc(i, 16)

    @blk.sync
    def _(sy):
        b.st.wait(sy)
        i = sy.dma_start(out=b.s2[:, 0:128], in_=dram["bm01d"])
        b.sgs.inc(i, 16)

    @blk.vector
    def _(v):
        b.sgs.wait(v)
        i = v.tensor_scalar(b.tm3[:, 0:128], b.s2[:, 0:128], 0.0, None, OP.is_gt)
        b.va.inc(i)


def emit_emgen(b, blk, dram):
    @blk.gpsimd
    def _(g):
        b.sws.wait(g)
        i = g.dma_start(out=b.kv_sb[:, 0:4096], in_=dram["qexp"])
        b.sws.inc(i, 16)

    for kc in range(NCH):
        @blk.tensor
        def _(t, kc=kc):
            b.va.wait(t)
            if kc == 0:
                b.sws.wait(t)
            b.st.wait(t)
            i = t.transpose(b.psM[0:4, 0:128], b.tm3[:, 0:128][:, 4 * kc:4 * kc + 4],
                            ident_f32(b))
            b.pe.inc(i)

        @blk.vector
        def _(v, kc=kc):
            b.pe.wait(v)
            i = v.tensor_copy(b.sb16[0:4, 0:128], b.psM[0:4, 0:128])
            b.va.inc(i)

        @blk.tensor
        def _(t, kc=kc):
            b.va.wait(t)
            i = t.matmul(b.psQ[:, 0:128], b.sb16[0:4, 0:128],
                         b.cstb[0:4, CB_P32:CB_P32 + 128],
                         start=True, stop=True, skip_group_check=True)
            b.pe.inc(i)

        @blk.vector
        def _(v, kc=kc):
            b.pe.wait(v)
            i = v.tensor_copy(b.sb16[:, 0:128], b.psQ[:, 0:128])
            b.va.inc(i)

        for pair in range(4):
            @blk.tensor
            def _(t, kc=kc, pair=pair):
                b.va.wait(t)
                if pair > 0:
                    b.ac.wait(t)
                last = None
                for k2 in range(2):
                    last = t.matmul(b.psP0[:, k2 * 512:(k2 + 1) * 512],
                                    b.sb16[:, 0:128],
                                    b.kv_sb[:, (pair * 2 + k2) * 512:(pair * 2 + k2 + 1) * 512],
                                    start=True, stop=True, skip_group_check=True)
                b.pe.inc(last)

            @blk.scalar
            def _(s, kc=kc, pair=pair):
                b.pe.wait(s)
                i2 = s.copy(b.kv_sb[:, 4096 + pair * 1024:4096 + (pair + 1) * 1024],
                            b.psP0[:, 0:1024])
                b.ac.inc(i2)

        @blk.sync
        def _(sy, kc=kc):
            b.ac.wait(sy)
            i = sy.dma_start(out=dram["em_scr"][kc], in_=b.kv_sb[:, 4096:8192])
            b.st.inc(i, 16)


def emit_ffn_lora_only(b, blk, dram, bsm_name, xsrc):
    xmid = dram["xmid"]
    w_t, pe_t = {}, {}
    for j in range(NCH):
        @blk.gpsimd
        def _(g, j=j):
            if j >= 2:
                b.pe.wait(g, pe_t[j - 2])
            else:
                b.pe.wait(g)
            i = g.dma_start(out=b.wsm[:, (j % 2) * 128:(j % 2 + 1) * 128],
                            in_=dram[bsm_name][j])
            w_t[j] = b.sw_slot[j % 2].inc(i, 16)

        @blk.sync
        def _(sy, j=j):
            if j >= 2:
                b.va.wait(sy)
            i = sy.dma_start(out=b.xf[:, (j % 2) * T:(j % 2 + 1) * T],
                             in_=xsrc[j * 128:(j + 1) * 128, :])
            b.sx_slot[j % 2].inc(i, 16)
        sx_tok = b.sx_slot[j % 2].n

        @blk.tensor
        def _(t, j=j):
            b.sw_slot[j % 2].wait(t, w_t[j])
            if j == 0:
                b.va.wait(t)
            b.ac.wait(t) if j < 2 else b.ac.wait(t)
            ps = [b.psP0, b.psP1][j % 2]
            for half in range(2):
                last = t.matmul(ps[:, half * 512:(half + 1) * 512],
                                b.wsm[:, (j % 2) * 128:(j % 2 + 1) * 128],
                                b.qh_sb[:, half * 512:half * 512 + 512],
                                start=True, stop=True, skip_group_check=True)
            pe_t[j] = b.pe.inc(last)

        @blk.scalar
        def _(s, j=j):
            b.pe.wait(s, pe_t[j])
            b.va.wait(s)
            i = s.copy(b.s1[:], [b.psP0, b.psP1][j % 2][:])
            b.ac.inc(i)

        @blk.vector
        def _(v, j=j, sx_tok=sx_tok):
            b.ac.wait(v)
            b.sx_slot[j % 2].wait(v, sx_tok)
            i = v.tensor_tensor(b.s2[:], b.s1[:], b.xf[:, (j % 2) * T:(j % 2 + 1) * T], OP.add)
            b.va.inc(i)

        @blk.sync
        def _(sy, j=j):
            b.va.wait(sy)
            i = sy.dma_start(out=xmid[j * 128:(j + 1) * 128, :], in_=b.s2[:])
            b.st.inc(i, 16)


def emit_masked_mm(b, blk, dram, xdst):
    xmid = dram["xmid"]
    w_t, em_t, pe_t, va_t = {}, {}, {}, {}
    for j in range(NCH):
        @blk.gpsimd
        def _(g, j=j):
            if j >= 2:
                b.pe.wait(g, pe_t[j - 2])
            else:
                b.pe.wait(g)
            sl = b.sw_slot[j % 2]
            i = g.dma_start(out=b.wst[:, (j % 2) * D:(j % 2 + 1) * D], in_=dram["l1_w"][j])
            w_t[j] = sl.inc(i, 16)
            i = g.dma_start(
                out=b.kv_sb[:, (j % 2) * 4096:(j % 2 + 1) * 4096].rearrange(
                    "p (kc n) -> p kc n", kc=NCH),
                in_=dram["em_scr"][:, :, j * 128:(j + 1) * 128].rearrange("kc p n -> p kc n"))
            em_t[j] = sl.inc(i, 16)

        @blk.vector
        def _(v, j=j):
            b.sw_slot[j % 2].wait(v, em_t[j])
            if j >= 2:
                b.pe.wait(v, pe_t[j - 2])
            i = v.tensor_tensor(b.wst[:, (j % 2) * D:(j % 2 + 1) * D],
                                b.wst[:, (j % 2) * D:(j % 2 + 1) * D],
                                b.kv_sb[:, (j % 2) * 4096:(j % 2 + 1) * 4096], OP.mult)
            va_t[j] = b.va.inc(i)

        @blk.sync
        def _(sy, j=j):
            if j >= 2:
                b.va.wait(sy)
            i = sy.dma_start(out=b.xf[:, (j % 2) * T:(j % 2 + 1) * T],
                             in_=xmid[j * 128:(j + 1) * 128, :])
            b.sx_slot[j % 2].inc(i, 16)
        sx_tok = b.sx_slot[j % 2].n

        @blk.tensor
        def _(t, j=j):
            b.va.wait(t, va_t[j])
            b.ac.wait(t)
            ps = [b.psP0, b.psP1][j % 2]
            w = b.wst[:, (j % 2) * D:(j % 2 + 1) * D]
            for half in range(2):
                for dc in range(NCH):
                    last = t.matmul(ps[:, half * 512:(half + 1) * 512],
                                    w[:, dc * 128:(dc + 1) * 128],
                                    xn_rhs(b, dc, half),
                                    start=(dc == 0), stop=(dc == NCH - 1),
                                    skip_group_check=True)
            pe_t[j] = b.pe.inc(last)

        @blk.scalar
        def _(s, j=j):
            b.pe.wait(s, pe_t[j])
            b.va.wait(s)
            i = s.copy(b.s1[:], [b.psP0, b.psP1][j % 2][:])
            b.ac.inc(i)

        @blk.vector
        def _(v, j=j, sx_tok=sx_tok):
            b.ac.wait(v)
            b.sx_slot[j % 2].wait(v, sx_tok)
            i = v.tensor_tensor(b.s2[:], b.s1[:], b.xf[:, (j % 2) * T:(j % 2 + 1) * T], OP.add)
            b.va.inc(i)

        @blk.sync
        def _(sy, j=j):
            b.va.wait(sy)
            i = sy.dma_start(out=xdst[j * 128:(j + 1) * 128, :], in_=b.s2[:])
            b.st.inc(i, 16)


def emit_block1(b, blk, dram, xsrc, xdst, dbg):
    ph = 99
    if dbg:
        m = _re.search(r"ph(\d+)", dbg)
        if m:
            ph = int(m.group(1))
    emit_bm01(b, blk, dram)
    if ph < 1:
        return
    emit_emgen(b, blk, dram)
    if ph < 2:
        return
    emit_router(b, blk, dram, "l1_wr", None, 16.0, br_name="l1_br")
    emit_lora_z(b, blk, dram, "l1_A", 1.0)
    if ph < 3:
        return
    emit_ffn_lora_only(b, blk, dram, "l1_B", xsrc)
    if ph < 4:
        if dbg and "mid" in dbg:
            dbg_dump_dram(b, blk, dram, dram["xmid"])
        return
    emit_masked_mm(b, blk, dram, xdst)




def _vf(b, v, i):
    b.va.inc(i)
    b.va.wait(v)


def emit_ffn_l2(b, blk, dram, xsrc):
    ZR, AR, UR, VR, MR, SR, LR, HR, XR = 0, 128, 256, 288, 320, 352, 416, 448, 576

    pe_x = []
    for ch in range(NCH):
        @blk.scalar
        def _(s, ch=ch):
            b.va.wait(s)
            if ch == 0:
                b.pl.wait(s)
            i = s.activation(b.s2[:], b.xn_sb[:, ch * T:(ch + 1) * T], AF.Square)
            b.ac.inc(i)

        @blk.vector
        def _(v, ch=ch):
            b.ac.wait(v)
            if ch == 0:
                b.pe.wait(v)
            else:
                b.pe.wait(v, pe_x[ch - 1])
            i = v.tensor_copy(b.pT[:], b.s2[:])
            b.va.inc(i)

        @blk.tensor
        def _(t, ch=ch):
            b.va.wait(t)
            for half in range(2):
                last = t.matmul(b.psQ[0:1, half * 512:(half + 1) * 512], onesc(b),
                                b.pT[:, half * 512:(half + 1) * 512],
                                start=(ch == 0), stop=(ch == NCH - 1), skip_group_check=True)
            pe_x.append(b.pe.inc(last))
    emit_rstd_bcast(b, blk, 1.1920929e-07)

    @blk.gpsimd
    def _(g):
        b.va.wait(g)
        b.pe.wait(g)
        b.sws.wait(g)
        i = g.dma_start(out=b.sb16[:, 0:NCH * 24], in_=dram["l2_wrpp"])
        b.sws.inc(i, 16)
        b.sb0.wait(g)
        i = g.dma_start(out=b.btile[0:24, 32:33], in_=dram["l2_b24"])
        b.sb0.inc(i, 16)
        i = g.dma_start(out=b.btile[0:1, 33:34], in_=dram["l2_gate"])
        b.sb0.inc(i, 16)

    @blk.tensor
    def _(t):
        b.sws.wait(t)
        b.va.wait(t)
        for half in range(2):
            dst = b.psM[0:24, 0:512] if half == 0 else b.psV[0:24, 0:512]
            for dc in range(NCH):
                last = t.matmul(dst, b.sb16[:, dc * 24:(dc + 1) * 24], xn_rhs(b, dc, half),
                                start=(dc == 0), stop=(dc == NCH - 1), skip_group_check=True)
        b.pe.inc(last)

    @blk.vector
    def _(v):
        b.pe.wait(v)
        b.sb0.wait(v)
        v.tensor_tensor(b.s1[0:24, 0:512], b.psM[0:24, 0:512], b.bc2[0:24, 0:512], OP.mult)
        i = v.tensor_tensor(b.s1[0:24, 512:1024], b.psV[0:24, 0:512], b.bc2[0:24, 512:1024], OP.mult)
        _vf(b, v, i)
        i = v.tensor_scalar(b.s1[0:24, :], b.s1[0:24, :], b.btile[0:24, 32:33], None, OP.add)
        b.va.inc(i)

    for tt in range(8):
        @blk.tensor
        def _(t, tt=tt):
            b.va.wait(t)
            i = t.transpose(b.psQ[:, 0:24], b.s1[0:24, tt * 128:(tt + 1) * 128],
                            ident_f32(b)[0:24, 0:24])
            b.pe.inc(i)

        @blk.vector
        def _(v, tt=tt):
            b.pe.wait(v)
            v.tensor_scalar(b.s2[:, ZR + tt * 16:ZR + (tt + 1) * 16], b.psQ[:, 0:16],
                            10.0, None, OP.mult)
            v.tensor_copy(b.tm[:, tt * 4:(tt + 1) * 4], b.psQ[:, 16:20])
            i = v.tensor_copy(b.tm[:, 32 + tt * 4:32 + (tt + 1) * 4], b.psQ[:, 20:24])
            b.va.inc(i)

    Z3i = lambda off, j: b.s2[:, off + j:off + 113 + j].rearrange(
        "p (a r) -> p a r", a=8)[:, :, 0:1].squeeze(2) if False else None

    def ap_ti(off, j):
        return b.s2[:, off + j:off + j + 126].rearrange("p c -> p c").unsqueeze(0) if False else None

    @blk.vector
    def _(v):
        b.va.wait(v)
        i = v.memset(b.s2[:, VR:VR + 32], 0.0)
        b.va.inc(i)

    def sub_bcast_j(v, dst_off, src_off, row_off):
        for j in range(4):
            sl = [None]
            i = v.tensor_tensor(
                b.s2[:, dst_off:dst_off + 128].rearrange("p (a i j) -> p a i j", a=8, i=4)[:, :, :, j],
                b.s2[:, src_off:src_off + 128].rearrange("p (a i j) -> p a i j", a=8, i=4)[:, :, :, j],
                b.s2[:, row_off:row_off + 32].rearrange("p (a i) -> p a i", a=8),
                OP.subtract)
        _vf(b, v, i)

    def sub_bcast_i(v, dst_off, src_off, row_off):
        for i2 in range(4):
            i = v.tensor_tensor(
                b.s2[:, dst_off:dst_off + 128].rearrange("p (a i j) -> p a i j", a=8, i=4)[:, :, i2, :],
                b.s2[:, src_off:src_off + 128].rearrange("p (a i j) -> p a i j", a=8, i=4)[:, :, i2, :],
                b.s2[:, row_off:row_off + 32].rearrange("p (a j) -> p a j", a=8),
                OP.subtract)
        _vf(b, v, i)

    for it in range(50):
        @blk.vector
        def _(v, it=it):
            if it == 0:
                b.va.wait(v)
            else:
                b.ac.wait(v)
            sub_bcast_i(v, AR, ZR, VR)
            i = v.tensor_reduce(b.s2[:, MR:MR + 32],
                                b.s2[:, AR:AR + 128].rearrange("p (a i j) -> p (a i) j", a=8, i=4),
                                axis=AX, op=OP.max)
            _vf(b, v, i)
            sub_bcast_j(v, AR, AR, MR)
            b.va.inc(v.engine_nop()) if False else None

        @blk.scalar
        def _(s, it=it):
            b.va.wait(s)
            i = s.activation(b.s2[:, AR:AR + 128], b.s2[:, AR:AR + 128], AF.Exp)
            b.ac.inc(i)

        @blk.vector
        def _(v, it=it):
            b.ac.wait(v)
            i = v.tensor_reduce(b.s2[:, SR:SR + 32],
                                b.s2[:, AR:AR + 128].rearrange("p (a i j) -> p (a i) j", a=8, i=4),
                                axis=AX, op=OP.add)
            b.va.inc(i)

        @blk.scalar
        def _(s, it=it):
            b.ac.wait(s)
            b.va.wait(s)
            i = s.activation(b.s2[:, LR:LR + 32], b.s2[:, SR:SR + 32], AF.Ln)
            b.ac.inc(i)

        @blk.vector
        def _(v, it=it):
            b.ac.wait(v)
            i = v.tensor_tensor(b.s2[:, UR:UR + 32], b.s2[:, MR:MR + 32],
                                b.s2[:, LR:LR + 32], OP.add)
            _vf(b, v, i)
            sub_bcast_j(v, AR, ZR, UR)
            i = v.tensor_reduce(b.s2[:, MR:MR + 32].rearrange("p (a j) -> p a j", a=8),
                                b.s2[:, AR:AR + 128].rearrange("p (a i j) -> p a j i", a=8, i=4),
                                axis=AX, op=OP.max)
            _vf(b, v, i)
            sub_bcast_i(v, AR, AR, MR)
            b.va.inc(i) if False else None

        @blk.scalar
        def _(s, it=it):
            b.va.wait(s)
            i = s.activation(b.s2[:, AR:AR + 128], b.s2[:, AR:AR + 128], AF.Exp)
            b.ac.inc(i)

        @blk.vector
        def _(v, it=it):
            b.ac.wait(v)
            i = v.tensor_reduce(b.s2[:, SR:SR + 32].rearrange("p (a j) -> p a j", a=8),
                                b.s2[:, AR:AR + 128].rearrange("p (a i j) -> p a j i", a=8, i=4),
                                axis=AX, op=OP.add)
            b.va.inc(i)

        @blk.scalar
        def _(s, it=it):
            b.ac.wait(s)
            b.va.wait(s)
            i = s.activation(b.s2[:, LR:LR + 32], b.s2[:, SR:SR + 32], AF.Ln)
            b.ac.inc(i)

        @blk.vector
        def _(v, it=it):
            b.ac.wait(v)
            i = v.tensor_tensor(b.s2[:, VR:VR + 32], b.s2[:, MR:MR + 32],
                                b.s2[:, LR:LR + 32], OP.add)
            b.va.inc(i)

    @blk.vector
    def _(v):
        b.va.wait(v)
        sub_bcast_i(v, HR, ZR, VR)
        sub_bcast_j(v, HR, HR, UR)
        i = v.memset(b.s2[0:1, XR:XR + 1], 0.0)
        b.va.inc(i)

    @blk.scalar
    def _(s):
        b.va.wait(s)
        i = s.activation(b.s2[:, HR:HR + 128], b.s2[:, HR:HR + 128], AF.Exp)
        b.ac.inc(i)
        i = s.activation(b.s1[0:1, 0:1], b.btile[0:1, 33:34], AF.Sigmoid)
        b.ac.inc(i)

    @blk.vector
    def _(v):
        b.ac.wait(v)
        i = v.tensor_copy(b.sb16[0:1, 0:1], b.s1[0:1, 0:1])
        b.va.inc(i)

    @blk.tensor
    def _(t):
        b.va.wait(t)
        i = t.matmul(b.psM[:, 0:1], onesr(b), b.sb16[0:1, 0:1], start=True, stop=True,
                     skip_group_check=True)
        b.pe.inc(i)

    @blk.vector
    def _(v):
        b.pe.wait(v)
        i = v.tensor_copy(b.btile[:, 34:35], b.psM[:, 0:1])
        _vf(b, v, i)
        i = v.tensor_scalar(b.btile[:, 35:36], b.btile[:, 34:35], -1.0, 1.0, OP.mult, OP.add)
        _vf(b, v, i)
        i = v.tensor_scalar(b.s2[:, HR:HR + 128], b.s2[:, HR:HR + 128],
                            b.btile[:, 34:35], None, OP.mult)
        _vf(b, v, i)
        dg = b.s2[:, HR:HR + 128].rearrange("p (a i) -> p a i", a=8, i=16)[:, :, 0:16:5]
        i = v.tensor_scalar(dg, dg, b.btile[:, 35:36], None, OP.add)
        b.va.inc(i)

    for which, OFF in [(0, 0), (1, 32)]:
        @blk.vector
        def _(v, OFF=OFF):
            b.va.wait(v)
            i = v.tensor_reduce(b.s2[:, XR:XR + 8],
                                b.tm[:, OFF:OFF + 32].rearrange("p (a n) -> p a n", a=8),
                                axis=AX, op=OP.max)
            _vf(b, v, i)
            for n in range(4):
                i = v.tensor_tensor(
                    b.tm2[:, OFF:OFF + 32].rearrange("p (a n) -> p a n", a=8)[:, :, n],
                    b.tm[:, OFF:OFF + 32].rearrange("p (a n) -> p a n", a=8)[:, :, n],
                    b.s2[:, XR:XR + 8], OP.subtract)
            b.va.inc(i)

        @blk.scalar
        def _(s, OFF=OFF):
            b.va.wait(s)
            i = s.activation(b.tm2[:, OFF:OFF + 32], b.tm2[:, OFF:OFF + 32], AF.Exp)
            b.ac.inc(i)

        @blk.vector
        def _(v, OFF=OFF):
            b.ac.wait(v)
            i = v.tensor_reduce(b.s2[:, XR + 8:XR + 16],
                                b.tm2[:, OFF:OFF + 32].rearrange("p (a n) -> p a n", a=8),
                                axis=AX, op=OP.add)
            _vf(b, v, i)
            i = v.reciprocal(b.s2[:, XR + 8:XR + 16], b.s2[:, XR + 8:XR + 16])
            _vf(b, v, i)
            for n in range(4):
                i = v.tensor_tensor(
                    b.tm2[:, OFF:OFF + 32].rearrange("p (a n) -> p a n", a=8)[:, :, n],
                    b.tm2[:, OFF:OFF + 32].rearrange("p (a n) -> p a n", a=8)[:, :, n],
                    b.s2[:, XR + 8:XR + 16], OP.mult)
            b.va.inc(i)

    for src_ap, nrow, dst0 in [
            (lambda tt: b.s2[:, HR + tt * 16:HR + (tt + 1) * 16], 16, 0),
            (lambda tt: b.tm2[:, tt * 4:(tt + 1) * 4], 4, 16),
            (lambda tt: b.tm2[:, 32 + tt * 4:32 + (tt + 1) * 4], 4, 20)]:
        for tt in range(8):
            @blk.tensor
            def _(t, tt=tt, src_ap=src_ap, nrow=nrow):
                b.va.wait(t)
                i = t.transpose(b.psQ[0:nrow, 0:128], src_ap(tt), ident_f32(b))
                b.pe.inc(i)

            @blk.vector
            def _(v, tt=tt, nrow=nrow):
                b.pe.wait(v)
                if tt == 0:
                    b.st.wait(v)
                i = v.tensor_copy(b.s1[0:nrow, tt * 128:(tt + 1) * 128],
                                  b.psQ[0:nrow, 0:128])
                b.va.inc(i)

        @blk.sync
        def _(sy, nrow=nrow, dst0=dst0):
            b.va.wait(sy)
            i = sy.dma_start(out=dram["l2_hrows"][dst0:dst0 + nrow, :], in_=b.s1[0:nrow, :])
            b.st.inc(i, 16)

    def bcast_hrow(idx, slot):
        @blk.sync
        def _(sy, idx=idx):
            b.st.wait(sy)
            b.sgs.wait(sy)
            b.va.wait(sy)
            i = sy.dma_start(out=row(b, R_MEAN, 1024), in_=dram["l2_hrows"][idx:idx + 1, :])
            b.sgs.inc(i, 16)

        @blk.vector
        def _(v, idx=idx):
            b.sgs.wait(v)
            b.pe.wait(v)
            i = v.tensor_copy(b.sb16[0:1, 0:1024], row(b, R_MEAN, 1024))
            b.va.inc(i)

        @blk.tensor
        def _(t, idx=idx):
            b.va.wait(t)
            b.ac.wait(t)
            for half in range(2):
                last = t.matmul(b.psP1[:, half * 512:(half + 1) * 512], onesr(b),
                                b.sb16[0:1, half * 512:(half + 1) * 512], start=True,
                                stop=True, skip_group_check=True)
            b.pe.inc(last)

        @blk.vector
        def _(v, idx=idx, slot=slot):
            b.pe.wait(v)
            i = v.tensor_copy(b.kv_sb[:, slot * 1024:(slot + 1) * 1024], b.psP1[:, 0:1024])
            b.va.inc(i)

    for n in range(4):
        bcast_hrow(16 + n, 0)

        @blk.vector
        def _(v, n=n):
            b.va.wait(v)
            for c in range(8):
                i = v.tensor_tensor(b.pT[:], b.xn_sb[:, (n * 8 + c) * T:(n * 8 + c + 1) * T],
                                    b.kv_sb[:, 0:1024], OP.mult)
                if n == 0:
                    i = v.tensor_copy(b.qk_sb[:, c * T:(c + 1) * T], b.pT[:])
                else:
                    i = v.tensor_tensor(b.qk_sb[:, c * T:(c + 1) * T],
                                        b.qk_sb[:, c * T:(c + 1) * T], b.pT[:], OP.add)
            b.va.inc(i)

    w_t, pe_t = {}, {}
    for j in range(8):
        @blk.gpsimd
        def _(g, j=j):
            b.pe.wait(g)
            i = g.dma_start(out=b.wst[:, (j % 2) * D:(j % 2) * D + 1024], in_=dram["l2_wbr"][j])
            w_t[j] = b.sw_slot[j % 2].inc(i, 16)

        @blk.tensor
        def _(t, j=j):
            b.sw_slot[j % 2].wait(t, w_t[j])
            if j == 0:
                b.va.wait(t)
            b.ac.wait(t) if j < 2 else b.ac.wait(t)
            ps = [b.psP0, b.psP1][j % 2]
            for half in range(2):
                for dc in range(8):
                    last = t.matmul(ps[:, half * 512:(half + 1) * 512],
                                    b.wst[:, (j % 2) * D + dc * 128:(j % 2) * D + (dc + 1) * 128],
                                    b.qk_sb[:, dc * T + half * 512:dc * T + half * 512 + 512],
                                    start=(dc == 0), stop=(dc == 7), skip_group_check=True)
            pe_t[j] = b.pe.inc(last)

        @blk.scalar
        def _(s, j=j):
            b.pe.wait(s, pe_t[j])
            i = s.copy(b.qk_sb[:, (8 + j) * T:(9 + j) * T], [b.psP0, b.psP1][j % 2][:])
            b.ac.inc(i)

    for n in range(4):
        for k5, idx in enumerate([n * 4 + 0, n * 4 + 1, n * 4 + 2, n * 4 + 3, 20 + n]):
            bcast_hrow(idx, k5)
        for c in range(8):
            @blk.vector
            def _(v, n=n, c=c):
                b.va.wait(v)
                b.ac.wait(v)
                v.tensor_tensor(b.pT[:], b.xn_sb[:, c * T:(c + 1) * T],
                                b.kv_sb[:, 0:1024], OP.mult)
                for j2 in range(1, 4):
                    v.tensor_tensor(b.sb16[:], b.xn_sb[:, (j2 * 8 + c) * T:(j2 * 8 + c + 1) * T],
                                    b.kv_sb[:, j2 * 1024:(j2 + 1) * 1024], OP.mult)
                    v.tensor_tensor(b.pT[:], b.pT[:], b.sb16[:], OP.add)
                v.tensor_tensor(b.sb16[:], b.qk_sb[:, (8 + c) * T:(9 + c) * T],
                                b.kv_sb[:, 4 * 1024:5 * 1024], OP.mult)
                i = v.tensor_tensor(b.pT[:], b.pT[:], b.sb16[:], OP.add)
                b.va.inc(i)

            @blk.sync
            def _(sy, n=n, c=c):
                nc2 = n * 8 + c
                if True:
                    b.va.wait(sy)
                i = sy.dma_start(out=b.xf[:, (nc2 % 2) * T:(nc2 % 2 + 1) * T],
                                 in_=xsrc[nc2 * 128:(nc2 + 1) * 128, :])
                b.sx_slot[nc2 % 2].inc(i, 16)

            @blk.vector
            def _(v, n=n, c=c):
                nc2 = n * 8 + c
                b.sx_slot[nc2 % 2].wait(v)
                i = v.tensor_tensor(b.s1[:], b.pT[:], b.xf[:, (nc2 % 2) * T:(nc2 % 2 + 1) * T],
                                    OP.add)
                b.va.inc(i)

            @blk.sync
            def _(sy, n=n, c=c):
                nc2 = n * 8 + c
                b.va.wait(sy)
                i = sy.dma_start(out=dram["xmid"][nc2 * 128:(nc2 + 1) * 128, :], in_=b.s1[:])
                b.st.inc(i, 16)


def emit_q_l2(b, blk, dram):
    @blk.gpsimd
    def _(g):
        b.sb0.wait(g)
        i = g.dma_start(out=b.btile[:, 0:32], in_=dram["l2_bq"]); b.sb0.inc(i, 16)
        i = g.dma_start(out=b.btile[:, 64:96], in_=dram["l2_gq"]); b.sb0.inc(i, 16)

    def evac_q(idx, j, ps, pe_t):
        @blk.scalar
        def _(s, idx=idx, j=j, ps=ps):
            b.pe.wait(s, pe_t[idx])
            b.va.wait(s)
            b.sb0.wait(s)
            s.activation(b.s1[:], ps[:], AF.Identity, bias=b.btile[:, j:j + 1])
            i = s.activation(b.s2[:], b.s1[:], AF.Square)
            b.ac.inc(i)

        @blk.vector
        def _(v, idx=idx, j=j):
            b.ac.wait(v)
            b.pe.wait(v)
            v.tensor_copy(b.qk_sb[:, j * T:(j + 1) * T], b.s1[:])
            i = v.tensor_copy(b.pT[:], b.s2[:])
            b.va.inc(i)

        @blk.tensor
        def _(t, idx=idx, j=j):
            b.va.wait(t)
            for half in range(2):
                last = t.matmul(b.psQ[0:1, half * 512:(half + 1) * 512], onesc(b),
                                b.pT[:, half * 512:(half + 1) * 512],
                                start=(j == 0), stop=(j == NCH - 1), skip_group_check=True)
            b.pe.inc(last)
        return (b.va, b.va.n)

    emit_proj(b, blk, dram["l2_wq"], list(range(NCH)),
              lambda dc, half: xn_rhs(b, dc, half), evac_q)
    emit_rstd_bcast(b, blk, 1e-6)
    for j in range(NCH):
        @blk.vector
        def _(v, j=j):
            b.va.wait(v)
            v.tensor_tensor(b.s1[:], b.qk_sb[:, j * T:(j + 1) * T], b.bc2[:], OP.mult)
            i = v.tensor_scalar(b.qk_sb[:, j * T:(j + 1) * T], b.s1[:],
                                b.btile[:, 64 + j:65 + j], None, OP.mult)
            b.va.inc(i)


def emit_block2(b, blk, dram, xsrc, xdst, dbg):
    ph = 99
    if dbg:
        m = _re.search(r"ph(\d+)", dbg)
        if m:
            ph = int(m.group(1))

    @blk.gpsimd
    def _(g):
        b.sb0.wait(g)
        i = g.dma_start(out=b.btile[:, 0:32], in_=dram["l2_bk"]); b.sb0.inc(i, 16)
        i = g.dma_start(out=b.btile[:, 64:96], in_=dram["l2_gk"]); b.sb0.inc(i, 16)
        i = g.dma_start(out=b.btile[:, 32:64], in_=dram["l2_bv"]); b.sb0.inc(i, 16)

    if ph < 1:
        return
    cc_end = emit_kv_and_gather(
        b, blk, dram, "g2", 32, dram["l2_wk"], dram["l2_wv"],
        lambda j: b.btile[:, j:j + 1], lambda j: b.btile[:, 32 + j:33 + j],
        l2_gain=lambda j: b.btile[:, 64 + j:65 + j])
    if ph < 2:
        return
    emit_ffn_l2(b, blk, dram, xsrc)
    if ph < 3:
        if dbg and "mid" in dbg:
            dbg_dump_dram(b, blk, dram, dram["xmid"])
        return
    emit_q_l2(b, blk, dram)
    if ph < 4:
        return
    emit_gs_readback(b, blk, dram, cc_end - 2)
    cfg = dict(pfx="g2", nkv=32, group=1, q_mode="buf", out="l2_wo", bo_name="l2_bo",
               cc_k=cc_end - 1, cc_v=cc_end)
    emit_attention(b, blk, dram, cfg, xdst)




def emit_cbound(b, blk, qsrc, kv, h):
    @blk.scalar
    def _(s, h=h):
        b.va.wait(s)
        b.pl.wait(s)
        i = s.activation(b.s2[:], qsrc, AF.Square)
        b.ac.inc(i)

    @blk.vector
    def _(v, h=h):
        b.ac.wait(v)
        b.pe.wait(v)
        i = v.tensor_copy(b.sb16[:], b.s2[:])
        b.va.inc(i)

    @blk.tensor
    def _(t, h=h):
        b.va.wait(t)
        t.matmul(b.psM[0:1, 0:512], onesc(b), b.sb16[:, 0:512], start=True, stop=True,
                 skip_group_check=True)
        i = t.matmul(b.psV[0:1, 0:512], onesc(b), b.sb16[:, 512:1024], start=True,
                     stop=True, skip_group_check=True)
        b.pe.inc(i)

    @blk.vector
    def _(v, h=h, kv=kv):
        b.pe.wait(v)
        v.reduce_max(row(b, R_C, 1), b.psM[0:1, 0:512], axis=AX)
        i = v.reduce_max(row(b, R_C + 1, 1), b.psV[0:1, 0:512], axis=AX)
        b.va.inc(i); b.va.wait(v)
        i = v.tensor_tensor(row(b, R_C + 2, 1), row(b, R_C, 1), row(b, R_C + 1, 1), OP.max)
        b.va.inc(i); b.va.wait(v)
        i = v.tensor_tensor(row(b, R_C + 3, 1), row(b, R_C + 2, 1),
                            row(b, R_GK2 + kv, 1), OP.mult)
        b.va.inc(i)

    @blk.scalar
    def _(s, h=h):
        b.va.wait(s)
        i = s.activation(row(b, R_C + 4, 1), row(b, R_C + 3, 1), AF.Sqrt)
        b.ac.inc(i)

    @blk.vector
    def _(v, h=h):
        b.ac.wait(v)
        i = v.tensor_scalar(row(b, R_C + 5, 1), row(b, R_C + 4, 1), -ISQ, None, OP.mult)
        b.va.inc(i); b.va.wait(v)
        i = v.tensor_copy(b.sb16[0:1, 0:1], row(b, R_C + 5, 1))
        b.va.inc(i)

    @blk.tensor
    def _(t, h=h):
        b.va.wait(t)
        i = t.matmul(b.psM[:, 0:1], onesr(b), b.sb16[0:1, 0:1], start=True, stop=True,
                     skip_group_check=True)
        b.pe.inc(i)

    @blk.vector
    def _(v, h=h):
        b.pe.wait(v)
        i = v.tensor_copy(b.biasC[:, h % 16:h % 16 + 1], b.psM[:, 0:1])
        b.va.inc(i)
    return b.va.n


def emit_router_tail(b, blk, topk, scale_out):
    for tt in range(8):
        @blk.vector
        def _(v, tt=tt):
            b.va.wait(v)
            i = v.reduce_max(b.tm2[:, tt:tt + 1], b.tm[:, tt * 8:(tt + 1) * 8], axis=AX)
            b.va.inc(i); b.va.wait(v)
            i = v.tensor_scalar(b.tm2[:, 8 + tt:9 + tt], b.tm2[:, tt:tt + 1], -1.0, None, OP.mult)
            b.va.inc(i)

        @blk.scalar
        def _(s, tt=tt):
            b.va.wait(s)
            i = s.activation(b.tm3[:, tt * 8:(tt + 1) * 8], b.tm[:, tt * 8:(tt + 1) * 8],
                             AF.Identity, bias=b.tm2[:, 8 + tt:9 + tt])
            b.ac.inc(i)

        @blk.scalar
        def _(s, tt=tt):
            i = s.activation(b.tm3[:, 64 + tt * 8:72 + tt * 8],
                             b.tm3[:, tt * 8:(tt + 1) * 8], AF.Exp)
            b.ac.inc(i)

        @blk.vector
        def _(v, tt=tt):
            b.ac.wait(v)
            i = v.reduce_sum(b.tm2[:, 40 + tt:41 + tt], b.tm3[:, 64 + tt * 8:72 + tt * 8], axis=AX)
            b.va.inc(i); b.va.wait(v)
            i = v.reciprocal(b.tm2[:, 48 + tt:49 + tt], b.tm2[:, 40 + tt:41 + tt])
            b.va.inc(i)

        @blk.gpsimd
        def _(g, tt=tt):
            b.va.wait(g)
            b.ac.wait(g)
            i = g.tensor_scalar(b.tm3[:, 128 + tt * 8:136 + tt * 8],
                                b.tm3[:, 64 + tt * 8:72 + tt * 8],
                                b.tm2[:, 48 + tt:49 + tt], scale_out, OP.mult, OP.mult)
            b.pl.inc(i)

    @blk.tensor
    def _(t):
        b.pl.wait(t)
        for tt in range(8):
            last = t.transpose(b.psP0[0:8, tt * 128:(tt + 1) * 128],
                               b.tm3[:, 128 + tt * 8:136 + tt * 8], ident_f32(b))
        b.pe.inc(last)

    @blk.vector
    def _(v):
        b.pe.wait(v)
        i = v.tensor_copy(b.pT[0:8, 0:1024], b.psP0[0:8, 0:1024])
        b.va.inc(i)


def emit_l3_router(b, blk, dram):
    w_t = {}
    for c in range(2):
        @blk.gpsimd
        def _(g, c=c):
            b.pe.wait(g)
            i = g.dma_start(out=b.wst[:, 0:D], in_=dram["l3_wqk"][c])
            b.sw_slot[0].inc(i, 16)
            i = g.dma_start(out=b.wst[:, D:2 * D], in_=dram["l3_wqk"][2 + c])
            w_t[c] = b.sw_slot[0].inc(i, 16)

        @blk.tensor
        def _(t, c=c):
            b.sw_slot[0].wait(t, w_t[c])
            b.va.wait(t)
            b.ac.wait(t)
            for sl in range(2):
                ps = [b.psP0, b.psP1][sl]
                for half in range(2):
                    for dc in range(NCH):
                        last = t.matmul(ps[:, half * 512:(half + 1) * 512],
                                        b.wst[:, sl * D + dc * 128:sl * D + (dc + 1) * 128],
                                        xn_rhs(b, dc, half),
                                        start=(dc == 0), stop=(dc == NCH - 1),
                                        skip_group_check=True)
            b.pe.inc(last)

        @blk.scalar
        def _(s, c=c):
            b.pe.wait(s)
            b.va.wait(s)
            s.copy(b.s1[:], b.psP0[:])
            i = s.copy(b.s2[:], b.psP1[:])
            b.ac.inc(i)

        @blk.vector
        def _(v, c=c):
            b.ac.wait(v)
            i = v.tensor_tensor(b.qk_sb[:, (18 + c) * T:(19 + c) * T], b.s1[:], b.s2[:], OP.mult)
            b.va.inc(i)

    @blk.tensor
    def _(t):
        b.va.wait(t)
        for c in range(2):
            for tt in range(8):
                last = t.matmul(b.psQ[:, tt * 8 + 4 * c:tt * 8 + 4 * c + 4],
                                b.qk_sb[:, (18 + c) * T + tt * 128:(18 + c) * T + (tt + 1) * 128],
                                b.cstb[:, CB_G32:CB_G32 + 4],
                                start=True, stop=True, skip_group_check=True)
        b.pe.inc(last)

    @blk.vector
    def _(v):
        b.pe.wait(v)
        i = v.tensor_scalar(b.tm[:, 0:64], b.psQ[:, 0:64], 1.0 / math.sqrt(32.0), None, OP.mult)
        b.va.inc(i)

    emit_router_tail(b, blk, None, 1.0)


def emit_attention_l3(b, blk, dram, cfg, xdst):
    nkv, grp = cfg["nkv"], cfg["group"]
    kout, vout = dram[cfg["pfx"] + "_kout"], dram[cfg["pfx"] + "_vout"]
    FAR = b.cbig[:, 2048:2176]
    DIAG = b.cbig[:, 2176:2304]

    cidreg = [None]

    @blk.sync
    def _(sy):
        cidreg[0] = sy.partition_id() % 4

    for kv in range(nkv):
        @blk.sync
        def _(sy, kv=kv):
            b.cc.wait(sy, cfg["cc_v"])
            b.sb0.wait(sy)
            if kv >= 1:
                b.pe.wait(sy)
            kr = kout.rearrange("(r d) t -> d r t", r=4)[kv * 128:(kv + 1) * 128]
            vr = vout.rearrange("(r h c p) d -> r h p c d", r=4, h=nkv, c=8)
            for body in sy.Switch(cidreg[0], 4):
                ra = max(0, body - 1)
                sa = slice(0, 512) if body == 0 else slice(512, 1024)
                i = sy.dma_start(out=b.kv_sb[:, 0:512], in_=kr[:, ra, sa])
                i.then_inc(b.skr.h, 16)
                i = sy.dma_start(out=b.kv_sb[:, 512:1536], in_=kr[:, body, :])
                i.then_inc(b.skr.h, 16)
                ca = slice(0, 4) if body == 0 else slice(4, 8)
                i = sy.dma_start(
                    out=b.kv_sb[:, 1536:2048].rearrange("p (c d) -> p c d", c=4),
                    in_=vr[ra, kv][:, ca, :])
                i.then_inc(b.skr.h, 16)
                i = sy.dma_start(
                    out=b.kv_sb[:, 2048:3072].rearrange("p (c d) -> p c d", c=8),
                    in_=vr[body, kv])
                i.then_inc(b.skr.h, 16)
            b.skr.n += 64
        kv_tok = b.skr.n

        for hh in range(grp):
            h = kv * grp + hh
            emit_attn_head_l3(b, blk, dram, cfg, h, kv, kv_tok, FAR, DIAG)

    emit_o_proj(b, blk, dram, cfg, xdst)


def emit_attn_head_l3(b, blk, dram, cfg, h, kv, kv_tok, FAR, DIAG):
    wq = dram[cfg["wq_name"]]

    @blk.gpsimd
    def _(g, h=h):
        b.pe.wait(g)
        i = g.dma_start(out=b.wst[:, (h % 2) * D:(h % 2 + 1) * D], in_=wq[h])
        b.sw_slot[h % 2].inc(i, 16)
    sw_tok = b.sw_slot[h % 2].n

    @blk.tensor
    def _(t, h=h, sw_tok=sw_tok):
        b.sw_slot[h % 2].wait(t, sw_tok)
        b.ac.wait(t)
        b.va.wait(t)
        for half in range(2):
            for dc in range(NCH):
                last = t.matmul(b.psP0[:, half * 512:(half + 1) * 512],
                                b.wst[:, (h % 2) * D + dc * 128:(h % 2) * D + (dc + 1) * 128],
                                xn_rhs(b, dc, half),
                                start=(dc == 0), stop=(dc == NCH - 1), skip_group_check=True)
        b.pe.inc(last)

    @blk.scalar
    def _(s, h=h):
        b.pe.wait(s)
        b.va.wait(s)
        i = s.copy(b.qh_sb[:], b.psP0[:])
        b.ac.inc(i)

    biasC_tok = emit_cbound(b, blk, b.qh_sb[:], kv, h)

    for qt in range(8):
        ac_e, pe_q, va_m = {}, {}, {}
        for s5 in range(5):
            ws = qt + s5
            if s5 == 4:
                madd = DIAG
            elif qt < 4:
                madd = b.cbig[:, (qt * 4 + s5) * 128:(qt * 4 + s5 + 1) * 128]
            elif s5 == 0:
                madd = FAR
            else:
                madd = None
            po = (s5 % 2) * 512

            @blk.tensor
            def _(t, s5=s5, ws=ws, qt=qt, h=h, po=po):
                if s5 == 0 and qt == 0:
                    b.skr.wait(t, kv_tok)
                if s5 >= 2:
                    b.ac.wait(t, ac_e[s5 - 2])
                i = t.matmul(b.psQ[:, po:po + 128],
                             b.kv_sb[:, ws * 128:(ws + 1) * 128],
                             b.qh_sb[:, qt * 128:(qt + 1) * 128],
                             start=True, stop=True, skip_group_check=True)
                pe_q[s5] = b.pe.inc(i)

            if madd is not None:
                @blk.vector
                def _(v, s5=s5, madd=madd, po=po):
                    b.pe.wait(v, pe_q[s5])
                    i = v.tensor_tensor(b.psQ[:, po:po + 128], b.psQ[:, po:po + 128],
                                        madd, OP.add)
                    va_m[s5] = b.va.inc(i)

            @blk.scalar
            def _(s, s5=s5, qt=qt, h=h, po=po, masked=(madd is not None)):
                if masked:
                    b.va.wait(s, va_m[s5])
                else:
                    b.pe.wait(s, pe_q[s5])
                if s5 == 0 and qt == 0:
                    b.va.wait(s, biasC_tok)
                i = s.activation(b.pT[:, po:po + 128], b.psQ[:, po:po + 128], AF.Exp,
                                 bias=b.biasC[:, h % 16:h % 16 + 1], scale=ISQ)
                ac_e[s5] = b.ac.inc(i)

            @blk.tensor
            def _(t, s5=s5, ws=ws, po=po, qt=qt):
                b.ac.wait(t, ac_e[s5])
                if s5 == 0:
                    b.va.wait(t)
                t.matmul(b.psM[0:1, 0:128], onesc(b), b.pT[:, po:po + 128],
                         start=(s5 == 0), stop=(s5 == 4), skip_group_check=True)
                i = t.matmul(b.psV[:, 0:128], b.kv_sb[:, 1536 + ws * 128:1536 + (ws + 1) * 128],
                             b.pT[:, po:po + 128],
                             start=(s5 == 0), stop=(s5 == 4), skip_group_check=True)
                b.pe.inc(i)

        @blk.vector
        def _(v, qt=qt, h=h):
            b.pe.wait(v)
            i = v.reciprocal(row(b, R_RECIP, 128), b.psM[0:1, 0:128])
            b.va.inc(i); b.va.wait(v)
            i = v.tensor_copy(b.sb16[0:1, 0:128], row(b, R_RECIP, 128))
            b.va.inc(i)

        @blk.tensor
        def _(t, qt=qt):
            b.va.wait(t)
            i = t.matmul(b.psM[:, 0:128], onesr(b), b.sb16[0:1, 0:128], start=True,
                         stop=True, skip_group_check=True)
            b.pe.inc(i)

        @blk.vector
        def _(v, qt=qt, h=h):
            b.pe.wait(v)
            v.tensor_copy(b.bc1[:, 0:128], b.psM[:, 0:128])
            i = v.tensor_tensor(b.qk_sb[:, h * T + qt * 128:h * T + (qt + 1) * 128],
                                b.psV[:, 0:128], b.bc1[:, 0:128], OP.mult)
            b.va.inc(i)


def emit_block3(b, blk, dram, xsrc, xdst, dbg):
    ph = 99
    if dbg:
        m = _re.search(r"ph(\d+)", dbg)
        if m:
            ph = int(m.group(1))

    @blk.gpsimd
    def _(g):
        b.sb0.wait(g)
        i = g.dma_start(out=b.cbig[:, 0:2048], in_=dram["l3_me"]); b.sb0.inc(i, 16)
        i = g.dma_start(out=b.cbig[:, 2048:2304], in_=dram["l3_mfd"]); b.sb0.inc(i, 16)

    if ph < 1:
        return
    cc_end = emit_kv_and_gather(b, blk, dram, "g3", 8, dram["l3_wk"], dram["l3_wv"],
                                None, None, rope=False)
    if ph < 2:
        return
    emit_l3_router(b, blk, dram)
    emit_lora_z(b, blk, dram, "l3_A", 1.0)
    if ph < 3:
        return
    emit_ffn_base(b, blk, dram, "l3_w", "l3_b", "l3_B", xsrc)
    if ph < 4:
        if dbg and "mid" in dbg:
            dbg_dump_dram(b, blk, dram, dram["xmid"])
        return
    emit_gs_readback(b, blk, dram, cc_end - 2)
    cfg = dict(pfx="g3", nkv=8, group=4, wq_name="l3_wq", out="l3_wo", bo_name=None,
               cc_k=cc_end - 1, cc_v=cc_end)
    emit_attention_l3(b, blk, dram, cfg, xdst)



def _colchunk(w):
    din, dout = w.shape
    return np.ascontiguousarray(
        w.reshape(din // 128, 128, dout // 128, 128).transpose(2, 1, 0, 3)
    ).reshape(dout // 128, 128, din)


def _thin(w):
    din, m = w.shape
    return np.ascontiguousarray(
        w.reshape(din // 128, 128, m).transpose(1, 0, 2)).reshape(128, din // 128 * m)


def _bias_cols(v):
    return np.ascontiguousarray(v.reshape(-1, 128).T)


def _make_consts():
    c = np.zeros((128, 257), np.float32)
    c[:, 0] = 1.0
    c[0, 1:129] = 1.0
    c[:, 129:257] = np.eye(128, dtype=np.float32)
    cbv = np.zeros((128, CB_W), np.float32)
    cbv[:, CB_ID:CB_ID + 128] = np.eye(128)
    cbv[:, CB_ONEC] = 1.0
    cbv[0, CB_ONER:CB_ONER + 128] = 1.0
    for e in range(8):
        cbv[e, CB_E8 + e * 16:CB_E8 + (e + 1) * 16] = 1.0
    for g in range(4):
        cbv[g, CB_P32 + g * 32:CB_P32 + (g + 1) * 32] = 1.0
    for p in range(128):
        cbv[p, CB_G32 + p // 32] = 1.0
    for r in range(4):
        cbv[r, CB_Q4 + r * 32:CB_Q4 + (r + 1) * 32] = 1.0
    return c, cbv


def _rope_tables(core):
    pos0 = (core % 4) * 1024
    inv = 1.0 / (10000.0 ** (np.arange(0, 128, 2, dtype=np.float64) / 128.0))
    ang = (pos0 + np.arange(T, dtype=np.float64))[None, :] * inv[:, None]
    cc = np.empty((128, T), np.float32)
    ss = np.empty((128, T), np.float32)
    cc[0:64] = np.cos(ang); cc[64:128] = np.cos(ang)
    ss[0:64] = -np.sin(ang); ss[64:128] = np.sin(ang)
    return cc, ss


def _expert_B(lb):
    ball = lb.transpose(0, 2, 1).reshape(128, D)
    return np.ascontiguousarray(ball.reshape(128, 32, 128).transpose(1, 0, 2))


def prep_inputs(inputs, blocks):
    need = set(blocks)
    shared = {}
    c, cbv = _make_consts()
    shared["consts"] = c
    shared["constsb"] = cbv
    f32 = lambda k: np.asarray(inputs[k], dtype=np.float32)
    if 0 in need:
        shared["l0_wq"] = _colchunk(f32("L0_attn_wq"))
        shared["l0_wk"] = _colchunk(f32("L0_attn_wk"))
        shared["l0_wv"] = _colchunk(f32("L0_attn_wv"))
        shared["l0_bq"] = _bias_cols(f32("L0_attn_bq"))
        shared["l0_bk"] = _bias_cols(f32("L0_attn_bk"))
        shared["l0_bv"] = _bias_cols(f32("L0_attn_bv"))
        shared["l0_gw"] = _thin(np.ascontiguousarray(f32("L0_ffn_gate_w").T))
        shared["l0_bw"] = _colchunk(np.ascontiguousarray(f32("L0_ffn_base_w").T))
        shared["l0_bb"] = _bias_cols(f32("L0_ffn_base_b"))
        shared["l0_A"] = _colchunk(np.ascontiguousarray(f32("L0_ffn_lora_A").reshape(128, D).T))
        shared["l0_B"] = _expert_B(f32("L0_ffn_lora_B"))
    if 1 in need:
        shared["l1_w"] = _colchunk(np.ascontiguousarray(f32("L1_attn_weight").T))
        shared["l1_mask"] = np.ascontiguousarray(np.asarray(inputs["L1_attn_mask"], np.int32))
        shared["l1_A"] = _colchunk(np.ascontiguousarray(f32("L1_ffn_A").reshape(128, D).T))
        shared["l1_B"] = _expert_B(f32("L1_ffn_B"))
        shared["l1_wr"] = _thin(np.ascontiguousarray(f32("L1_ffn_Wr").T))
        shared["l1_br"] = f32("L1_ffn_br").reshape(8, 1)
        qx = np.zeros((128, D), np.float32)
        for nb in range(128):
            qx[nb, nb * 32:(nb + 1) * 32] = 1.0
        shared["qexp"] = qx
    if 2 in need:
        shared["l2_wq"] = _colchunk(f32("L2_attn_Wq"))
        shared["l2_wk"] = _colchunk(f32("L2_attn_Wk"))
        shared["l2_wv"] = _colchunk(f32("L2_attn_Wv"))
        shared["l2_wo"] = _colchunk(f32("L2_attn_Wo"))
        shared["l2_bq"] = _bias_cols(f32("L2_attn_bq"))
        shared["l2_bk"] = _bias_cols(f32("L2_attn_bk"))
        shared["l2_bv"] = _bias_cols(f32("L2_attn_bv"))
        shared["l2_bo"] = _bias_cols(f32("L2_attn_bo"))
        shared["l2_gq"] = _bias_cols(f32("L2_attn_gq"))
        shared["l2_gk"] = _bias_cols(f32("L2_attn_gk"))
        wrpp = np.concatenate([f32("L2_ffn_W_res").T, f32("L2_ffn_W_pre").T,
                               f32("L2_ffn_W_post").T], axis=1)
        shared["l2_wrpp"] = _thin(wrpp)
        shared["l2_b24"] = np.concatenate(
            [f32("L2_ffn_H_res_base").reshape(16), f32("L2_ffn_H_pre_base"),
             f32("L2_ffn_H_post_base")]).reshape(24, 1)
        shared["l2_gate"] = f32("L2_ffn_gate").reshape(1, 1)
        shared["l2_wbr"] = _colchunk(f32("L2_ffn_W_branch"))
    if 3 in need:
        shared["l3_wq"] = _colchunk(np.ascontiguousarray(f32("L3_attn_wq").T))
        shared["l3_wk"] = _colchunk(np.ascontiguousarray(f32("L3_attn_wk").T))
        shared["l3_wv"] = _colchunk(np.ascontiguousarray(f32("L3_attn_wv").T))
        shared["l3_wo"] = _colchunk(np.ascontiguousarray(f32("L3_attn_wo").T))
        shared["l3_w"] = _colchunk(np.ascontiguousarray(f32("L3_ffn_W").T))
        shared["l3_b"] = _bias_cols(f32("L3_ffn_b"))
        shared["l3_wqk"] = _colchunk(np.ascontiguousarray(
            np.concatenate([f32("L3_ffn_Wq"), f32("L3_ffn_Wk")], axis=0).T))
        shared["l3_A"] = _colchunk(np.ascontiguousarray(f32("L3_ffn_A").reshape(128, D).T))
        shared["l3_B"] = _expert_B(f32("L3_ffn_Bm"))
        NEGM = -30000.0
        far = np.where(np.arange(128)[None, :] < np.arange(128)[:, None], 0.0, NEGM
                       ).astype(np.float32)
        diag = np.where(np.arange(128)[:, None] <= np.arange(128)[None, :], 0.0, NEGM
                        ).astype(np.float32)
        shared["l3_mfd"] = np.concatenate([far, diag], axis=1)
        shared["_l3far"] = far
    percore = []
    for core in range(8):
        d = {}
        if 0 in need:
            ccc, sss = _rope_tables(core)
            d["l0_cc"] = ccc
            d["l0_ss"] = sss
        if 3 in need:
            r0 = core % 4
            me = np.zeros((128, 2048), np.float32)
            for qt in range(4):
                g = r0 * 8 + qt
                for s5 in range(4):
                    col = (qt * 4 + s5) * 128
                    if g >= 4:
                        if s5 == 0:
                            me[:, col:col + 128] = shared["_l3far"]
                    else:
                        if s5 < 4 - g:
                            me[:, col:col + 128] = -30000.0
            d["l3_me"] = me
        percore.append(d)
    return shared, percore


def run_blocks(inputs, blocks, x_in=None, dbg=None, trace=False):
    if x_in is None:
        x_in = np.asarray(inputs["L0_attn_x"], np.float32).reshape(8192, D)
    xT = np.ascontiguousarray(x_in.T)
    shared, percore = prep_inputs(inputs, blocks)
    nc = build_program(blocks, dbg=dbg)
    in_maps = []
    for core in range(8):
        m = dict(shared)
        m.update(percore[core])
        m["xin"] = np.ascontiguousarray(xT[:, core * T:(core + 1) * T])
        in_maps.append(m)
    for m in in_maps:
        m.pop("_l3far", None)
    res = run_bass_kernel_spmd(nc, in_maps, core_ids=list(range(8)), trace=trace)
    out = np.concatenate([r["xout"].T for r in res.results], axis=0)
    extra = None
    if dbg is not None:
        extra = np.concatenate([r["dbg"].T for r in res.results], axis=0)
    return out, extra, res


def kernel(**inputs):
    out, _, _ = run_blocks(inputs, [0, 1, 2, 3])
    return out.reshape(2, 4096, D)


# version 03
# speedup vs baseline: 1.5703x; 1.5703x over previous
import contextlib
import math
import re as _re
import numpy as np
import ml_dtypes
import concourse.bass as bass
import concourse.mybir as mybir
from concourse.bass_utils import run_bass_kernel_spmd
from concourse.alu_op_type import AluOpType

F32 = mybir.dt.float32
BF16 = mybir.dt.bfloat16
I32 = mybir.dt.int32
AF = mybir.ActivationFunctionType
OP = AluOpType
AX = mybir.AxisListType.X

D = 4096
T = 1024
NCH = 32
ISQ = 1.0 / math.sqrt(128.0)

CB_ID = 0
CB_ONEC = 128
CB_ONER = 129
CB_E8 = 257
CB_P32 = 385
CB_G32 = 513
CB_Q4 = 517
CB_W = 645

R_MEAN = 0
R_RSTD = 1024
R_KN = 0
R_SC = 100
R_GS = 256
R_RECIP = 512
R_GK2 = 1600
R_C = 1700


class Sem:
    def __init__(self, es, nc, name):
        self.h = es.enter_context(nc.semaphore(name))
        self.n = 0

    def inc(self, instr, k=1):
        instr.then_inc(self.h, k)
        self.n += k
        return self.n

    def wait(self, eng, n=None):
        eng.wait_ge(self.h, self.n if n is None else n)


class Bld:
    def __init__(self):
        self.nc = bass.Bass("TRN2", target_bir_lowering=False)
        self.es = contextlib.ExitStack()

    def sb(self, name, shape, dtype):
        setattr(self, name, self.es.enter_context(self.nc.sbuf_tensor(name, shape, dtype)))

    def ps(self, name, shape, dtype=F32):
        setattr(self, name, self.es.enter_context(self.nc.psum_tensor(name, shape, dtype)))

    def sem(self, name):
        setattr(self, name, Sem(self.es, self.nc, name))


def onesr(b):
    return b.cstb[0:1, CB_ONER:CB_ONER + 128]


def onesc(b):
    return b.cstb[:, CB_ONEC:CB_ONEC + 1]


def ident_f32(b):
    return b.cst[:, 129:257]


def xn_rhs(b, dc, half):
    return b.xn_sb[:, dc * T + half * 512:dc * T + half * 512 + 512]


def row(b, off, w):
    return b.rows[0:1, off:off + w]


def build_program(blocks, dbg=None):
    b = Bld()
    nc = b.nc
    dram = {}

    def din(name, shape, dtype=F32):
        dram[name] = nc.dram_tensor(name, shape, dtype, kind="ExternalInput").ap()

    def dint(name, shape, dtype=F32):
        dram[name] = nc.dram_tensor(name, shape, dtype).ap()

    din("xin", [D, T])
    dram["xout"] = nc.dram_tensor("xout", [D, T], F32, kind="ExternalOutput").ap()
    if dbg is not None:
        dram["dbg"] = nc.dram_tensor("dbg", [D, T], F32, kind="ExternalOutput").ap()

    need = set(blocks)
    din("consts", [128, 257])
    din("constsb", [128, CB_W])
    if 0 in need:
        din("l0_wq", [32, 128, D]); din("l0_wk", [8, 128, D]); din("l0_wv", [8, 128, D])
        din("l0_bq", [128, 32]); din("l0_bk", [128, 8]); din("l0_bv", [128, 8])
        din("l0_gw", [128, NCH * 8])
        din("l0_bw", [32, 128, D]); din("l0_bb", [128, 32])
        din("l0_A", [1, 128, D]); din("l0_B", [32, 128, 128])
        din("l0_cc", [128, T]); din("l0_ss", [128, T])
        dint("g0_kin", [1024, T], BF16); dint("g0_kout", [4096, T], BF16)
        dint("g0_vin", [8 * T, 128], BF16); dint("g0_vout", [4 * 8 * T, 128], BF16)
    if 1 in need:
        din("l1_w", [32, 128, D]); din("l1_mask", [D, D], I32)
        din("l1_A", [1, 128, D]); din("l1_B", [32, 128, 128])
        din("l1_wr", [128, NCH * 8]); din("l1_br", [8, 1]); din("qexp", [128, D])
        dint("t1scr", [128, D]); dint("bm01d", [128, 128])
        dint("em_scr", [NCH, 128, D], BF16)
    if 2 in need:
        din("l2_wq", [32, 128, D]); din("l2_wk", [32, 128, D])
        din("l2_wv", [32, 128, D]); din("l2_wo", [32, 128, D])
        din("l2_bq", [128, 32]); din("l2_bk", [128, 32]); din("l2_bv", [128, 32])
        din("l2_bo", [128, 32]); din("l2_gq", [128, 32]); din("l2_gk", [128, 32])
        din("l2_wrpp", [128, NCH * 24]); din("l2_b24", [24, 1]); din("l2_gate", [1, 1])
        din("l2_wbr", [8, 128, 1024])
        dint("g2_kin", [D, T], BF16); dint("g2_kout", [4 * D, T], BF16)
        dint("l2_hrows", [24, T])
        dint("g2_vin", [32 * T, 128], BF16); dint("g2_vout", [4 * 32 * T, 128], BF16)
    if 3 in need:
        din("l3_wq", [32, 128, D]); din("l3_wk", [8, 128, D]); din("l3_wv", [8, 128, D])
        din("l3_wo", [32, 128, D]); din("l3_w", [32, 128, D]); din("l3_b", [128, 32])
        din("l3_wqk", [4, 128, D])
        din("l3_A", [1, 128, D]); din("l3_B", [32, 128, 128])
        din("l3_me", [128, 16 * 128]); din("l3_mfd", [128, 256])
        dint("g3_kin", [1024, T], BF16); dint("g3_kout", [4096, T], BF16)
        dint("g3_vin", [8 * T, 128], BF16); dint("g3_vout", [4 * 8 * T, 128], BF16)
    if need & {0, 2, 3}:
        dint("gs_in", [1, 64]); dint("gs_out", [4, 64])
    dint("xbuf", [D, T]); dint("xmid", [D, T])

    b.sb("xn_sb", [128, NCH * T], BF16)
    b.sb("qk_sb", [128, NCH * T], BF16)
    b.sb("wst", [128, 2 * D], BF16)
    b.sb("wsm", [128, 2 * 128], BF16)
    b.sb("xf", [128, 2 * T], F32)
    b.sb("kv_sb", [128, 8192], BF16)
    b.sb("s1", [128, T], F32)
    b.sb("s2", [128, T], F32)
    b.sb("bc1", [128, T], F32)
    b.sb("bc2", [128, T], F32)
    b.sb("cbig", [128, 2560], BF16)
    b.sb("pT", [128, 1024], BF16)
    b.sb("sb16", [128, 1024], BF16)
    b.sb("rows", [128, 2048], F32)
    b.sb("cst", [128, 257], F32)
    b.sb("cstb", [128, CB_W], BF16)
    b.sb("btile", [128, 96], F32)
    b.sb("tm", [128, 64], F32)
    b.sb("tm2", [128, 64], F32)
    b.sb("tm3", [128, 192], F32)
    b.sb("qh_sb", [128, T], BF16)
    b.sb("biasC", [128, 16], F32)

    b.ps("psW", [128, 3072])
    b.psP0 = b.psW[:, 0:1024]
    b.psP1 = b.psW[:, 1024:2048]
    b.psQ = b.psW[:, 2048:3072]
    b.ps("psV", [128, 512])
    b.ps("psM", [128, 512])

    for s in ["sxa", "sxb", "swa", "swb", "sws", "skk", "skr", "skga", "skgb",
              "sgs", "sb0", "pe", "va", "ac", "pl", "cc", "st"]:
        b.sem(s)
    b.sx_slot = [b.sxa, b.sxb]
    b.sw_slot = [b.swa, b.swb]
    b.skg_slot = [b.skga, b.skgb]

    with b.es, nc.Block() as blk:
        @blk.gpsimd
        def _(g):
            i = g.dma_start(out=b.cst[:], in_=dram["consts"])
            b.sb0.inc(i, 16)
            i = g.dma_start(out=b.cstb[:], in_=dram["constsb"])
            b.sb0.inc(i, 16)

        first = True
        for bi in blocks:
            xsrc = dram["xin"] if first else dram["xbuf"]
            xdst = dram["xout"] if bi == blocks[-1] else dram["xbuf"]
            emit_block(b, blk, dram, bi, xsrc, xdst, dbg)
            first = False

        @blk.sync
        def _(sy):
            b.st.wait(sy)
    return nc


def emit_block(b, blk, dram, bi, xsrc, xdst, dbg):
    emit_stats_xn(b, blk, xsrc)
    if dbg is not None and f"xn{bi}" in dbg:
        dbg_dump(b, blk, dram, lambda ch: b.xn_sb[:, ch * T:(ch + 1) * T])
    [emit_block0, emit_block1, emit_block2, emit_block3][bi](b, blk, dram, xsrc, xdst, dbg)


def dbg_dump_dram(b, blk, dram, src_ap):
    for ch in range(NCH):
        @blk.sync
        def _(sy, ch=ch):
            b.st.wait(sy)
            b.va.wait(sy)
            b.sxa.wait(sy)
            i = sy.dma_start(out=b.xf[:, 0:T], in_=src_ap[ch * 128:(ch + 1) * 128, :])
            b.sxa.inc(i, 16)

        @blk.vector
        def _(v, ch=ch):
            b.sxa.wait(v)
            i = v.tensor_copy(b.s1[:], b.xf[:, 0:T])
            b.va.inc(i)

        @blk.sync
        def _(sy, ch=ch):
            b.va.wait(sy)
            i = sy.dma_start(out=dram["dbg"][ch * 128:(ch + 1) * 128, :], in_=b.s1[:])
            b.st.inc(i, 16)


def dbg_dump(b, blk, dram, src_fn):
    for ch in range(NCH):
        @blk.vector
        def _(v, ch=ch):
            b.st.wait(v)
            i = v.tensor_copy(b.s1[:], src_fn(ch))
            b.va.inc(i)

        @blk.sync
        def _(sy, ch=ch):
            b.va.wait(sy)
            i = sy.dma_start(out=dram["dbg"][ch * 128:(ch + 1) * 128, :], in_=b.s1[:])
            b.st.inc(i, 16)



def emit_stats_xn(b, blk, xsrc):
    ac_t, va_t, sx_t, pe_s = [], [], [], []
    for ch in range(NCH):
        @blk.sync
        def _(sy, ch=ch):
            if ch >= 2:
                b.ac.wait(sy, ac_t[ch - 2])
                b.va.wait(sy, va_t[ch - 2])
            else:
                b.ac.wait(sy)
                b.va.wait(sy)
            i = sy.dma_start(out=b.xf[:, (ch % 2) * T:(ch % 2 + 1) * T],
                             in_=xsrc[ch * 128:(ch + 1) * 128, :])
            sx_t.append(b.sx_slot[ch % 2].inc(i, 16))

        @blk.scalar
        def _(s, ch=ch):
            b.sx_slot[ch % 2].wait(s, sx_t[ch])
            if ch >= 1:
                b.va.wait(s, va_t[ch - 1])
            i = s.activation(b.s1[:], b.xf[:, (ch % 2) * T:(ch % 2 + 1) * T], AF.Square)
            ac_t.append(b.ac.inc(i))

        @blk.vector
        def _(v, ch=ch):
            b.sx_slot[ch % 2].wait(v, sx_t[ch])
            if ch == 0:
                b.pe.wait(v)
            else:
                b.pe.wait(v, pe_s[ch - 1])
            v.tensor_copy(b.sb16[:], b.xf[:, (ch % 2) * T:(ch % 2 + 1) * T])
            b.ac.wait(v, ac_t[ch])
            i = v.tensor_copy(b.pT[:], b.s1[:])
            va_t.append(b.va.inc(i))

        @blk.tensor
        def _(t, ch=ch):
            if ch == 0:
                b.sb0.wait(t)
            b.va.wait(t, va_t[ch])
            for half in range(2):
                dst = b.psV[0:1, 0:512] if half == 0 else b.psM[0:1, 0:512]
                t.matmul(dst, onesc(b), b.sb16[:, half * 512:(half + 1) * 512],
                         start=(ch == 0), stop=(ch == NCH - 1), skip_group_check=True)
            for half in range(2):
                last = t.matmul(b.psQ[0:1, half * 512:(half + 1) * 512], onesc(b),
                                b.pT[:, half * 512:(half + 1) * 512],
                                start=(ch == 0), stop=(ch == NCH - 1), skip_group_check=True)
            pe_s.append(b.pe.inc(last))

    @blk.vector
    def _(v):
        b.pe.wait(v)
        v.tensor_scalar(row(b, R_MEAN, 512), b.psV[0:1, 0:512], 1.0 / D, None, OP.mult)
        v.tensor_scalar(row(b, R_MEAN + 512, 512), b.psM[0:1, 0:512], 1.0 / D, None, OP.mult)
        i = v.tensor_scalar(row(b, R_RSTD, 1024), b.psQ[0:1, 0:1024], 1.0 / D, None, OP.mult)
        b.va.inc(i); b.va.wait(v)
        i = v.tensor_tensor(b.s1[0:1, 0:1024], row(b, R_MEAN, 1024), row(b, R_MEAN, 1024), OP.mult)
        b.va.inc(i); b.va.wait(v)
        v.tensor_tensor(row(b, R_RSTD, 1024), row(b, R_RSTD, 1024), b.s1[0:1, 0:1024], OP.subtract)
        i = v.memset(b.s1[0:1, 0:1], 1e-5)
        b.va.inc(i)

    @blk.scalar
    def _(s):
        b.va.wait(s)
        i = s.activation(row(b, R_RSTD, 1024), row(b, R_RSTD, 1024), AF.Sqrt,
                         bias=b.s1[0:1, 0:1])
        b.ac.inc(i)

    @blk.vector
    def _(v):
        b.ac.wait(v)
        i = v.reciprocal(row(b, R_RSTD, 1024), row(b, R_RSTD, 1024))
        b.va.inc(i); b.va.wait(v)
        v.tensor_copy(b.sb16[0:1, 0:1024], row(b, R_MEAN, 1024))
        i = v.tensor_copy(b.pT[0:1, 0:1024], row(b, R_RSTD, 1024))
        b.va.inc(i)

    @blk.tensor
    def _(t):
        b.va.wait(t)
        for half in range(2):
            t.matmul(b.psP0[:, half * 512:(half + 1) * 512], onesr(b),
                     b.sb16[0:1, half * 512:(half + 1) * 512], start=True, stop=True)
            last = t.matmul(b.psP1[:, half * 512:(half + 1) * 512], onesr(b),
                            b.pT[0:1, half * 512:(half + 1) * 512], start=True, stop=True)
        b.pe.inc(last)

    @blk.vector
    def _(v):
        b.pe.wait(v)
        v.tensor_copy(b.bc1[:], b.psP0[:])
        i = v.tensor_copy(b.bc2[:], b.psP1[:])
        b.va.inc(i)

    va2_t, sx2_t = [], []
    for ch in range(NCH):
        @blk.sync
        def _(sy, ch=ch):
            if ch >= 2:
                b.va.wait(sy, va2_t[ch - 2])
            else:
                b.va.wait(sy)
                b.ac.wait(sy)
            i = sy.dma_start(out=b.xf[:, (ch % 2) * T:(ch % 2 + 1) * T],
                             in_=xsrc[ch * 128:(ch + 1) * 128, :])
            sx2_t.append(b.sx_slot[ch % 2].inc(i, 16))

        @blk.vector
        def _(v, ch=ch):
            b.sx_slot[ch % 2].wait(v, sx2_t[ch])
            v.tensor_tensor(b.s1[:], b.xf[:, (ch % 2) * T:(ch % 2 + 1) * T], b.bc1[:], OP.subtract)
            i = v.tensor_tensor(b.xn_sb[:, ch * T:(ch + 1) * T], b.s1[:], b.bc2[:], OP.mult)
            va2_t.append(b.va.inc(i))



def emit_proj(b, blk, wdram, js, rhs_tiles, evac, extra_mm=None, wsm_dram=None,
              psums=None, pre_wait=None):
    if psums is None:
        psums = [b.psP0, b.psP1]
    w_t, pe_t, free_t = {}, {}, {}

    for idx, j in enumerate(js):
        @blk.gpsimd
        def _(g, idx=idx, j=j):
            if idx >= 2:
                b.pe.wait(g, pe_t[idx - 2])
            else:
                b.pe.wait(g)
            sl = b.sw_slot[idx % 2]
            i = g.dma_start(out=b.wst[:, (idx % 2) * D:(idx % 2 + 1) * D], in_=wdram[j])
            tok = sl.inc(i, 16)
            if wsm_dram is not None:
                i = g.dma_start(out=b.wsm[:, (idx % 2) * 128:(idx % 2 + 1) * 128],
                                in_=wsm_dram[j])
                tok = sl.inc(i, 16)
            w_t[idx] = tok

        @blk.tensor
        def _(t, idx=idx, j=j):
            b.sw_slot[idx % 2].wait(t, w_t[idx])
            if idx == 0:
                b.va.wait(t)
                if pre_wait is not None:
                    pre_wait(t)
            if idx < 2:
                b.ac.wait(t)
                b.va.wait(t)
            else:
                sem, tok = free_t[idx - 2]
                sem.wait(t, tok)
            ps = psums[idx % 2]
            w = b.wst[:, (idx % 2) * D:(idx % 2 + 1) * D]
            for half in range(2):
                for dc in range(NCH):
                    last = t.matmul(ps[:, half * 512:(half + 1) * 512],
                                    w[:, dc * 128:(dc + 1) * 128],
                                    rhs_tiles(dc, half),
                                    start=(dc == 0),
                                    stop=(dc == NCH - 1 and extra_mm is None),
                                    skip_group_check=True)
            if extra_mm is not None:
                last = extra_mm(t, idx, j, ps)
            pe_t[idx] = b.pe.inc(last)

        free_t[idx] = evac(idx, j, psums[idx % 2], pe_t)



def emit_kv_and_gather(b, blk, dram, pfx, nkv, wk, wv, bk_col, bv_col, *, rope=False,
                       l2_gain=None, kvph=99):
    kin, vin = dram[pfx + "_kin"], dram[pfx + "_vin"]

    def evac_k(idx, j, ps, pe_t):
        @blk.scalar
        def _(s, idx=idx, j=j, ps=ps):
            b.pe.wait(s, pe_t[idx])
            b.va.wait(s)
            b.sb0.wait(s)
            if bk_col is None:
                s.copy(b.s1[:], ps[:])
            else:
                s.activation(b.s1[:], ps[:], AF.Identity, bias=bk_col(j))
            if rope:
                s.activation(b.s2[0:64, :], ps[64:128, :], AF.Identity, bias=bk_col(j, 1))
                i = s.activation(b.s2[64:128, :], ps[0:64, :], AF.Identity, bias=bk_col(j, 2))
            elif l2_gain is not None:
                i = s.activation(b.s2[:], b.s1[:], AF.Square)
            else:
                i = s.copy(b.s2[0:1, 0:1], b.s1[0:1, 0:1])
            b.ac.inc(i)

        @blk.vector
        def _(v, idx=idx, j=j):
            b.ac.wait(v)
            if rope:
                b.sb0.wait(v)
                v.tensor_tensor(b.s1[:], b.s1[:], b.cbig[:, 0:T], OP.mult)
                v.tensor_tensor(b.s2[:], b.s2[:], b.cbig[:, T:2 * T], OP.mult)
                i = v.tensor_tensor(b.qk_sb[:, j * T:(j + 1) * T], b.s1[:], b.s2[:], OP.add)
            elif l2_gain is not None:
                b.pe.wait(v)
                v.tensor_copy(b.qk_sb[:, j * T:(j + 1) * T], b.s1[:])
                i = v.tensor_copy(b.pT[:], b.s2[:])
                b.va.inc(i)
                i = None
            else:
                i = v.tensor_copy(b.qk_sb[:, j * T:(j + 1) * T], b.s1[:])
            if i is not None:
                b.va.inc(i)
        if l2_gain is not None:
            @blk.tensor
            def _(t, idx=idx, j=j):
                b.va.wait(t)
                for half in range(2):
                    last = t.matmul(b.psQ[0:1, half * 512:(half + 1) * 512], onesc(b),
                                    b.pT[:, half * 512:(half + 1) * 512],
                                    start=(j == 0), stop=(j == nkv - 1), skip_group_check=True)
                b.pe.inc(last)
        return (b.va, b.va.n)

    emit_proj(b, blk, wk, list(range(nkv)), lambda dc, half: xn_rhs(b, dc, half), evac_k)
    if kvph < 2:
        return b.cc.n

    if l2_gain is not None:
        emit_rstd_bcast(b, blk, 1e-6)

    for j in range(nkv):
        if l2_gain is not None:
            @blk.vector
            def _(v, j=j):
                b.va.wait(v)
                b.pe.wait(v)
                b.skk.wait(v)
                v.tensor_tensor(b.s1[:], b.qk_sb[:, j * T:(j + 1) * T], b.bc2[:], OP.mult)
                i = v.tensor_scalar(b.sb16[:], b.s1[:], l2_gain(j), None, OP.mult)
                b.va.inc(i)

            @blk.scalar
            def _(s, j=j):
                b.va.wait(s)
                i = s.activation(b.s2[:], b.sb16[:], AF.Square)
                b.ac.inc(i)
        else:
            @blk.scalar
            def _(s, j=j):
                b.va.wait(s)
                i = s.activation(b.s2[:], b.qk_sb[:, j * T:(j + 1) * T], AF.Square)
                b.ac.inc(i)

        sqb = b.pT if l2_gain is not None else b.sb16

        @blk.vector
        def _(v, j=j, sqb=sqb):
            b.ac.wait(v)
            b.pe.wait(v)
            i = v.tensor_copy(sqb[:], b.s2[:])
            b.va.inc(i)

        @blk.tensor
        def _(t, j=j, sqb=sqb):
            b.va.wait(t)
            t.matmul(b.psM[0:1, 0:512], onesc(b), sqb[:, 0:512], start=True, stop=True,
                     skip_group_check=True)
            i = t.matmul(b.psV[0:1, 0:512], onesc(b), sqb[:, 512:1024], start=True,
                         stop=True, skip_group_check=True)
            b.pe.inc(i)

        @blk.vector
        def _(v, j=j):
            b.pe.wait(v)
            if j == 0:
                v.memset(row(b, R_KN, 64), 0.0)
            v.reduce_max(row(b, R_SC, 1), b.psM[0:1, 0:512], axis=AX)
            i = v.reduce_max(row(b, R_SC + 1, 1), b.psV[0:1, 0:512], axis=AX)
            b.va.inc(i); b.va.wait(v)
            i = v.tensor_tensor(row(b, R_KN + j, 1), row(b, R_SC, 1), row(b, R_SC + 1, 1), OP.max)
            b.va.inc(i)

        @blk.sync
        def _(sy, j=j):
            b.va.wait(sy)
            i = sy.dma_start(out=kin[j * 128:(j + 1) * 128, :],
                             in_=b.qk_sb[:, j * T:(j + 1) * T])
            b.skk.inc(i, 16)

    @blk.sync
    def _(sy):
        b.va.wait(sy)
        i = sy.dma_start(out=dram["gs_in"][0:1, 0:64], in_=row(b, R_KN, 64))
        b.skk.inc(i, 16)

    if kvph < 3:
        return b.cc.n

    def evac_v(idx, j, ps, pe_t):
        @blk.scalar
        def _(s, idx=idx, j=j, ps=ps):
            b.pe.wait(s, pe_t[idx])
            b.va.wait(s)
            b.sb0.wait(s)
            if bv_col is None:
                i = s.copy(b.s1[:], ps[:])
            else:
                i = s.activation(b.s1[:], ps[:], AF.Identity, bias=bv_col(j))
            b.ac.inc(i)

        @blk.vector
        def _(v, j=j):
            b.ac.wait(v)
            voff2 = 0 if nkv == 32 else nkv
            i = v.tensor_copy(b.qk_sb[:, (voff2 + j) * T:(voff2 + j + 1) * T], b.s1[:])
            b.va.inc(i)
        return (b.va, b.va.n)

    emit_proj(b, blk, wv, list(range(nkv)), lambda dc, half: xn_rhs(b, dc, half), evac_v)
    if kvph < 4:
        return b.cc.n

    voff = 0 if nkv == 32 else nkv
    gi = 0
    for tt in range(8):
        for g0 in range(0, nkv, 8):
            g1 = min(g0 + 8, nkv)
            par = gi % 2
            gi += 1
            stg = b.sb16 if par == 0 else b.pT

            @blk.tensor
            def _(t, tt=tt, g0=g0, g1=g1):
                b.va.wait(t)
                for j in range(g0, g1):
                    i = t.transpose(
                        b.psQ[:, (j - g0) * 128:(j - g0 + 1) * 128].bitcast(BF16)[:, 0:128],
                        b.qk_sb[:, (voff + j) * T + tt * 128:(voff + j) * T + (tt + 1) * 128],
                        b.cstb[:, CB_ID:CB_ID + 128])
                b.pe.inc(i)

            @blk.vector
            def _(v, tt=tt, g0=g0, g1=g1, par=par, stg=stg):
                b.pe.wait(v)
                b.skg_slot[par].wait(v)
                for j in range(g0, g1):
                    i = v.tensor_copy(stg[:, (j - g0) * 128:(j - g0 + 1) * 128],
                                      b.psQ[:, (j - g0) * 128:(j - g0 + 1) * 128].bitcast(BF16)[:, 0:128])
                b.va.inc(i)

            @blk.sync
            def _(sy, tt=tt, g0=g0, g1=g1, par=par, stg=stg):
                b.va.wait(sy)
                i = sy.dma_start(
                    out=vin.rearrange("(h t) d -> t h d", h=nkv)[tt * 128:(tt + 1) * 128, g0:g1, :],
                    in_=stg[:, 0:(g1 - g0) * 128].rearrange("p (h d) -> p h d", h=g1 - g0))
                b.skg_slot[par].inc(i, 16)

    if kvph < 5:
        return b.cc.n

    @blk.gpsimd
    def _(g):
        b.skk.wait(g)
        b.skga.wait(g)
        b.skgb.wait(g)
        rg = [[0, 1, 2, 3], [4, 5, 6, 7]]
        i = g.collective_compute("AllGather", OP.bypass, replica_groups=rg,
                                 ins=[dram["gs_in"].opt()], outs=[dram["gs_out"].opt()])
        b.cc.inc(i, 1)
        i = g.collective_compute("AllGather", OP.bypass, replica_groups=rg,
                                 ins=[kin.opt()], outs=[dram[pfx + "_kout"].opt()])
        b.cc.inc(i, 1)
        i = g.collective_compute("AllGather", OP.bypass, replica_groups=rg,
                                 ins=[vin.opt()], outs=[dram[pfx + "_vout"].opt()])
        b.cc.inc(i, 1)
    return b.cc.n


def emit_rstd_bcast(b, blk, eps):
    @blk.vector
    def _(v):
        b.pe.wait(v)
        i = v.tensor_scalar(row(b, R_RSTD, 1024), b.psQ[0:1, 0:1024], 1.0 / D, None, OP.mult)
        b.va.inc(i); b.va.wait(v)
        i = v.memset(b.s1[0:1, 0:1], eps)
        b.va.inc(i)

    @blk.scalar
    def _(s):
        b.va.wait(s)
        i = s.activation(row(b, R_RSTD, 1024), row(b, R_RSTD, 1024), AF.Sqrt,
                         bias=b.s1[0:1, 0:1])
        b.ac.inc(i)

    @blk.vector
    def _(v):
        b.ac.wait(v)
        i = v.reciprocal(row(b, R_RSTD, 1024), row(b, R_RSTD, 1024))
        b.va.inc(i); b.va.wait(v)
        i = v.tensor_copy(b.pT[0:1, 0:1024], row(b, R_RSTD, 1024))
        b.va.inc(i)

    @blk.tensor
    def _(t):
        b.va.wait(t)
        for half in range(2):
            last = t.matmul(b.psP0[:, half * 512:(half + 1) * 512], onesr(b),
                            b.pT[0:1, half * 512:(half + 1) * 512], start=True, stop=True,
                            skip_group_check=True)
        b.pe.inc(last)

    @blk.vector
    def _(v):
        b.pe.wait(v)
        i = v.tensor_copy(b.bc2[:], b.psP0[:])
        b.va.inc(i)


def emit_gs_readback(b, blk, dram, cc_stats_tok):
    @blk.sync
    def _(sy):
        b.cc.wait(sy, cc_stats_tok)
        i = sy.dma_start(out=row(b, R_GS, 256),
                         in_=dram["gs_out"].rearrange("r k -> (r k)").unsqueeze(0))
        b.sgs.inc(i, 16)

    @blk.vector
    def _(v):
        b.sgs.wait(v)
        i = v.tensor_reduce(row(b, R_GK2, 64),
                            row(b, R_GS, 256).rearrange("p (r k) -> p k r", r=4),
                            axis=AX, op=OP.max)
        b.va.inc(i)



def emit_router(b, blk, dram, gw_name, topk, scale_out, br_name=None):
    E = 8

    @blk.gpsimd
    def _(g):
        b.va.wait(g)
        b.pe.wait(g)
        b.sws.wait(g)
        i = g.dma_start(out=b.sb16[:, 0:NCH * E], in_=dram[gw_name])
        b.sws.inc(i, 16)

    if br_name is not None:
        @blk.gpsimd
        def _(g):
            b.sb0.wait(g)
            i = g.dma_start(out=b.btile[0:8, 64:65], in_=dram[br_name])
            b.sb0.inc(i, 16)

    @blk.tensor
    def _(t):
        b.sws.wait(t)
        b.va.wait(t)
        for half in range(2):
            dst = b.psM[0:E, 0:512] if half == 0 else b.psV[0:E, 0:512]
            for dc in range(NCH):
                last = t.matmul(dst, b.sb16[:, dc * E:(dc + 1) * E], xn_rhs(b, dc, half),
                                start=(dc == 0), stop=(dc == NCH - 1), skip_group_check=True)
        b.pe.inc(last)

    @blk.scalar
    def _(s):
        b.pe.wait(s)
        if br_name is not None:
            b.sb0.wait(s)
            s.activation(b.s1[0:E, 0:512], b.psM[0:E, 0:512], AF.Identity, bias=b.btile[0:8, 64:65])
            i = s.activation(b.s1[0:E, 512:1024], b.psV[0:E, 0:512], AF.Identity, bias=b.btile[0:8, 64:65])
        else:
            s.copy(b.s1[0:E, 0:512], b.psM[0:E, 0:512])
            i = s.copy(b.s1[0:E, 512:1024], b.psV[0:E, 0:512])
        b.ac.inc(i)

    @blk.tensor
    def _(t):
        b.ac.wait(t)
        for tt in range(8):
            last = t.transpose(b.psQ[:, tt * 8:(tt + 1) * 8],
                               b.s1[0:E, tt * 128:(tt + 1) * 128], ident_f32(b)[0:E, 0:E])
        b.pe.inc(last)

    @blk.vector
    def _(v):
        b.pe.wait(v)
        i = v.tensor_copy(b.tm[:, 0:64], b.psQ[:, 0:64])
        b.va.inc(i)

    for tt in range(8):
        @blk.vector
        def _(v, tt=tt):
            b.va.wait(v)
            i = v.reduce_max(b.tm2[:, tt:tt + 1], b.tm[:, tt * 8:(tt + 1) * 8], axis=AX)
            b.va.inc(i); b.va.wait(v)
            i = v.tensor_scalar(b.tm2[:, 8 + tt:9 + tt], b.tm2[:, tt:tt + 1], -1.0, None, OP.mult)
            b.va.inc(i)

        @blk.scalar
        def _(s, tt=tt):
            b.va.wait(s)
            i = s.activation(b.tm3[:, tt * 8:(tt + 1) * 8], b.tm[:, tt * 8:(tt + 1) * 8],
                             AF.Identity, bias=b.tm2[:, 8 + tt:9 + tt])
            b.ac.inc(i)

        if topk == 2:
            @blk.vector
            def _(v, tt=tt):
                b.ac.wait(v)
                tl = b.tm3[:, tt * 8:(tt + 1) * 8]
                i = v.tensor_scalar(b.s2[:, 0:8], tl, 0.0, None, OP.is_ge)
                b.va.inc(i); b.va.wait(v)
                i = v.scalar_tensor_tensor(b.s2[:, 8:16], b.s2[:, 0:8], -1e30, tl, OP.mult, OP.add)
                b.va.inc(i); b.va.wait(v)
                i = v.reduce_max(b.tm2[:, 24 + tt:25 + tt], b.s2[:, 8:16], axis=AX)
                b.va.inc(i)

            @blk.scalar
            def _(s, tt=tt):
                b.va.wait(s)
                s.activation(b.tm3[:, 64 + tt * 8:72 + tt * 8], b.tm3[:, tt * 8:(tt + 1) * 8], AF.Exp)
                i = s.activation(b.tm2[:, 32 + tt:33 + tt], b.tm2[:, 24 + tt:25 + tt], AF.Exp)
                b.ac.inc(i)

            @blk.vector
            def _(v, tt=tt):
                b.ac.wait(v)
                i = v.tensor_scalar(b.tm2[:, 40 + tt:41 + tt], b.tm2[:, 32 + tt:33 + tt], 1.0, None, OP.add)
                b.va.inc(i); b.va.wait(v)
                i = v.reciprocal(b.tm2[:, 48 + tt:49 + tt], b.tm2[:, 40 + tt:41 + tt])
                b.va.inc(i)

            @blk.gpsimd
            def _(g, tt=tt):
                b.va.wait(g)
                b.ac.wait(g)
                i = g.tensor_scalar(b.s2[:, 16:24],
                                b.tm3[:, tt * 8:(tt + 1) * 8], b.tm2[:, 24 + tt:25 + tt],
                                None, OP.is_ge)
                b.pl.inc(i); b.pl.wait(g)
                i = g.tensor_tensor(b.s2[:, 24:32], b.tm3[:, 64 + tt * 8:72 + tt * 8],
                                b.s2[:, 16:24], OP.mult)
                b.pl.inc(i); b.pl.wait(g)
                i = g.tensor_scalar(b.tm3[:, 128 + tt * 8:136 + tt * 8], b.s2[:, 24:32],
                                    b.tm2[:, 48 + tt:49 + tt], scale_out, OP.mult, OP.mult)
                b.pl.inc(i)
        else:
            @blk.scalar
            def _(s, tt=tt):
                i = s.activation(b.tm3[:, 64 + tt * 8:72 + tt * 8],
                                 b.tm3[:, tt * 8:(tt + 1) * 8], AF.Exp)
                b.ac.inc(i)

            @blk.vector
            def _(v, tt=tt):
                b.ac.wait(v)
                i = v.reduce_sum(b.tm2[:, 40 + tt:41 + tt], b.tm3[:, 64 + tt * 8:72 + tt * 8], axis=AX)
                b.va.inc(i); b.va.wait(v)
                i = v.reciprocal(b.tm2[:, 48 + tt:49 + tt], b.tm2[:, 40 + tt:41 + tt])
                b.va.inc(i)

            @blk.gpsimd
            def _(g, tt=tt):
                b.va.wait(g)
                b.ac.wait(g)
                i = g.tensor_scalar(b.tm3[:, 128 + tt * 8:136 + tt * 8],
                                    b.tm3[:, 64 + tt * 8:72 + tt * 8],
                                    b.tm2[:, 48 + tt:49 + tt], scale_out, OP.mult, OP.mult)
                b.pl.inc(i)

    @blk.tensor
    def _(t):
        b.pl.wait(t)
        for tt in range(8):
            last = t.transpose(b.psP0[0:8, tt * 128:(tt + 1) * 128],
                               b.tm3[:, 128 + tt * 8:136 + tt * 8], ident_f32(b))
        b.pe.inc(last)

    @blk.vector
    def _(v):
        b.pe.wait(v)
        i = v.tensor_copy(b.pT[0:8, 0:1024], b.psP0[0:8, 0:1024])
        b.va.inc(i)


def emit_lora_z(b, blk, dram, a_name, scaling, wb_off=17 * T):
    @blk.tensor
    def _(t):
        b.va.wait(t)
        for half in range(2):
            last = t.matmul(b.psQ[:, half * 512:(half + 1) * 512],
                            b.cstb[0:8, CB_E8:CB_E8 + 128],
                            b.pT[0:8, half * 512:(half + 1) * 512],
                            start=True, stop=True, skip_group_check=True)
        b.pe.inc(last)

    @blk.vector
    def _(v):
        b.pe.wait(v)
        i = v.tensor_copy(b.qk_sb[:, wb_off:wb_off + 1024], b.psQ[:, 0:1024])
        b.va.inc(i)

    @blk.gpsimd
    def _(g):
        b.pe.wait(g)
        b.sws.wait(g)
        i = g.dma_start(out=b.wst[:, 0:D], in_=dram[a_name][0])
        b.sws.inc(i, 16)

    @blk.tensor
    def _(t):
        b.sws.wait(t)
        b.va.wait(t)
        for half in range(2):
            for dc in range(NCH):
                last = t.matmul(b.psP0[:, half * 512:(half + 1) * 512],
                                b.wst[:, dc * 128:(dc + 1) * 128], xn_rhs(b, dc, half),
                                start=(dc == 0), stop=(dc == NCH - 1), skip_group_check=True)
        b.pe.inc(last)

    @blk.vector
    def _(v):
        b.pe.wait(v)
        i = v.scalar_tensor_tensor(b.qh_sb[:, 0:1024], b.psP0[:, 0:1024], scaling,
                                   b.qk_sb[:, wb_off:wb_off + 1024], OP.mult, OP.mult)
        b.va.inc(i)


def emit_ffn_base(b, blk, dram, w_name, b_name, bsm_name, xsrc):
    xmid = dram["xmid"]

    @blk.gpsimd
    def _(g):
        b.sb0.wait(g)
        i = g.dma_start(out=b.btile[:, 0:32], in_=dram[b_name])
        b.sb0.inc(i, 16)
    bias_tok = b.sb0.n

    def extra(t, idx, j, ps):
        for half in range(2):
            last = t.matmul(ps[:, half * 512:(half + 1) * 512],
                            b.wsm[:, (idx % 2) * 128:(idx % 2 + 1) * 128],
                            b.qh_sb[:, half * 512:half * 512 + 512],
                            start=False, stop=True, skip_group_check=True)
        return last

    def evac(idx, j, ps, pe_t):
        @blk.sync
        def _(sy, idx=idx, j=j):
            if idx >= 2:
                b.va.wait(sy)
            i = sy.dma_start(out=b.xf[:, (idx % 2) * T:(idx % 2 + 1) * T],
                             in_=xsrc[j * 128:(j + 1) * 128, :])
            b.sx_slot[idx % 2].inc(i, 16)
        sx_tok = b.sx_slot[idx % 2].n

        @blk.scalar
        def _(s, idx=idx, j=j, ps=ps):
            b.pe.wait(s, pe_t[idx])
            b.va.wait(s)
            if idx == 0:
                b.sb0.wait(s, bias_tok)
            i = s.activation(b.s1[:], ps[:], AF.Identity, bias=b.btile[:, j:j + 1])
            b.ac.inc(i)

        @blk.vector
        def _(v, idx=idx, j=j, sx_tok=sx_tok):
            b.ac.wait(v)
            b.sx_slot[idx % 2].wait(v, sx_tok)
            i = v.tensor_tensor(b.s2[:], b.s1[:], b.xf[:, (idx % 2) * T:(idx % 2 + 1) * T], OP.add)
            b.va.inc(i)

        @blk.sync
        def _(sy, idx=idx, j=j):
            b.va.wait(sy)
            i = sy.dma_start(out=xmid[j * 128:(j + 1) * 128, :], in_=b.s2[:])
            b.st.inc(i, 16)
        return (b.va, b.va.n)

    emit_proj(b, blk, dram[w_name], list(range(NCH)),
              lambda dc, half: xn_rhs(b, dc, half), evac, extra_mm=extra,
              wsm_dram=dram[bsm_name], pre_wait=lambda t: b.va.wait(t))



def emit_attention(b, blk, dram, cfg, xdst):
    nkv, grp = cfg["nkv"], cfg["group"]
    kout, vout = dram[cfg["pfx"] + "_kout"], dram[cfg["pfx"] + "_vout"]

    for kv in range(nkv):
        @blk.sync
        def _(sy, kv=kv):
            b.cc.wait(sy, cfg["cc_v"])
            if kv >= 1:
                b.pe.wait(sy)
            i = sy.dma_start(out=b.kv_sb[:, 0:4096].rearrange("p (r t) -> p r t", r=4),
                             in_=kout.rearrange("(r d) t -> d r t", r=4)[kv * 128:(kv + 1) * 128, :, :])
            b.skr.inc(i, 16)
            vv = vout.rearrange("(r h c p) d -> r h p c d", r=4, h=nkv, c=8)
            for r in range(4):
                i = sy.dma_start(
                    out=b.kv_sb[:, 4096 + r * 1024:4096 + (r + 1) * 1024].rearrange(
                        "p (c d) -> p c d", c=8),
                    in_=vv[r, kv])
                b.skr.inc(i, 16)
        kv_tok = b.skr.n

        for hh in range(grp):
            emit_attn_head(b, blk, dram, cfg, kv * grp + hh, kv, kv_tok, xdst)

    if cfg["out"] != "residual":
        emit_o_proj(b, blk, dram, cfg, xdst)


def emit_attn_head(b, blk, dram, cfg, h, kv, kv_tok, xdst):
    if cfg["q_mode"] == "jit":
        wq = dram[cfg["wq_name"]]

        @blk.gpsimd
        def _(g, h=h):
            b.pe.wait(g)
            i = g.dma_start(out=b.wst[:, (h % 2) * D:(h % 2 + 1) * D], in_=wq[h])
            b.sw_slot[h % 2].inc(i, 16)
        sw_tok = b.sw_slot[h % 2].n

        @blk.tensor
        def _(t, h=h, sw_tok=sw_tok):
            b.sw_slot[h % 2].wait(t, sw_tok)
            b.ac.wait(t)
            b.va.wait(t)
            for half in range(2):
                for dc in range(NCH):
                    last = t.matmul(b.psP0[:, half * 512:(half + 1) * 512],
                                    b.wst[:, (h % 2) * D + dc * 128:(h % 2) * D + (dc + 1) * 128],
                                    xn_rhs(b, dc, half),
                                    start=(dc == 0), stop=(dc == NCH - 1), skip_group_check=True)
            b.pe.inc(last)

        @blk.scalar
        def _(s, h=h):
            b.pe.wait(s)
            b.va.wait(s)
            b.sb0.wait(s)
            bq = cfg.get("bq_col")
            if cfg.get("rope"):
                s.activation(b.s1[:], b.psP0[:], AF.Identity, bias=bq(h, 0))
                s.activation(b.s2[0:64, :], b.psP0[64:128, :], AF.Identity, bias=bq(h, 1))
                i = s.activation(b.s2[64:128, :], b.psP0[0:64, :], AF.Identity, bias=bq(h, 2))
            elif bq is not None:
                i = s.activation(b.s1[:], b.psP0[:], AF.Identity, bias=bq(h, 0))
            else:
                i = s.copy(b.s1[:], b.psP0[:])
            b.ac.inc(i)

        @blk.vector
        def _(v, h=h):
            b.ac.wait(v)
            if cfg.get("rope"):
                b.sb0.wait(v)
                v.tensor_tensor(b.s1[:], b.s1[:], b.cbig[:, 0:T], OP.mult)
                v.tensor_tensor(b.s2[:], b.s2[:], b.cbig[:, T:2 * T], OP.mult)
                i = v.tensor_tensor(b.qh_sb[:], b.s1[:], b.s2[:], OP.add)
            else:
                i = v.tensor_copy(b.qh_sb[:], b.s1[:])
            b.va.inc(i)
        qsrc = b.qh_sb[:]
    else:
        qsrc = b.qk_sb[:, h * T:(h + 1) * T]

    @blk.scalar
    def _(s, h=h):
        b.va.wait(s)
        i = s.activation(b.s2[:], qsrc, AF.Square)
        b.ac.inc(i)

    @blk.vector
    def _(v, h=h):
        b.ac.wait(v)
        b.pe.wait(v)
        i = v.tensor_copy(b.sb16[:], b.s2[:])
        b.va.inc(i)

    @blk.tensor
    def _(t, h=h):
        b.va.wait(t)
        t.matmul(b.psM[0:1, 0:512], onesc(b), b.sb16[:, 0:512], start=True, stop=True,
                 skip_group_check=True)
        i = t.matmul(b.psV[0:1, 0:512], onesc(b), b.sb16[:, 512:1024], start=True,
                     stop=True, skip_group_check=True)
        b.pe.inc(i)

    @blk.vector
    def _(v, h=h, kv=kv):
        b.pe.wait(v)
        v.reduce_max(row(b, R_C, 1), b.psM[0:1, 0:512], axis=AX)
        i = v.reduce_max(row(b, R_C + 1, 1), b.psV[0:1, 0:512], axis=AX)
        b.va.inc(i); b.va.wait(v)
        i = v.tensor_tensor(row(b, R_C + 2, 1), row(b, R_C, 1), row(b, R_C + 1, 1), OP.max)
        b.va.inc(i); b.va.wait(v)
        i = v.tensor_tensor(row(b, R_C + 3, 1), row(b, R_C + 2, 1),
                            row(b, R_GK2 + kv, 1), OP.mult)
        b.va.inc(i)

    @blk.scalar
    def _(s, h=h):
        b.va.wait(s)
        i = s.activation(row(b, R_C + 4, 1), row(b, R_C + 3, 1), AF.Sqrt)
        b.ac.inc(i)

    @blk.vector
    def _(v, h=h):
        b.ac.wait(v)
        i = v.tensor_scalar(row(b, R_C + 5, 1), row(b, R_C + 4, 1), -ISQ, None, OP.mult)
        b.va.inc(i); b.va.wait(v)
        i = v.tensor_copy(b.sb16[0:1, 0:1], row(b, R_C + 5, 1))
        b.va.inc(i)

    @blk.tensor
    def _(t, h=h):
        b.va.wait(t)
        i = t.matmul(b.psM[:, 0:1], onesr(b), b.sb16[0:1, 0:1], start=True, stop=True,
                     skip_group_check=True)
        b.pe.inc(i)

    @blk.vector
    def _(v, h=h):
        b.pe.wait(v)
        i = v.tensor_copy(b.biasC[:, h % 16:h % 16 + 1], b.psM[:, 0:1])
        b.va.inc(i)
    biasC_tok = b.va.n

    nchunks = 32
    GA = 3
    slots = cfg["expslots"]
    ac_g, pe_g = {}, {}
    groups = [list(range(g0, min(g0 + GA, nchunks))) for g0 in range(0, nchunks, GA)]
    ngr = len(groups)

    def emit_qkt(qt, g):
        cs = groups[g]

        @blk.tensor
        def _(t, g=g, cs=cs, qt=qt, h=h):
            if g == 0 and qt == 0:
                b.skr.wait(t, kv_tok)
            if g >= 2:
                b.ac.wait(t, ac_g[(qt, g - 2)])
            elif qt == 1:
                b.ac.wait(t, ac_g[(0, ngr - 1)])
            for c in cs:
                i = t.matmul(b.psW[:, (c % 6) * 512:(c % 6 + 1) * 512],
                             b.kv_sb[:, c * 128:(c + 1) * 128],
                             qsrc[:, qt * 512:(qt + 1) * 512],
                             start=True, stop=True, skip_group_check=True)
            pe_g[(qt, g)] = b.pe.inc(i)

    for qt in range(2):
        emit_qkt(qt, 0)
        for g, cs in enumerate(groups):
            @blk.scalar
            def _(s, g=g, cs=cs, qt=qt, h=h):
                b.pe.wait(s, pe_g[(qt, g)])
                if g == 0 and qt == 0:
                    b.va.wait(s, biasC_tok)
                for c in cs:
                    s.activation(slots(c % 6), b.psW[:, (c % 6) * 512:(c % 6 + 1) * 512],
                                 AF.Exp, bias=b.biasC[:, h % 16:h % 16 + 1], scale=ISQ)
                i = s.drain()
                ac_g[(qt, g)] = b.ac.inc(i)

            if g + 1 < ngr:
                emit_qkt(qt, g + 1)

            @blk.tensor
            def _(t, g=g, cs=cs, qt=qt):
                b.ac.wait(t, ac_g[(qt, g)])
                if g == 0:
                    b.va.wait(t)
                for c in cs:
                    t.matmul(b.psM[0:1, 0:512], onesc(b), slots(c % 6),
                             start=(c == 0), stop=(c == nchunks - 1), skip_group_check=True)
                    i = t.matmul(b.psV[:, 0:512], b.kv_sb[:, 4096 + c * 128:4096 + (c + 1) * 128],
                                 slots(c % 6),
                                 start=(c == 0), stop=(c == nchunks - 1), skip_group_check=True)
                b.pe.inc(i)
        @blk.vector
        def _(v, qt=qt, h=h):
            b.pe.wait(v)
            i = v.reciprocal(row(b, R_RECIP, 512), b.psM[0:1, 0:512])
            b.va.inc(i); b.va.wait(v)
            i = v.tensor_copy(b.sb16[0:1, 0:512], row(b, R_RECIP, 512))
            b.va.inc(i)

        @blk.tensor
        def _(t, qt=qt):
            b.va.wait(t)
            i = t.matmul(b.psM[:, 0:512], onesr(b), b.sb16[0:1, 0:512], start=True,
                         stop=True, skip_group_check=True)
            b.pe.inc(i)

        @blk.vector
        def _(v, qt=qt, h=h):
            b.pe.wait(v)
            v.tensor_copy(b.bc1[:, 0:512], b.psM[:, 0:512])
            och = 16 if cfg["out"] == "residual" else h
            i = v.tensor_tensor(b.qk_sb[:, och * T + qt * 512:och * T + qt * 512 + 512],
                                b.psV[:, 0:512], b.bc1[:, 0:512], OP.mult)
            b.va.inc(i)

    if cfg["out"] == "residual":
        @blk.sync
        def _(sy, h=h):
            b.va.wait(sy)
            i = sy.dma_start(out=b.xf[:, (h % 2) * T:(h % 2 + 1) * T],
                             in_=dram["xmid"][h * 128:(h + 1) * 128, :])
            b.sx_slot[h % 2].inc(i, 16)
        sx_tok = b.sx_slot[h % 2].n

        @blk.vector
        def _(v, h=h, sx_tok=sx_tok):
            b.sx_slot[h % 2].wait(v, sx_tok)
            i = v.tensor_tensor(b.s1[:], b.qk_sb[:, 16 * T:17 * T],
                                b.xf[:, (h % 2) * T:(h % 2 + 1) * T], OP.add)
            b.va.inc(i)

        @blk.sync
        def _(sy, h=h):
            b.va.wait(sy)
            i = sy.dma_start(out=xdst[h * 128:(h + 1) * 128, :], in_=b.s1[:])
            b.st.inc(i, 16)


def emit_o_proj(b, blk, dram, cfg, xdst):
    xmid = dram["xmid"]

    if cfg.get("bo_name"):
        @blk.gpsimd
        def _(g):
            b.sb0.wait(g)
            i = g.dma_start(out=b.btile[:, 32:64], in_=dram[cfg["bo_name"]])
            b.sb0.inc(i, 16)
    bias_tok = b.sb0.n

    def evac(idx, j, ps, pe_t):
        @blk.sync
        def _(sy, idx=idx, j=j):
            if idx >= 2:
                b.va.wait(sy)
            i = sy.dma_start(out=b.xf[:, (idx % 2) * T:(idx % 2 + 1) * T],
                             in_=xmid[j * 128:(j + 1) * 128, :])
            b.sx_slot[idx % 2].inc(i, 16)
        sx_tok = b.sx_slot[idx % 2].n

        @blk.scalar
        def _(s, idx=idx, j=j, ps=ps):
            b.pe.wait(s, pe_t[idx])
            b.va.wait(s)
            if idx == 0:
                b.sb0.wait(s, bias_tok)
            if cfg.get("bo_name"):
                i = s.activation(b.s1[:], ps[:], AF.Identity, bias=b.btile[:, 32 + j:33 + j])
            else:
                i = s.copy(b.s1[:], ps[:])
            b.ac.inc(i)

        @blk.vector
        def _(v, idx=idx, j=j, sx_tok=sx_tok):
            b.ac.wait(v)
            b.sx_slot[idx % 2].wait(v, sx_tok)
            i = v.tensor_tensor(b.s2[:], b.s1[:], b.xf[:, (idx % 2) * T:(idx % 2 + 1) * T], OP.add)
            b.va.inc(i)

        @blk.sync
        def _(sy, idx=idx, j=j):
            b.va.wait(sy)
            i = sy.dma_start(out=xdst[j * 128:(j + 1) * 128, :], in_=b.s2[:])
            b.st.inc(i, 16)
        return (b.va, b.va.n)

    emit_proj(b, blk, dram[cfg["out"]], list(range(NCH)),
              lambda dc, half: b.qk_sb[:, dc * T + half * 512:dc * T + half * 512 + 512],
              evac)



def emit_block0(b, blk, dram, xsrc, xdst, dbg):
    ph = 99
    if dbg:
        m = _re.search(r"ph(\d+)", dbg)
        if m:
            ph = int(m.group(1))
    @blk.gpsimd
    def _(g):
        b.sb0.wait(g)
        i = g.dma_start(out=b.cbig[:, 0:T], in_=dram["l0_cc"]); b.sb0.inc(i, 16)
        i = g.dma_start(out=b.cbig[:, T:2 * T], in_=dram["l0_ss"]); b.sb0.inc(i, 16)
        i = g.dma_start(out=b.btile[:, 32:64], in_=dram["l0_bq"]); b.sb0.inc(i, 16)
        i = g.dma_start(out=b.btile[:, 80:88], in_=dram["l0_bk"]); b.sb0.inc(i, 16)
        i = g.dma_start(out=b.btile[:, 88:96], in_=dram["l0_bv"]); b.sb0.inc(i, 16)

    def bk_col(j, swap=0):
        col = b.btile[:, 80 + j:81 + j]
        return [col, b.btile[64:128, 80 + j:81 + j], b.btile[0:64, 80 + j:81 + j]][swap]

    def bv_col(j):
        return b.btile[:, 88 + j:89 + j]

    def bq_col(h, swap=0):
        col = b.btile[:, 32 + h:33 + h]
        return [col, b.btile[64:128, 32 + h:33 + h], b.btile[0:64, 32 + h:33 + h]][swap]

    if ph < 1:
        return
    kvph = 99
    if dbg:
        m2 = _re.search(r"kv(\d+)", dbg)
        if m2:
            kvph = int(m2.group(1))
    cc_end = emit_kv_and_gather(b, blk, dram, "g0", 8, dram["l0_wk"], dram["l0_wv"],
                                bk_col, bv_col, rope=True, kvph=kvph)
    if ph < 2:
        return
    emit_router(b, blk, dram, "l0_gw", 2, 2.0)
    if ph < 3:
        return
    emit_lora_z(b, blk, dram, "l0_A", 1.0)
    if ph < 4:
        return
    emit_ffn_base(b, blk, dram, "l0_bw", "l0_bb", "l0_B", xsrc)
    if ph < 5:
        if dbg and "mid" in dbg:
            dbg_dump_dram(b, blk, dram, dram["xmid"])
        if dbg and "ffn" in dbg:
            @blk.sync
            def _(sy):
                b.va.wait(sy)
                b.st.wait(sy)
                b.pl.wait(sy)
                i = sy.dma_start(out=dram["dbg"][0:128, 0:192], in_=b.tm3[:]); b.st.inc(i, 16)
                i = sy.dma_start(out=dram["dbg"][128:256, 0:512], in_=b.pT[:].bitcast(F32)); b.st.inc(i, 16)
                i = sy.dma_start(out=dram["dbg"][256:384, 0:512], in_=b.qh_sb[:].bitcast(F32)); b.st.inc(i, 16)
                i = sy.dma_start(out=dram["dbg"][384:512, 0:512],
                                 in_=b.qk_sb[:, 17 * T:18 * T].bitcast(F32)[:, 0:512]); b.st.inc(i, 16)
                i = sy.dma_start(out=dram["dbg"][512:640, 0:512], in_=b.sb16[:].bitcast(F32)); b.st.inc(i, 16)
                i = sy.dma_start(out=dram["dbg"][640:768, 0:512],
                                 in_=b.xn_sb[:, 0:1024].bitcast(F32)[:, 0:512]); b.st.inc(i, 16)
                i = sy.dma_start(out=dram["dbg"][768:896, 0:64], in_=b.tm[:]); b.st.inc(i, 16)
        return
    emit_gs_readback(b, blk, dram, cc_end - 2)
    if ph < 6:
        return
    cfg = dict(pfx="g0", nkv=8, group=4, q_mode="jit", wq_name="l0_wq",
               bq_col=bq_col, rope=True, out="residual",
               expslots=lambda i: b.qk_sb[:, 24 * T + i * 512:24 * T + (i + 1) * 512],
               cc_k=cc_end - 1, cc_v=cc_end)
    emit_attention(b, blk, dram, cfg, xdst)
    if dbg and "mid" in dbg:
        dbg_dump_dram(b, blk, dram, dram["xmid"])
    if dbg and "rows" in dbg:
        @blk.sync
        def _(sy):
            b.va.wait(sy)
            b.st.wait(sy)
            i = sy.dma_start(out=dram["dbg"][0:128, :], in_=b.rows[:, 0:1024])
            b.st.inc(i, 16)
            i = sy.dma_start(out=dram["dbg"][128:256, :], in_=b.rows[:, 1024:2048])
            b.st.inc(i, 16)
            i = sy.dma_start(out=dram["dbg"][256:384, 0:16], in_=b.biasC[:, 0:16])
            b.st.inc(i, 16)
            i = sy.dma_start(out=dram["dbg"][384:512, 0:512],
                             in_=b.qk_sb[:, 16 * T:17 * T].bitcast(F32)[:, 0:512])
            b.st.inc(i, 16)




def emit_bm01(b, blk, dram):
    mask = dram["l1_mask"]
    for nc_ in range(NCH):
        for qtr in range(4):
            sl = (nc_ * 4 + qtr) % 2

            @blk.sync
            def _(sy, nc_=nc_, qtr=qtr, sl=sl):
                b.va.wait(sy)
                b.ac.wait(sy)
                i = sy.dma_start(
                    out=b.xf[:, sl * T:(sl + 1) * T].bitcast(I32),
                    in_=mask[nc_ * 128:(nc_ + 1) * 128, qtr * 1024:(qtr + 1) * 1024])
                b.sx_slot[sl].inc(i, 16)

            @blk.vector
            def _(v, nc_=nc_, qtr=qtr, sl=sl):
                b.sx_slot[sl].wait(v)
                i = v.tensor_copy(b.s1[:], b.xf[:, sl * T:(sl + 1) * T].bitcast(I32))
                b.va.inc(i); b.va.wait(v)
                i = v.tensor_reduce(b.s2[:, qtr * 32:(qtr + 1) * 32],
                                    b.s1[:].rearrange("p (kb q) -> p kb q", q=32),
                                    axis=AX, op=OP.add)
                b.va.inc(i); b.va.wait(v)
                if qtr == 3:
                    i = v.tensor_copy(b.sb16[:, 0:128], b.s2[:, 0:128])
                    b.va.inc(i)

        @blk.tensor
        def _(t, nc_=nc_):
            b.va.wait(t)
            i = t.matmul(b.psM[0:4, 0:128], b.cstb[:, CB_G32:CB_G32 + 4], b.sb16[:, 0:128],
                         start=True, stop=True, skip_group_check=True)
            b.pe.inc(i)

        @blk.scalar
        def _(s, nc_=nc_):
            b.pe.wait(s)
            b.st.wait(s)
            i = s.copy(b.s1[0:4, 0:128], b.psM[0:4, 0:128])
            b.ac.inc(i)

        @blk.sync
        def _(sy, nc_=nc_):
            b.ac.wait(sy)
            i = sy.dma_start(out=dram["bm01d"][nc_ * 4:(nc_ + 1) * 4, :], in_=b.s1[0:4, 0:128])
            b.st.inc(i, 16)

    @blk.sync
    def _(sy):
        b.st.wait(sy)
        i = sy.dma_start(out=b.s2[:, 0:128], in_=dram["bm01d"])
        b.sgs.inc(i, 16)

    @blk.vector
    def _(v):
        b.sgs.wait(v)
        i = v.tensor_scalar(b.tm3[:, 0:128], b.s2[:, 0:128], 0.0, None, OP.is_gt)
        b.va.inc(i)


def emit_emgen(b, blk, dram):
    @blk.gpsimd
    def _(g):
        b.sws.wait(g)
        i = g.dma_start(out=b.kv_sb[:, 0:4096], in_=dram["qexp"])
        b.sws.inc(i, 16)

    for kc in range(NCH):
        @blk.tensor
        def _(t, kc=kc):
            b.va.wait(t)
            if kc == 0:
                b.sws.wait(t)
            b.st.wait(t)
            i = t.transpose(b.psM[0:4, 0:128], b.tm3[:, 0:128][:, 4 * kc:4 * kc + 4],
                            ident_f32(b))
            b.pe.inc(i)

        @blk.vector
        def _(v, kc=kc):
            b.pe.wait(v)
            i = v.tensor_copy(b.sb16[0:4, 0:128], b.psM[0:4, 0:128])
            b.va.inc(i)

        @blk.tensor
        def _(t, kc=kc):
            b.va.wait(t)
            i = t.matmul(b.psQ[:, 0:128], b.sb16[0:4, 0:128],
                         b.cstb[0:4, CB_P32:CB_P32 + 128],
                         start=True, stop=True, skip_group_check=True)
            b.pe.inc(i)

        @blk.vector
        def _(v, kc=kc):
            b.pe.wait(v)
            i = v.tensor_copy(b.sb16[:, 0:128], b.psQ[:, 0:128])
            b.va.inc(i)

        for pair in range(4):
            @blk.tensor
            def _(t, kc=kc, pair=pair):
                b.va.wait(t)
                if pair > 0:
                    b.ac.wait(t)
                last = None
                for k2 in range(2):
                    last = t.matmul(b.psP0[:, k2 * 512:(k2 + 1) * 512],
                                    b.sb16[:, 0:128],
                                    b.kv_sb[:, (pair * 2 + k2) * 512:(pair * 2 + k2 + 1) * 512],
                                    start=True, stop=True, skip_group_check=True)
                b.pe.inc(last)

            @blk.scalar
            def _(s, kc=kc, pair=pair):
                b.pe.wait(s)
                i2 = s.copy(b.kv_sb[:, 4096 + pair * 1024:4096 + (pair + 1) * 1024],
                            b.psP0[:, 0:1024])
                b.ac.inc(i2)

        @blk.sync
        def _(sy, kc=kc):
            b.ac.wait(sy)
            i = sy.dma_start(out=dram["em_scr"][kc], in_=b.kv_sb[:, 4096:8192])
            b.st.inc(i, 16)


def emit_ffn_lora_only(b, blk, dram, bsm_name, xsrc):
    xmid = dram["xmid"]
    w_t, pe_t = {}, {}
    for j in range(NCH):
        @blk.gpsimd
        def _(g, j=j):
            if j >= 2:
                b.pe.wait(g, pe_t[j - 2])
            else:
                b.pe.wait(g)
            i = g.dma_start(out=b.wsm[:, (j % 2) * 128:(j % 2 + 1) * 128],
                            in_=dram[bsm_name][j])
            w_t[j] = b.sw_slot[j % 2].inc(i, 16)

        @blk.sync
        def _(sy, j=j):
            if j >= 2:
                b.va.wait(sy)
            i = sy.dma_start(out=b.xf[:, (j % 2) * T:(j % 2 + 1) * T],
                             in_=xsrc[j * 128:(j + 1) * 128, :])
            b.sx_slot[j % 2].inc(i, 16)
        sx_tok = b.sx_slot[j % 2].n

        @blk.tensor
        def _(t, j=j):
            b.sw_slot[j % 2].wait(t, w_t[j])
            if j == 0:
                b.va.wait(t)
            b.ac.wait(t) if j < 2 else b.ac.wait(t)
            ps = [b.psP0, b.psP1][j % 2]
            for half in range(2):
                last = t.matmul(ps[:, half * 512:(half + 1) * 512],
                                b.wsm[:, (j % 2) * 128:(j % 2 + 1) * 128],
                                b.qh_sb[:, half * 512:half * 512 + 512],
                                start=True, stop=True, skip_group_check=True)
            pe_t[j] = b.pe.inc(last)

        @blk.scalar
        def _(s, j=j):
            b.pe.wait(s, pe_t[j])
            b.va.wait(s)
            i = s.copy(b.s1[:], [b.psP0, b.psP1][j % 2][:])
            b.ac.inc(i)

        @blk.vector
        def _(v, j=j, sx_tok=sx_tok):
            b.ac.wait(v)
            b.sx_slot[j % 2].wait(v, sx_tok)
            i = v.tensor_tensor(b.s2[:], b.s1[:], b.xf[:, (j % 2) * T:(j % 2 + 1) * T], OP.add)
            b.va.inc(i)

        @blk.sync
        def _(sy, j=j):
            b.va.wait(sy)
            i = sy.dma_start(out=xmid[j * 128:(j + 1) * 128, :], in_=b.s2[:])
            b.st.inc(i, 16)


def emit_masked_mm(b, blk, dram, xdst):
    xmid = dram["xmid"]
    w_t, em_t, pe_t, va_t = {}, {}, {}, {}
    for j in range(NCH):
        @blk.gpsimd
        def _(g, j=j):
            if j >= 2:
                b.pe.wait(g, pe_t[j - 2])
            else:
                b.pe.wait(g)
            sl = b.sw_slot[j % 2]
            i = g.dma_start(out=b.wst[:, (j % 2) * D:(j % 2 + 1) * D], in_=dram["l1_w"][j])
            w_t[j] = sl.inc(i, 16)
            i = g.dma_start(
                out=b.kv_sb[:, (j % 2) * 4096:(j % 2 + 1) * 4096].rearrange(
                    "p (kc n) -> p kc n", kc=NCH),
                in_=dram["em_scr"][:, :, j * 128:(j + 1) * 128].rearrange("kc p n -> p kc n"))
            em_t[j] = sl.inc(i, 16)

        @blk.vector
        def _(v, j=j):
            b.sw_slot[j % 2].wait(v, em_t[j])
            if j >= 2:
                b.pe.wait(v, pe_t[j - 2])
            i = v.tensor_tensor(b.wst[:, (j % 2) * D:(j % 2 + 1) * D],
                                b.wst[:, (j % 2) * D:(j % 2 + 1) * D],
                                b.kv_sb[:, (j % 2) * 4096:(j % 2 + 1) * 4096], OP.mult)
            va_t[j] = b.va.inc(i)

        @blk.sync
        def _(sy, j=j):
            if j >= 2:
                b.va.wait(sy)
            i = sy.dma_start(out=b.xf[:, (j % 2) * T:(j % 2 + 1) * T],
                             in_=xmid[j * 128:(j + 1) * 128, :])
            b.sx_slot[j % 2].inc(i, 16)
        sx_tok = b.sx_slot[j % 2].n

        @blk.tensor
        def _(t, j=j):
            b.va.wait(t, va_t[j])
            b.ac.wait(t)
            ps = [b.psP0, b.psP1][j % 2]
            w = b.wst[:, (j % 2) * D:(j % 2 + 1) * D]
            for half in range(2):
                for dc in range(NCH):
                    last = t.matmul(ps[:, half * 512:(half + 1) * 512],
                                    w[:, dc * 128:(dc + 1) * 128],
                                    xn_rhs(b, dc, half),
                                    start=(dc == 0), stop=(dc == NCH - 1),
                                    skip_group_check=True)
            pe_t[j] = b.pe.inc(last)

        @blk.scalar
        def _(s, j=j):
            b.pe.wait(s, pe_t[j])
            b.va.wait(s)
            i = s.copy(b.s1[:], [b.psP0, b.psP1][j % 2][:])
            b.ac.inc(i)

        @blk.vector
        def _(v, j=j, sx_tok=sx_tok):
            b.ac.wait(v)
            b.sx_slot[j % 2].wait(v, sx_tok)
            i = v.tensor_tensor(b.s2[:], b.s1[:], b.xf[:, (j % 2) * T:(j % 2 + 1) * T], OP.add)
            b.va.inc(i)

        @blk.sync
        def _(sy, j=j):
            b.va.wait(sy)
            i = sy.dma_start(out=xdst[j * 128:(j + 1) * 128, :], in_=b.s2[:])
            b.st.inc(i, 16)


def emit_block1(b, blk, dram, xsrc, xdst, dbg):
    ph = 99
    if dbg:
        m = _re.search(r"ph(\d+)", dbg)
        if m:
            ph = int(m.group(1))
    emit_bm01(b, blk, dram)
    if ph < 1:
        return
    emit_emgen(b, blk, dram)
    if ph < 2:
        return
    emit_router(b, blk, dram, "l1_wr", None, 16.0, br_name="l1_br")
    emit_lora_z(b, blk, dram, "l1_A", 1.0)
    if ph < 3:
        return
    emit_ffn_lora_only(b, blk, dram, "l1_B", xsrc)
    if ph < 4:
        if dbg and "mid" in dbg:
            dbg_dump_dram(b, blk, dram, dram["xmid"])
        return
    emit_masked_mm(b, blk, dram, xdst)




def _vf(b, v, i):
    b.va.inc(i)
    b.va.wait(v)


def emit_ffn_l2(b, blk, dram, xsrc):
    ZR, AR, UR, VR, MR, SR, LR, HR, XR = 0, 128, 256, 288, 320, 352, 416, 448, 576

    pe_x = []
    for ch in range(NCH):
        @blk.scalar
        def _(s, ch=ch):
            b.va.wait(s)
            if ch == 0:
                b.pl.wait(s)
            i = s.activation(b.s2[:], b.xn_sb[:, ch * T:(ch + 1) * T], AF.Square)
            b.ac.inc(i)

        @blk.vector
        def _(v, ch=ch):
            b.ac.wait(v)
            if ch == 0:
                b.pe.wait(v)
            else:
                b.pe.wait(v, pe_x[ch - 1])
            i = v.tensor_copy(b.pT[:], b.s2[:])
            b.va.inc(i)

        @blk.tensor
        def _(t, ch=ch):
            b.va.wait(t)
            for half in range(2):
                last = t.matmul(b.psQ[0:1, half * 512:(half + 1) * 512], onesc(b),
                                b.pT[:, half * 512:(half + 1) * 512],
                                start=(ch == 0), stop=(ch == NCH - 1), skip_group_check=True)
            pe_x.append(b.pe.inc(last))
    emit_rstd_bcast(b, blk, 1.1920929e-07)

    @blk.gpsimd
    def _(g):
        b.va.wait(g)
        b.pe.wait(g)
        b.sws.wait(g)
        i = g.dma_start(out=b.sb16[:, 0:NCH * 24], in_=dram["l2_wrpp"])
        b.sws.inc(i, 16)
        b.sb0.wait(g)
        i = g.dma_start(out=b.btile[0:24, 32:33], in_=dram["l2_b24"])
        b.sb0.inc(i, 16)
        i = g.dma_start(out=b.btile[0:1, 33:34], in_=dram["l2_gate"])
        b.sb0.inc(i, 16)

    @blk.tensor
    def _(t):
        b.sws.wait(t)
        b.va.wait(t)
        for half in range(2):
            dst = b.psM[0:24, 0:512] if half == 0 else b.psV[0:24, 0:512]
            for dc in range(NCH):
                last = t.matmul(dst, b.sb16[:, dc * 24:(dc + 1) * 24], xn_rhs(b, dc, half),
                                start=(dc == 0), stop=(dc == NCH - 1), skip_group_check=True)
        b.pe.inc(last)

    @blk.vector
    def _(v):
        b.pe.wait(v)
        b.sb0.wait(v)
        v.tensor_tensor(b.s1[0:24, 0:512], b.psM[0:24, 0:512], b.bc2[0:24, 0:512], OP.mult)
        i = v.tensor_tensor(b.s1[0:24, 512:1024], b.psV[0:24, 0:512], b.bc2[0:24, 512:1024], OP.mult)
        _vf(b, v, i)
        i = v.tensor_scalar(b.s1[0:24, :], b.s1[0:24, :], b.btile[0:24, 32:33], None, OP.add)
        b.va.inc(i)

    for tt in range(8):
        @blk.tensor
        def _(t, tt=tt):
            b.va.wait(t)
            i = t.transpose(b.psQ[:, 0:24], b.s1[0:24, tt * 128:(tt + 1) * 128],
                            ident_f32(b)[0:24, 0:24])
            b.pe.inc(i)

        @blk.vector
        def _(v, tt=tt):
            b.pe.wait(v)
            v.tensor_scalar(b.s2[:, ZR + tt * 16:ZR + (tt + 1) * 16], b.psQ[:, 0:16],
                            10.0, None, OP.mult)
            v.tensor_copy(b.tm[:, tt * 4:(tt + 1) * 4], b.psQ[:, 16:20])
            i = v.tensor_copy(b.tm[:, 32 + tt * 4:32 + (tt + 1) * 4], b.psQ[:, 20:24])
            b.va.inc(i)

    Z3i = lambda off, j: b.s2[:, off + j:off + 113 + j].rearrange(
        "p (a r) -> p a r", a=8)[:, :, 0:1].squeeze(2) if False else None

    def ap_ti(off, j):
        return b.s2[:, off + j:off + j + 126].rearrange("p c -> p c").unsqueeze(0) if False else None

    @blk.vector
    def _(v):
        b.va.wait(v)
        i = v.memset(b.s2[:, VR:VR + 32], 0.0)
        b.va.inc(i)

    def sub_bcast_j(v, dst_off, src_off, row_off):
        for j in range(4):
            sl = [None]
            i = v.tensor_tensor(
                b.s2[:, dst_off:dst_off + 128].rearrange("p (a i j) -> p a i j", a=8, i=4)[:, :, :, j],
                b.s2[:, src_off:src_off + 128].rearrange("p (a i j) -> p a i j", a=8, i=4)[:, :, :, j],
                b.s2[:, row_off:row_off + 32].rearrange("p (a i) -> p a i", a=8),
                OP.subtract)
        _vf(b, v, i)

    def sub_bcast_i(v, dst_off, src_off, row_off):
        for i2 in range(4):
            i = v.tensor_tensor(
                b.s2[:, dst_off:dst_off + 128].rearrange("p (a i j) -> p a i j", a=8, i=4)[:, :, i2, :],
                b.s2[:, src_off:src_off + 128].rearrange("p (a i j) -> p a i j", a=8, i=4)[:, :, i2, :],
                b.s2[:, row_off:row_off + 32].rearrange("p (a j) -> p a j", a=8),
                OP.subtract)
        _vf(b, v, i)

    for it in range(50):
        @blk.vector
        def _(v, it=it):
            if it == 0:
                b.va.wait(v)
            else:
                b.ac.wait(v)
            sub_bcast_i(v, AR, ZR, VR)
            i = v.tensor_reduce(b.s2[:, MR:MR + 32],
                                b.s2[:, AR:AR + 128].rearrange("p (a i j) -> p (a i) j", a=8, i=4),
                                axis=AX, op=OP.max)
            _vf(b, v, i)
            sub_bcast_j(v, AR, AR, MR)
            b.va.inc(v.engine_nop()) if False else None

        @blk.scalar
        def _(s, it=it):
            b.va.wait(s)
            i = s.activation(b.s2[:, AR:AR + 128], b.s2[:, AR:AR + 128], AF.Exp)
            b.ac.inc(i)

        @blk.vector
        def _(v, it=it):
            b.ac.wait(v)
            i = v.tensor_reduce(b.s2[:, SR:SR + 32],
                                b.s2[:, AR:AR + 128].rearrange("p (a i j) -> p (a i) j", a=8, i=4),
                                axis=AX, op=OP.add)
            b.va.inc(i)

        @blk.scalar
        def _(s, it=it):
            b.ac.wait(s)
            b.va.wait(s)
            i = s.activation(b.s2[:, LR:LR + 32], b.s2[:, SR:SR + 32], AF.Ln)
            b.ac.inc(i)

        @blk.vector
        def _(v, it=it):
            b.ac.wait(v)
            i = v.tensor_tensor(b.s2[:, UR:UR + 32], b.s2[:, MR:MR + 32],
                                b.s2[:, LR:LR + 32], OP.add)
            _vf(b, v, i)
            sub_bcast_j(v, AR, ZR, UR)
            i = v.tensor_reduce(b.s2[:, MR:MR + 32].rearrange("p (a j) -> p a j", a=8),
                                b.s2[:, AR:AR + 128].rearrange("p (a i j) -> p a j i", a=8, i=4),
                                axis=AX, op=OP.max)
            _vf(b, v, i)
            sub_bcast_i(v, AR, AR, MR)
            b.va.inc(i) if False else None

        @blk.scalar
        def _(s, it=it):
            b.va.wait(s)
            i = s.activation(b.s2[:, AR:AR + 128], b.s2[:, AR:AR + 128], AF.Exp)
            b.ac.inc(i)

        @blk.vector
        def _(v, it=it):
            b.ac.wait(v)
            i = v.tensor_reduce(b.s2[:, SR:SR + 32].rearrange("p (a j) -> p a j", a=8),
                                b.s2[:, AR:AR + 128].rearrange("p (a i j) -> p a j i", a=8, i=4),
                                axis=AX, op=OP.add)
            b.va.inc(i)

        @blk.scalar
        def _(s, it=it):
            b.ac.wait(s)
            b.va.wait(s)
            i = s.activation(b.s2[:, LR:LR + 32], b.s2[:, SR:SR + 32], AF.Ln)
            b.ac.inc(i)

        @blk.vector
        def _(v, it=it):
            b.ac.wait(v)
            i = v.tensor_tensor(b.s2[:, VR:VR + 32], b.s2[:, MR:MR + 32],
                                b.s2[:, LR:LR + 32], OP.add)
            b.va.inc(i)

    @blk.vector
    def _(v):
        b.va.wait(v)
        sub_bcast_i(v, HR, ZR, VR)
        sub_bcast_j(v, HR, HR, UR)
        i = v.memset(b.s2[0:1, XR:XR + 1], 0.0)
        b.va.inc(i)

    @blk.scalar
    def _(s):
        b.va.wait(s)
        i = s.activation(b.s2[:, HR:HR + 128], b.s2[:, HR:HR + 128], AF.Exp)
        b.ac.inc(i)
        i = s.activation(b.s1[0:1, 0:1], b.btile[0:1, 33:34], AF.Sigmoid)
        b.ac.inc(i)

    @blk.vector
    def _(v):
        b.ac.wait(v)
        i = v.tensor_copy(b.sb16[0:1, 0:1], b.s1[0:1, 0:1])
        b.va.inc(i)

    @blk.tensor
    def _(t):
        b.va.wait(t)
        i = t.matmul(b.psM[:, 0:1], onesr(b), b.sb16[0:1, 0:1], start=True, stop=True,
                     skip_group_check=True)
        b.pe.inc(i)

    @blk.vector
    def _(v):
        b.pe.wait(v)
        i = v.tensor_copy(b.btile[:, 34:35], b.psM[:, 0:1])
        _vf(b, v, i)
        i = v.tensor_scalar(b.btile[:, 35:36], b.btile[:, 34:35], -1.0, 1.0, OP.mult, OP.add)
        _vf(b, v, i)
        i = v.tensor_scalar(b.s2[:, HR:HR + 128], b.s2[:, HR:HR + 128],
                            b.btile[:, 34:35], None, OP.mult)
        _vf(b, v, i)
        dg = b.s2[:, HR:HR + 128].rearrange("p (a i) -> p a i", a=8, i=16)[:, :, 0:16:5]
        i = v.tensor_scalar(dg, dg, b.btile[:, 35:36], None, OP.add)
        b.va.inc(i)

    for which, OFF in [(0, 0), (1, 32)]:
        @blk.vector
        def _(v, OFF=OFF):
            b.va.wait(v)
            i = v.tensor_reduce(b.s2[:, XR:XR + 8],
                                b.tm[:, OFF:OFF + 32].rearrange("p (a n) -> p a n", a=8),
                                axis=AX, op=OP.max)
            _vf(b, v, i)
            for n in range(4):
                i = v.tensor_tensor(
                    b.tm2[:, OFF:OFF + 32].rearrange("p (a n) -> p a n", a=8)[:, :, n],
                    b.tm[:, OFF:OFF + 32].rearrange("p (a n) -> p a n", a=8)[:, :, n],
                    b.s2[:, XR:XR + 8], OP.subtract)
            b.va.inc(i)

        @blk.scalar
        def _(s, OFF=OFF):
            b.va.wait(s)
            i = s.activation(b.tm2[:, OFF:OFF + 32], b.tm2[:, OFF:OFF + 32], AF.Exp)
            b.ac.inc(i)

        @blk.vector
        def _(v, OFF=OFF):
            b.ac.wait(v)
            i = v.tensor_reduce(b.s2[:, XR + 8:XR + 16],
                                b.tm2[:, OFF:OFF + 32].rearrange("p (a n) -> p a n", a=8),
                                axis=AX, op=OP.add)
            _vf(b, v, i)
            i = v.reciprocal(b.s2[:, XR + 8:XR + 16], b.s2[:, XR + 8:XR + 16])
            _vf(b, v, i)
            for n in range(4):
                i = v.tensor_tensor(
                    b.tm2[:, OFF:OFF + 32].rearrange("p (a n) -> p a n", a=8)[:, :, n],
                    b.tm2[:, OFF:OFF + 32].rearrange("p (a n) -> p a n", a=8)[:, :, n],
                    b.s2[:, XR + 8:XR + 16], OP.mult)
            b.va.inc(i)

    for src_ap, nrow, dst0 in [
            (lambda tt: b.s2[:, HR + tt * 16:HR + (tt + 1) * 16], 16, 0),
            (lambda tt: b.tm2[:, tt * 4:(tt + 1) * 4], 4, 16),
            (lambda tt: b.tm2[:, 32 + tt * 4:32 + (tt + 1) * 4], 4, 20)]:
        for tt in range(8):
            @blk.tensor
            def _(t, tt=tt, src_ap=src_ap, nrow=nrow):
                b.va.wait(t)
                i = t.transpose(b.psQ[0:nrow, 0:128], src_ap(tt), ident_f32(b))
                b.pe.inc(i)

            @blk.vector
            def _(v, tt=tt, nrow=nrow):
                b.pe.wait(v)
                if tt == 0:
                    b.st.wait(v)
                i = v.tensor_copy(b.s1[0:nrow, tt * 128:(tt + 1) * 128],
                                  b.psQ[0:nrow, 0:128])
                b.va.inc(i)

        @blk.sync
        def _(sy, nrow=nrow, dst0=dst0):
            b.va.wait(sy)
            i = sy.dma_start(out=dram["l2_hrows"][dst0:dst0 + nrow, :], in_=b.s1[0:nrow, :])
            b.st.inc(i, 16)

    def bcast_hrow(idx, slot):
        @blk.sync
        def _(sy, idx=idx):
            b.st.wait(sy)
            b.sgs.wait(sy)
            b.va.wait(sy)
            i = sy.dma_start(out=row(b, R_MEAN, 1024), in_=dram["l2_hrows"][idx:idx + 1, :])
            b.sgs.inc(i, 16)

        @blk.vector
        def _(v, idx=idx):
            b.sgs.wait(v)
            b.pe.wait(v)
            i = v.tensor_copy(b.sb16[0:1, 0:1024], row(b, R_MEAN, 1024))
            b.va.inc(i)

        @blk.tensor
        def _(t, idx=idx):
            b.va.wait(t)
            b.ac.wait(t)
            for half in range(2):
                last = t.matmul(b.psP1[:, half * 512:(half + 1) * 512], onesr(b),
                                b.sb16[0:1, half * 512:(half + 1) * 512], start=True,
                                stop=True, skip_group_check=True)
            b.pe.inc(last)

        @blk.vector
        def _(v, idx=idx, slot=slot):
            b.pe.wait(v)
            i = v.tensor_copy(b.kv_sb[:, slot * 1024:(slot + 1) * 1024], b.psP1[:, 0:1024])
            b.va.inc(i)

    for n in range(4):
        bcast_hrow(16 + n, 0)

        @blk.vector
        def _(v, n=n):
            b.va.wait(v)
            for c in range(8):
                i = v.tensor_tensor(b.pT[:], b.xn_sb[:, (n * 8 + c) * T:(n * 8 + c + 1) * T],
                                    b.kv_sb[:, 0:1024], OP.mult)
                if n == 0:
                    i = v.tensor_copy(b.qk_sb[:, c * T:(c + 1) * T], b.pT[:])
                else:
                    i = v.tensor_tensor(b.qk_sb[:, c * T:(c + 1) * T],
                                        b.qk_sb[:, c * T:(c + 1) * T], b.pT[:], OP.add)
            b.va.inc(i)

    w_t, pe_t = {}, {}
    for j in range(8):
        @blk.gpsimd
        def _(g, j=j):
            b.pe.wait(g)
            i = g.dma_start(out=b.wst[:, (j % 2) * D:(j % 2) * D + 1024], in_=dram["l2_wbr"][j])
            w_t[j] = b.sw_slot[j % 2].inc(i, 16)

        @blk.tensor
        def _(t, j=j):
            b.sw_slot[j % 2].wait(t, w_t[j])
            if j == 0:
                b.va.wait(t)
            b.ac.wait(t) if j < 2 else b.ac.wait(t)
            ps = [b.psP0, b.psP1][j % 2]
            for half in range(2):
                for dc in range(8):
                    last = t.matmul(ps[:, half * 512:(half + 1) * 512],
                                    b.wst[:, (j % 2) * D + dc * 128:(j % 2) * D + (dc + 1) * 128],
                                    b.qk_sb[:, dc * T + half * 512:dc * T + half * 512 + 512],
                                    start=(dc == 0), stop=(dc == 7), skip_group_check=True)
            pe_t[j] = b.pe.inc(last)

        @blk.scalar
        def _(s, j=j):
            b.pe.wait(s, pe_t[j])
            i = s.copy(b.qk_sb[:, (8 + j) * T:(9 + j) * T], [b.psP0, b.psP1][j % 2][:])
            b.ac.inc(i)

    for n in range(4):
        for k5, idx in enumerate([n * 4 + 0, n * 4 + 1, n * 4 + 2, n * 4 + 3, 20 + n]):
            bcast_hrow(idx, k5)
        for c in range(8):
            @blk.vector
            def _(v, n=n, c=c):
                b.va.wait(v)
                b.ac.wait(v)
                v.tensor_tensor(b.pT[:], b.xn_sb[:, c * T:(c + 1) * T],
                                b.kv_sb[:, 0:1024], OP.mult)
                for j2 in range(1, 4):
                    v.tensor_tensor(b.sb16[:], b.xn_sb[:, (j2 * 8 + c) * T:(j2 * 8 + c + 1) * T],
                                    b.kv_sb[:, j2 * 1024:(j2 + 1) * 1024], OP.mult)
                    v.tensor_tensor(b.pT[:], b.pT[:], b.sb16[:], OP.add)
                v.tensor_tensor(b.sb16[:], b.qk_sb[:, (8 + c) * T:(9 + c) * T],
                                b.kv_sb[:, 4 * 1024:5 * 1024], OP.mult)
                i = v.tensor_tensor(b.pT[:], b.pT[:], b.sb16[:], OP.add)
                b.va.inc(i)

            @blk.sync
            def _(sy, n=n, c=c):
                nc2 = n * 8 + c
                if True:
                    b.va.wait(sy)
                i = sy.dma_start(out=b.xf[:, (nc2 % 2) * T:(nc2 % 2 + 1) * T],
                                 in_=xsrc[nc2 * 128:(nc2 + 1) * 128, :])
                b.sx_slot[nc2 % 2].inc(i, 16)

            @blk.vector
            def _(v, n=n, c=c):
                nc2 = n * 8 + c
                b.sx_slot[nc2 % 2].wait(v)
                i = v.tensor_tensor(b.s1[:], b.pT[:], b.xf[:, (nc2 % 2) * T:(nc2 % 2 + 1) * T],
                                    OP.add)
                b.va.inc(i)

            @blk.sync
            def _(sy, n=n, c=c):
                nc2 = n * 8 + c
                b.va.wait(sy)
                i = sy.dma_start(out=dram["xmid"][nc2 * 128:(nc2 + 1) * 128, :], in_=b.s1[:])
                b.st.inc(i, 16)


def emit_q_l2(b, blk, dram):
    @blk.gpsimd
    def _(g):
        b.sb0.wait(g)
        i = g.dma_start(out=b.btile[:, 0:32], in_=dram["l2_bq"]); b.sb0.inc(i, 16)
        i = g.dma_start(out=b.btile[:, 64:96], in_=dram["l2_gq"]); b.sb0.inc(i, 16)

    def evac_q(idx, j, ps, pe_t):
        @blk.scalar
        def _(s, idx=idx, j=j, ps=ps):
            b.pe.wait(s, pe_t[idx])
            b.va.wait(s)
            b.sb0.wait(s)
            s.activation(b.s1[:], ps[:], AF.Identity, bias=b.btile[:, j:j + 1])
            i = s.activation(b.s2[:], b.s1[:], AF.Square)
            b.ac.inc(i)

        @blk.vector
        def _(v, idx=idx, j=j):
            b.ac.wait(v)
            b.pe.wait(v)
            v.tensor_copy(b.qk_sb[:, j * T:(j + 1) * T], b.s1[:])
            i = v.tensor_copy(b.pT[:], b.s2[:])
            b.va.inc(i)

        @blk.tensor
        def _(t, idx=idx, j=j):
            b.va.wait(t)
            for half in range(2):
                last = t.matmul(b.psQ[0:1, half * 512:(half + 1) * 512], onesc(b),
                                b.pT[:, half * 512:(half + 1) * 512],
                                start=(j == 0), stop=(j == NCH - 1), skip_group_check=True)
            b.pe.inc(last)
        return (b.va, b.va.n)

    emit_proj(b, blk, dram["l2_wq"], list(range(NCH)),
              lambda dc, half: xn_rhs(b, dc, half), evac_q)
    emit_rstd_bcast(b, blk, 1e-6)
    for j in range(NCH):
        @blk.vector
        def _(v, j=j):
            b.va.wait(v)
            v.tensor_tensor(b.s1[:], b.qk_sb[:, j * T:(j + 1) * T], b.bc2[:], OP.mult)
            i = v.tensor_scalar(b.qk_sb[:, j * T:(j + 1) * T], b.s1[:],
                                b.btile[:, 64 + j:65 + j], None, OP.mult)
            b.va.inc(i)


def emit_block2(b, blk, dram, xsrc, xdst, dbg):
    ph = 99
    if dbg:
        m = _re.search(r"ph(\d+)", dbg)
        if m:
            ph = int(m.group(1))

    @blk.gpsimd
    def _(g):
        b.sb0.wait(g)
        i = g.dma_start(out=b.btile[:, 0:32], in_=dram["l2_bk"]); b.sb0.inc(i, 16)
        i = g.dma_start(out=b.btile[:, 64:96], in_=dram["l2_gk"]); b.sb0.inc(i, 16)
        i = g.dma_start(out=b.btile[:, 32:64], in_=dram["l2_bv"]); b.sb0.inc(i, 16)

    if ph < 1:
        return
    cc_end = emit_kv_and_gather(
        b, blk, dram, "g2", 32, dram["l2_wk"], dram["l2_wv"],
        lambda j: b.btile[:, j:j + 1], lambda j: b.btile[:, 32 + j:33 + j],
        l2_gain=lambda j: b.btile[:, 64 + j:65 + j])
    if ph < 2:
        return
    emit_ffn_l2(b, blk, dram, xsrc)
    if ph < 3:
        if dbg and "mid" in dbg:
            dbg_dump_dram(b, blk, dram, dram["xmid"])
        return
    emit_q_l2(b, blk, dram)
    if ph < 4:
        return
    emit_gs_readback(b, blk, dram, cc_end - 2)
    def l2_slots(i):
        base = [b.pT, b.sb16, b.qh_sb][i // 2]
        return base[:, (i % 2) * 512:(i % 2 + 1) * 512]
    cfg = dict(pfx="g2", nkv=32, group=1, q_mode="buf", out="l2_wo", bo_name="l2_bo",
               expslots=l2_slots, cc_k=cc_end - 1, cc_v=cc_end)
    emit_attention(b, blk, dram, cfg, xdst)




def emit_cbound(b, blk, qsrc, kv, h):
    @blk.scalar
    def _(s, h=h):
        b.va.wait(s)
        b.pl.wait(s)
        i = s.activation(b.s2[:], qsrc, AF.Square)
        b.ac.inc(i)

    @blk.vector
    def _(v, h=h):
        b.ac.wait(v)
        b.pe.wait(v)
        i = v.tensor_copy(b.sb16[:], b.s2[:])
        b.va.inc(i)

    @blk.tensor
    def _(t, h=h):
        b.va.wait(t)
        t.matmul(b.psM[0:1, 0:512], onesc(b), b.sb16[:, 0:512], start=True, stop=True,
                 skip_group_check=True)
        i = t.matmul(b.psV[0:1, 0:512], onesc(b), b.sb16[:, 512:1024], start=True,
                     stop=True, skip_group_check=True)
        b.pe.inc(i)

    @blk.vector
    def _(v, h=h, kv=kv):
        b.pe.wait(v)
        v.reduce_max(row(b, R_C, 1), b.psM[0:1, 0:512], axis=AX)
        i = v.reduce_max(row(b, R_C + 1, 1), b.psV[0:1, 0:512], axis=AX)
        b.va.inc(i); b.va.wait(v)
        i = v.tensor_tensor(row(b, R_C + 2, 1), row(b, R_C, 1), row(b, R_C + 1, 1), OP.max)
        b.va.inc(i); b.va.wait(v)
        i = v.tensor_tensor(row(b, R_C + 3, 1), row(b, R_C + 2, 1),
                            row(b, R_GK2 + kv, 1), OP.mult)
        b.va.inc(i)

    @blk.scalar
    def _(s, h=h):
        b.va.wait(s)
        i = s.activation(row(b, R_C + 4, 1), row(b, R_C + 3, 1), AF.Sqrt)
        b.ac.inc(i)

    @blk.vector
    def _(v, h=h):
        b.ac.wait(v)
        i = v.tensor_scalar(row(b, R_C + 5, 1), row(b, R_C + 4, 1), -ISQ, None, OP.mult)
        b.va.inc(i); b.va.wait(v)
        i = v.tensor_copy(b.sb16[0:1, 0:1], row(b, R_C + 5, 1))
        b.va.inc(i)

    @blk.tensor
    def _(t, h=h):
        b.va.wait(t)
        i = t.matmul(b.psM[:, 0:1], onesr(b), b.sb16[0:1, 0:1], start=True, stop=True,
                     skip_group_check=True)
        b.pe.inc(i)

    @blk.vector
    def _(v, h=h):
        b.pe.wait(v)
        i = v.tensor_copy(b.biasC[:, h % 16:h % 16 + 1], b.psM[:, 0:1])
        b.va.inc(i)
    return b.va.n


def emit_router_tail(b, blk, topk, scale_out):
    for tt in range(8):
        @blk.vector
        def _(v, tt=tt):
            b.va.wait(v)
            i = v.reduce_max(b.tm2[:, tt:tt + 1], b.tm[:, tt * 8:(tt + 1) * 8], axis=AX)
            b.va.inc(i); b.va.wait(v)
            i = v.tensor_scalar(b.tm2[:, 8 + tt:9 + tt], b.tm2[:, tt:tt + 1], -1.0, None, OP.mult)
            b.va.inc(i)

        @blk.scalar
        def _(s, tt=tt):
            b.va.wait(s)
            i = s.activation(b.tm3[:, tt * 8:(tt + 1) * 8], b.tm[:, tt * 8:(tt + 1) * 8],
                             AF.Identity, bias=b.tm2[:, 8 + tt:9 + tt])
            b.ac.inc(i)

        @blk.scalar
        def _(s, tt=tt):
            i = s.activation(b.tm3[:, 64 + tt * 8:72 + tt * 8],
                             b.tm3[:, tt * 8:(tt + 1) * 8], AF.Exp)
            b.ac.inc(i)

        @blk.vector
        def _(v, tt=tt):
            b.ac.wait(v)
            i = v.reduce_sum(b.tm2[:, 40 + tt:41 + tt], b.tm3[:, 64 + tt * 8:72 + tt * 8], axis=AX)
            b.va.inc(i); b.va.wait(v)
            i = v.reciprocal(b.tm2[:, 48 + tt:49 + tt], b.tm2[:, 40 + tt:41 + tt])
            b.va.inc(i)

        @blk.gpsimd
        def _(g, tt=tt):
            b.va.wait(g)
            b.ac.wait(g)
            i = g.tensor_scalar(b.tm3[:, 128 + tt * 8:136 + tt * 8],
                                b.tm3[:, 64 + tt * 8:72 + tt * 8],
                                b.tm2[:, 48 + tt:49 + tt], scale_out, OP.mult, OP.mult)
            b.pl.inc(i)

    @blk.tensor
    def _(t):
        b.pl.wait(t)
        for tt in range(8):
            last = t.transpose(b.psP0[0:8, tt * 128:(tt + 1) * 128],
                               b.tm3[:, 128 + tt * 8:136 + tt * 8], ident_f32(b))
        b.pe.inc(last)

    @blk.vector
    def _(v):
        b.pe.wait(v)
        i = v.tensor_copy(b.pT[0:8, 0:1024], b.psP0[0:8, 0:1024])
        b.va.inc(i)


def emit_l3_router(b, blk, dram):
    w_t = {}
    for c in range(2):
        @blk.gpsimd
        def _(g, c=c):
            b.pe.wait(g)
            i = g.dma_start(out=b.wst[:, 0:D], in_=dram["l3_wqk"][c])
            b.sw_slot[0].inc(i, 16)
            i = g.dma_start(out=b.wst[:, D:2 * D], in_=dram["l3_wqk"][2 + c])
            w_t[c] = b.sw_slot[0].inc(i, 16)

        @blk.tensor
        def _(t, c=c):
            b.sw_slot[0].wait(t, w_t[c])
            b.va.wait(t)
            b.ac.wait(t)
            for sl in range(2):
                ps = [b.psP0, b.psP1][sl]
                for half in range(2):
                    for dc in range(NCH):
                        last = t.matmul(ps[:, half * 512:(half + 1) * 512],
                                        b.wst[:, sl * D + dc * 128:sl * D + (dc + 1) * 128],
                                        xn_rhs(b, dc, half),
                                        start=(dc == 0), stop=(dc == NCH - 1),
                                        skip_group_check=True)
            b.pe.inc(last)

        @blk.scalar
        def _(s, c=c):
            b.pe.wait(s)
            b.va.wait(s)
            s.copy(b.s1[:], b.psP0[:])
            i = s.copy(b.s2[:], b.psP1[:])
            b.ac.inc(i)

        @blk.vector
        def _(v, c=c):
            b.ac.wait(v)
            i = v.tensor_tensor(b.qk_sb[:, (18 + c) * T:(19 + c) * T], b.s1[:], b.s2[:], OP.mult)
            b.va.inc(i)

    @blk.tensor
    def _(t):
        b.va.wait(t)
        for c in range(2):
            for tt in range(8):
                last = t.matmul(b.psQ[:, tt * 8 + 4 * c:tt * 8 + 4 * c + 4],
                                b.qk_sb[:, (18 + c) * T + tt * 128:(18 + c) * T + (tt + 1) * 128],
                                b.cstb[:, CB_G32:CB_G32 + 4],
                                start=True, stop=True, skip_group_check=True)
        b.pe.inc(last)

    @blk.vector
    def _(v):
        b.pe.wait(v)
        i = v.tensor_scalar(b.tm[:, 0:64], b.psQ[:, 0:64], 1.0 / math.sqrt(32.0), None, OP.mult)
        b.va.inc(i)

    emit_router_tail(b, blk, None, 1.0)


def emit_attention_l3(b, blk, dram, cfg, xdst):
    nkv, grp = cfg["nkv"], cfg["group"]
    kout, vout = dram[cfg["pfx"] + "_kout"], dram[cfg["pfx"] + "_vout"]
    FAR = b.cbig[:, 2048:2176]
    DIAG = b.cbig[:, 2176:2304]

    cidreg = [None]

    @blk.sync
    def _(sy):
        cidreg[0] = sy.partition_id() % 4

    for kv in range(nkv):
        @blk.sync
        def _(sy, kv=kv):
            b.cc.wait(sy, cfg["cc_v"])
            b.sb0.wait(sy)
            if kv >= 1:
                b.pe.wait(sy)
            kr = kout.rearrange("(r d) t -> d r t", r=4)[kv * 128:(kv + 1) * 128]
            vr = vout.rearrange("(r h c p) d -> r h p c d", r=4, h=nkv, c=8)
            for body in sy.Switch(cidreg[0], 4):
                ra = max(0, body - 1)
                sa = slice(0, 512) if body == 0 else slice(512, 1024)
                i = sy.dma_start(out=b.kv_sb[:, 0:512], in_=kr[:, ra, sa])
                i.then_inc(b.skr.h, 16)
                i = sy.dma_start(out=b.kv_sb[:, 512:1536], in_=kr[:, body, :])
                i.then_inc(b.skr.h, 16)
                ca = slice(0, 4) if body == 0 else slice(4, 8)
                i = sy.dma_start(
                    out=b.kv_sb[:, 1536:2048].rearrange("p (c d) -> p c d", c=4),
                    in_=vr[ra, kv][:, ca, :])
                i.then_inc(b.skr.h, 16)
                i = sy.dma_start(
                    out=b.kv_sb[:, 2048:3072].rearrange("p (c d) -> p c d", c=8),
                    in_=vr[body, kv])
                i.then_inc(b.skr.h, 16)
            b.skr.n += 64
        kv_tok = b.skr.n

        for hh in range(grp):
            h = kv * grp + hh
            emit_attn_head_l3(b, blk, dram, cfg, h, kv, kv_tok, FAR, DIAG)

    emit_o_proj(b, blk, dram, cfg, xdst)


def emit_attn_head_l3(b, blk, dram, cfg, h, kv, kv_tok, FAR, DIAG):
    wq = dram[cfg["wq_name"]]

    @blk.gpsimd
    def _(g, h=h):
        b.pe.wait(g)
        i = g.dma_start(out=b.wst[:, (h % 2) * D:(h % 2 + 1) * D], in_=wq[h])
        b.sw_slot[h % 2].inc(i, 16)
    sw_tok = b.sw_slot[h % 2].n

    @blk.tensor
    def _(t, h=h, sw_tok=sw_tok):
        b.sw_slot[h % 2].wait(t, sw_tok)
        b.ac.wait(t)
        b.va.wait(t)
        for half in range(2):
            for dc in range(NCH):
                last = t.matmul(b.psP0[:, half * 512:(half + 1) * 512],
                                b.wst[:, (h % 2) * D + dc * 128:(h % 2) * D + (dc + 1) * 128],
                                xn_rhs(b, dc, half),
                                start=(dc == 0), stop=(dc == NCH - 1), skip_group_check=True)
        b.pe.inc(last)

    @blk.scalar
    def _(s, h=h):
        b.pe.wait(s)
        b.va.wait(s)
        i = s.copy(b.qh_sb[:], b.psP0[:])
        b.ac.inc(i)

    biasC_tok = emit_cbound(b, blk, b.qh_sb[:], kv, h)

    def l3_madd(qt, s5):
        if s5 == 4:
            return DIAG
        if qt < 4:
            return b.cbig[:, (qt * 4 + s5) * 128:(qt * 4 + s5 + 1) * 128]
        if s5 == 0:
            return FAR
        return None

    def l3_tiles(p):
        return [(ql * 5 + s5, 2 * p + ql, ql, s5, 2 * p + ql + s5)
                for ql in range(2) for s5 in range(5)]

    pe_p, va_p, ac_p = {}, {}, {}

    def l3_eslot(p, j):
        idx = (p % 2) * 10 + j
        if idx < 16:
            return b.s1[:].bitcast(BF16)[:, idx * 128:(idx + 1) * 128]
        return b.bc2[:].bitcast(BF16)[:, (idx - 16) * 128:(idx - 15) * 128]

    def l3_qkt(p):
        @blk.tensor
        def _(t, p=p, h=h):
            if p == 0:
                b.skr.wait(t, kv_tok)
            if p >= 2:
                b.ac.wait(t, ac_p[p - 2])
            for j, qt, ql, s5, ws in l3_tiles(p):
                i = t.matmul(b.psW[:, (p % 2) * 1536 + j * 128:(p % 2) * 1536 + (j + 1) * 128],
                             b.kv_sb[:, ws * 128:(ws + 1) * 128],
                             b.qh_sb[:, qt * 128:(qt + 1) * 128],
                             start=True, stop=True, skip_group_check=True)
            pe_p[p] = b.pe.inc(i)

    l3_qkt(0)
    for p in range(4):
        @blk.vector
        def _(v, p=p):
            b.pe.wait(v, pe_p[p])
            for j, qt, ql, s5, ws in l3_tiles(p):
                madd = l3_madd(qt, s5)
                if madd is None:
                    continue
                po = (p % 2) * 1536 + j * 128
                i = v.tensor_tensor(b.psW[:, po:po + 128], b.psW[:, po:po + 128],
                                    madd, OP.add)
            va_p[p] = b.va.inc(i)

        @blk.scalar
        def _(s, p=p, h=h):
            b.va.wait(s, va_p[p])
            if p == 0:
                b.va.wait(s, biasC_tok)
            for j, qt, ql, s5, ws in l3_tiles(p):
                po = (p % 2) * 1536 + j * 128
                s.activation(l3_eslot(p, j), b.psW[:, po:po + 128], AF.Exp,
                             bias=b.biasC[:, h % 16:h % 16 + 1], scale=ISQ)
            i = s.drain()
            ac_p[p] = b.ac.inc(i)

        if p + 1 < 4:
            l3_qkt(p + 1)

        @blk.tensor
        def _(t, p=p):
            b.ac.wait(t, ac_p[p])
            b.va.wait(t)
            for j, qt, ql, s5, ws in l3_tiles(p):
                eslot = l3_eslot(p, j)
                t.matmul(b.psM[0:1, ql * 128:(ql + 1) * 128], onesc(b), eslot,
                         start=(s5 == 0), stop=(s5 == 4), skip_group_check=True)
                i = t.matmul(b.psV[:, ql * 128:(ql + 1) * 128],
                             b.kv_sb[:, 1536 + ws * 128:1536 + (ws + 1) * 128], eslot,
                             start=(s5 == 0), stop=(s5 == 4), skip_group_check=True)
            b.pe.inc(i)

        @blk.vector
        def _(v, p=p, h=h):
            b.pe.wait(v)
            v.reciprocal(row(b, R_RECIP, 128), b.psM[0:1, 0:128])
            i = v.reciprocal(row(b, R_RECIP + 128, 128), b.psM[0:1, 128:256])
            b.va.inc(i); b.va.wait(v)
            i = v.tensor_copy(b.sb16[0:1, 0:256], row(b, R_RECIP, 256))
            b.va.inc(i)

        @blk.tensor
        def _(t, p=p):
            b.va.wait(t)
            t.matmul(b.psM[:, 0:128], onesr(b), b.sb16[0:1, 0:128], start=True,
                     stop=True, skip_group_check=True)
            i = t.matmul(b.psM[:, 128:256], onesr(b), b.sb16[0:1, 128:256], start=True,
                         stop=True, skip_group_check=True)
            b.pe.inc(i)

        @blk.vector
        def _(v, p=p, h=h):
            b.pe.wait(v)
            v.tensor_copy(b.bc1[:, 0:256], b.psM[:, 0:256])
            for ql in range(2):
                qt = 2 * p + ql
                i = v.tensor_tensor(b.qk_sb[:, h * T + qt * 128:h * T + (qt + 1) * 128],
                                    b.psV[:, ql * 128:(ql + 1) * 128],
                                    b.bc1[:, ql * 128:(ql + 1) * 128], OP.mult)
            b.va.inc(i)


def emit_block3(b, blk, dram, xsrc, xdst, dbg):
    ph = 99
    if dbg:
        m = _re.search(r"ph(\d+)", dbg)
        if m:
            ph = int(m.group(1))

    @blk.gpsimd
    def _(g):
        b.sb0.wait(g)
        i = g.dma_start(out=b.cbig[:, 0:2048], in_=dram["l3_me"]); b.sb0.inc(i, 16)
        i = g.dma_start(out=b.cbig[:, 2048:2304], in_=dram["l3_mfd"]); b.sb0.inc(i, 16)

    if ph < 1:
        return
    cc_end = emit_kv_and_gather(b, blk, dram, "g3", 8, dram["l3_wk"], dram["l3_wv"],
                                None, None, rope=False)
    if ph < 2:
        return
    emit_l3_router(b, blk, dram)
    emit_lora_z(b, blk, dram, "l3_A", 1.0)
    if ph < 3:
        return
    emit_ffn_base(b, blk, dram, "l3_w", "l3_b", "l3_B", xsrc)
    if ph < 4:
        if dbg and "mid" in dbg:
            dbg_dump_dram(b, blk, dram, dram["xmid"])
        return
    emit_gs_readback(b, blk, dram, cc_end - 2)
    cfg = dict(pfx="g3", nkv=8, group=4, wq_name="l3_wq", out="l3_wo", bo_name=None,
               cc_k=cc_end - 1, cc_v=cc_end)
    emit_attention_l3(b, blk, dram, cfg, xdst)



def _colchunk(w):
    din, dout = w.shape
    return np.ascontiguousarray(
        w.reshape(din // 128, 128, dout // 128, 128).transpose(2, 1, 0, 3)
    ).reshape(dout // 128, 128, din)


def _thin(w):
    din, m = w.shape
    return np.ascontiguousarray(
        w.reshape(din // 128, 128, m).transpose(1, 0, 2)).reshape(128, din // 128 * m)


def _bias_cols(v):
    return np.ascontiguousarray(v.reshape(-1, 128).T)


def _make_consts():
    c = np.zeros((128, 257), np.float32)
    c[:, 0] = 1.0
    c[0, 1:129] = 1.0
    c[:, 129:257] = np.eye(128, dtype=np.float32)
    cbv = np.zeros((128, CB_W), np.float32)
    cbv[:, CB_ID:CB_ID + 128] = np.eye(128)
    cbv[:, CB_ONEC] = 1.0
    cbv[0, CB_ONER:CB_ONER + 128] = 1.0
    for e in range(8):
        cbv[e, CB_E8 + e * 16:CB_E8 + (e + 1) * 16] = 1.0
    for g in range(4):
        cbv[g, CB_P32 + g * 32:CB_P32 + (g + 1) * 32] = 1.0
    for p in range(128):
        cbv[p, CB_G32 + p // 32] = 1.0
    for r in range(4):
        cbv[r, CB_Q4 + r * 32:CB_Q4 + (r + 1) * 32] = 1.0
    return c, cbv


def _rope_tables(core):
    pos0 = (core % 4) * 1024
    inv = 1.0 / (10000.0 ** (np.arange(0, 128, 2, dtype=np.float64) / 128.0))
    ang = (pos0 + np.arange(T, dtype=np.float64))[None, :] * inv[:, None]
    cc = np.empty((128, T), np.float32)
    ss = np.empty((128, T), np.float32)
    cc[0:64] = np.cos(ang); cc[64:128] = np.cos(ang)
    ss[0:64] = -np.sin(ang); ss[64:128] = np.sin(ang)
    return cc, ss


def _expert_B(lb):
    ball = lb.transpose(0, 2, 1).reshape(128, D)
    return np.ascontiguousarray(ball.reshape(128, 32, 128).transpose(1, 0, 2))


def prep_inputs(inputs, blocks):
    need = set(blocks)
    shared = {}
    c, cbv = _make_consts()
    shared["consts"] = c
    shared["constsb"] = cbv
    f32 = lambda k: np.asarray(inputs[k], dtype=np.float32)
    if 0 in need:
        shared["l0_wq"] = _colchunk(f32("L0_attn_wq"))
        shared["l0_wk"] = _colchunk(f32("L0_attn_wk"))
        shared["l0_wv"] = _colchunk(f32("L0_attn_wv"))
        shared["l0_bq"] = _bias_cols(f32("L0_attn_bq"))
        shared["l0_bk"] = _bias_cols(f32("L0_attn_bk"))
        shared["l0_bv"] = _bias_cols(f32("L0_attn_bv"))
        shared["l0_gw"] = _thin(np.ascontiguousarray(f32("L0_ffn_gate_w").T))
        shared["l0_bw"] = _colchunk(np.ascontiguousarray(f32("L0_ffn_base_w").T))
        shared["l0_bb"] = _bias_cols(f32("L0_ffn_base_b"))
        shared["l0_A"] = _colchunk(np.ascontiguousarray(f32("L0_ffn_lora_A").reshape(128, D).T))
        shared["l0_B"] = _expert_B(f32("L0_ffn_lora_B"))
    if 1 in need:
        shared["l1_w"] = _colchunk(np.ascontiguousarray(f32("L1_attn_weight").T))
        shared["l1_mask"] = np.ascontiguousarray(np.asarray(inputs["L1_attn_mask"], np.int32))
        shared["l1_A"] = _colchunk(np.ascontiguousarray(f32("L1_ffn_A").reshape(128, D).T))
        shared["l1_B"] = _expert_B(f32("L1_ffn_B"))
        shared["l1_wr"] = _thin(np.ascontiguousarray(f32("L1_ffn_Wr").T))
        shared["l1_br"] = f32("L1_ffn_br").reshape(8, 1)
        qx = np.zeros((128, D), np.float32)
        for nb in range(128):
            qx[nb, nb * 32:(nb + 1) * 32] = 1.0
        shared["qexp"] = qx
    if 2 in need:
        shared["l2_wq"] = _colchunk(f32("L2_attn_Wq"))
        shared["l2_wk"] = _colchunk(f32("L2_attn_Wk"))
        shared["l2_wv"] = _colchunk(f32("L2_attn_Wv"))
        shared["l2_wo"] = _colchunk(f32("L2_attn_Wo"))
        shared["l2_bq"] = _bias_cols(f32("L2_attn_bq"))
        shared["l2_bk"] = _bias_cols(f32("L2_attn_bk"))
        shared["l2_bv"] = _bias_cols(f32("L2_attn_bv"))
        shared["l2_bo"] = _bias_cols(f32("L2_attn_bo"))
        shared["l2_gq"] = _bias_cols(f32("L2_attn_gq"))
        shared["l2_gk"] = _bias_cols(f32("L2_attn_gk"))
        wrpp = np.concatenate([f32("L2_ffn_W_res").T, f32("L2_ffn_W_pre").T,
                               f32("L2_ffn_W_post").T], axis=1)
        shared["l2_wrpp"] = _thin(wrpp)
        shared["l2_b24"] = np.concatenate(
            [f32("L2_ffn_H_res_base").reshape(16), f32("L2_ffn_H_pre_base"),
             f32("L2_ffn_H_post_base")]).reshape(24, 1)
        shared["l2_gate"] = f32("L2_ffn_gate").reshape(1, 1)
        shared["l2_wbr"] = _colchunk(f32("L2_ffn_W_branch"))
    if 3 in need:
        shared["l3_wq"] = _colchunk(np.ascontiguousarray(f32("L3_attn_wq").T))
        shared["l3_wk"] = _colchunk(np.ascontiguousarray(f32("L3_attn_wk").T))
        shared["l3_wv"] = _colchunk(np.ascontiguousarray(f32("L3_attn_wv").T))
        shared["l3_wo"] = _colchunk(np.ascontiguousarray(f32("L3_attn_wo").T))
        shared["l3_w"] = _colchunk(np.ascontiguousarray(f32("L3_ffn_W").T))
        shared["l3_b"] = _bias_cols(f32("L3_ffn_b"))
        shared["l3_wqk"] = _colchunk(np.ascontiguousarray(
            np.concatenate([f32("L3_ffn_Wq"), f32("L3_ffn_Wk")], axis=0).T))
        shared["l3_A"] = _colchunk(np.ascontiguousarray(f32("L3_ffn_A").reshape(128, D).T))
        shared["l3_B"] = _expert_B(f32("L3_ffn_Bm"))
        NEGM = -30000.0
        far = np.where(np.arange(128)[None, :] < np.arange(128)[:, None], 0.0, NEGM
                       ).astype(np.float32)
        diag = np.where(np.arange(128)[:, None] <= np.arange(128)[None, :], 0.0, NEGM
                        ).astype(np.float32)
        shared["l3_mfd"] = np.concatenate([far, diag], axis=1)
        shared["_l3far"] = far
    percore = []
    for core in range(8):
        d = {}
        if 0 in need:
            ccc, sss = _rope_tables(core)
            d["l0_cc"] = ccc
            d["l0_ss"] = sss
        if 3 in need:
            r0 = core % 4
            me = np.zeros((128, 2048), np.float32)
            for qt in range(4):
                g = r0 * 8 + qt
                for s5 in range(4):
                    col = (qt * 4 + s5) * 128
                    if g >= 4:
                        if s5 == 0:
                            me[:, col:col + 128] = shared["_l3far"]
                    else:
                        if s5 < 4 - g:
                            me[:, col:col + 128] = -30000.0
            d["l3_me"] = me
        percore.append(d)
    return shared, percore


def run_blocks(inputs, blocks, x_in=None, dbg=None, trace=False):
    if x_in is None:
        x_in = np.asarray(inputs["L0_attn_x"], np.float32).reshape(8192, D)
    xT = np.ascontiguousarray(x_in.T)
    shared, percore = prep_inputs(inputs, blocks)
    nc = build_program(blocks, dbg=dbg)
    in_maps = []
    for core in range(8):
        m = dict(shared)
        m.update(percore[core])
        m["xin"] = np.ascontiguousarray(xT[:, core * T:(core + 1) * T])
        in_maps.append(m)
    for m in in_maps:
        m.pop("_l3far", None)
    res = run_bass_kernel_spmd(nc, in_maps, core_ids=list(range(8)), trace=trace)
    out = np.concatenate([r["xout"].T for r in res.results], axis=0)
    extra = None
    if dbg is not None:
        extra = np.concatenate([r["dbg"].T for r in res.results], axis=0)
    return out, extra, res


def kernel(**inputs):
    out, _, _ = run_blocks(inputs, [0, 1, 2, 3])
    return out.reshape(2, 4096, D)
